# version 15
# baseline (speedup 1.0000x reference)
import numpy as np
import ml_dtypes
from contextlib import ExitStack

import concourse.bass as bass
import concourse.mybir as mybir
from concourse.bass_utils import run_bass_kernel_spmd

F32 = mybir.dt.float32
BF16 = mybir.dt.bfloat16
U8 = mybir.dt.uint8
U16 = mybir.dt.uint16
U32 = mybir.dt.uint32
AF = mybir.ActivationFunctionType
ALU = mybir.AluOpType
AX = mybir.AxisListType
BFNP = ml_dtypes.bfloat16

T = 2048
DM = 1024
NB = 2
NCORES = 8
NTT = T // 128
EPS = 1e-5
MAGIC = 12582912.0


class Pr:
    ENGS = ("v", "s", "t", "g", "y")

    def __init__(self, nc, es):
        self.nc = nc
        self.es = es
        self.sems = {e: es.enter_context(nc.semaphore("sem_" + e)) for e in self.ENGS}
        self.dsem = es.enter_context(nc.semaphore("sem_dma"))
        self.lzsem = es.enter_context(nc.semaphore("sem_lzdma"))
        self.cnt = {e: 0 for e in self.ENGS}
        self.dcnt = 0
        self.pend_dma = 0
        self.lzcnt = 0
        self.need_lz = 0
        self.lz_seen = {e: 0 for e in self.ENGS}
        self.seen = {e: {o: 0 for o in self.ENGS} for e in self.ENGS}
        self.dseen = {e: 0 for e in self.ENGS}
        self.blk = es.enter_context(nc.Block())
        self.q = {e: [] for e in self.ENGS}
        self._names = 0

    def v(self, fn): self.q["v"].append(fn)
    def s(self, fn): self.q["s"].append(fn)
    def t(self, fn): self.q["t"].append(fn)
    def g(self, fn): self.q["g"].append(fn)
    def y(self, fn): self.q["y"].append(fn)

    def dma(self, out, in_, queue="y", lazy=False, **kw):
        if lazy:
            lzsem = self.lzsem
            def f(eng):
                eng.dma_start(out=out, in_=in_, **kw).then_inc(lzsem, 16)
                return None
            self.q[queue].append(f)
            self.lzcnt += 1
            return self.lzcnt
        dsem = self.dsem
        def f(eng):
            eng.dma_start(out=out, in_=in_, **kw).then_inc(dsem, 16)
            return None
        self.q[queue].append(f)
        self.pend_dma += 1
        return None

    def need(self, h):
        if h:
            self.need_lz = max(self.need_lz, h)

    def sbuf(self, shape, dtype=F32, name=None):
        if name is None:
            self._names += 1
            name = f"sb{self._names}"
        return self.es.enter_context(self.nc.sbuf_tensor(name, shape, dtype))

    def psum(self, shape, dtype=F32, name=None):
        if name is None:
            self._names += 1
            name = f"ps{self._names}"
        return self.es.enter_context(self.nc.psum_tensor(name, shape, dtype))

    def phase(self):
        waits = {}
        for e in self.ENGS:
            if self.q[e]:
                w = []
                for oe in self.ENGS:
                    c = self.cnt[oe]
                    if c > 0 and (oe == e or c > self.seen[e][oe]):
                        if c > self.seen[e][oe]:
                            w.append((self.sems[oe], c))
                            self.seen[e][oe] = c
                waits[e] = w
        dwait = self.dcnt
        for e in self.ENGS:
            ops = self.q[e]
            if not ops:
                continue
            sem = self.sems[e]
            mywaits = waits[e]
            dsem = self.dsem
            dw_eff = dwait if dwait > self.dseen[e] else 0
            if dw_eff:
                self.dseen[e] = dwait
            lzw = self.need_lz if self.need_lz > self.lz_seen[e] else 0
            if lzw:
                self.lz_seen[e] = lzw
            lzsem = self.lzsem
            def body(eng, ops=ops, mywaits=mywaits, dwait=dw_eff, sem=sem, dsem=dsem,
                     lzw=lzw, lzsem=lzsem):
                for s_, c_ in mywaits:
                    eng.wait_ge(s_, c_)
                if dwait:
                    eng.wait_ge(dsem, 16 * dwait)
                if lzw:
                    eng.wait_ge(lzsem, 16 * lzw)
                last = None
                for fn in ops:
                    r = fn(eng)
                    if r is not None:
                        last = r
                if last is not None:
                    last.then_inc(sem, 1)
                else:
                    eng.nop().then_inc(sem, 1)
            getattr(self.blk, {"v": "vector", "s": "scalar", "t": "tensor",
                               "g": "gpsimd", "y": "sync"}[e])(body)
            self.cnt[e] += 1
            self.seen[e][e] = self.cnt[e]
            self.q[e] = []
        self.dcnt += self.pend_dma
        self.pend_dma = 0

    def finish(self):
        self.phase()
        dw = self.dcnt
        lw = self.lzcnt
        waits = [(self.sems[e], self.cnt[e]) for e in self.ENGS if self.cnt[e] > 0]
        dsem = self.dsem
        lzsem = self.lzsem
        def body(eng):
            for s_, c_ in waits:
                eng.wait_ge(s_, c_)
            eng.wait_ge(dsem, 16 * dw)
            if lw:
                eng.wait_ge(lzsem, 16 * lw)
        self.blk.sync(body)


def load_consts(pr, nc, sel=False, tri=False, ones=False, identf=False, self32=False):
    cid = nc.dram_tensor("c_ident", [128, 128], F32, kind="ExternalInput").ap()
    pr.ident16 = pr.sbuf([128, 128], BF16, "identb")
    pr.dma(pr.ident16[:], cid, queue="g")
    if identf:
        pr.identf = pr.sbuf([128, 128], F32, "identf")
        pr.dma(pr.identf[:], cid)
    if sel or self32:
        csel = nc.dram_tensor("c_sel", [32, 32 * 128], F32, kind="ExternalInput").ap()
        pr.csel = csel
        if sel:
            pass
        if self32:
            pr.self32 = pr.sbuf([32, 32 * 128], F32, "self32")
            pr.dma(pr.self32[:], csel)
    if tri:
        ctri = nc.dram_tensor("c_tri", [128, 128], F32, kind="ExternalInput").ap()
        pr.tri16 = pr.sbuf([128, 128], BF16, "trib")
        pr.dma(pr.tri16[:], ctri, queue="g")
    if ones:
        pr.ones1f = pr.sbuf([1, 512], F32, "ones1f")
        def vf(vector):
            return vector.memset(pr.ones1f[:], 1.0)
        pr.v(vf)
    pr.phase()


def host_consts():
    ident = np.eye(128, dtype=np.float32)
    sel = np.zeros((32, 32 * 128), dtype=np.float32)
    for j in range(32):
        sel[j, j * 128:(j + 1) * 128] = 1.0
    tri = np.triu(np.ones((128, 128), dtype=np.float32))
    return {"c_ident": ident, "c_sel": sel, "c_tri": tri}


def emit_ln(pr, xin, xo, ntiles, D, stat, gamma_rep=None, beta_rep=None, xo2=None):
    sq = pr.scr_sq

    def s1(scalar):
        r = None
        for i in range(ntiles):
            r = scalar.activation(sq[:, 0:D], xin[:, i * D:(i + 1) * D], AF.Square,
                                  accum_out=stat[:, i * 8 + 1:i * 8 + 2])
        return r
    def v1(vector):
        r = None
        for i in range(ntiles):
            r = vector.reduce_sum(stat[:, i * 8:i * 8 + 1], xin[:, i * D:(i + 1) * D], axis=AX.X)
        return r
    pr.s(s1); pr.v(v1); pr.phase()

    def v2(vector):
        r = None
        for i in range(ntiles):
            c = i * 8
            vector.tensor_scalar(stat[:, c + 2:c + 3], stat[:, c:c + 1], 1.0 / D, None, ALU.mult)
            r = vector.tensor_scalar(stat[:, c + 4:c + 5], stat[:, c + 1:c + 2],
                                     1.0 / D, EPS, ALU.mult, ALU.add)
        return r
    pr.v(v2); pr.phase()

    def v3(vector):
        r = None
        for i in range(ntiles):
            c = i * 8
            vector.scalar_tensor_tensor(stat[:, c + 5:c + 6], stat[:, c + 2:c + 3],
                                        stat[:, c + 2:c + 3], stat[:, c + 4:c + 5],
                                        ALU.mult, ALU.subtract)
            r = vector.tensor_scalar(stat[:, c + 7:c + 8], stat[:, c + 2:c + 3],
                                     -1.0, None, ALU.mult)
        return r
    pr.v(v3); pr.phase()

    def s2(scalar):
        r = None
        for i in range(ntiles):
            c = i * 8
            r = scalar.activation(stat[:, c + 3:c + 4], stat[:, c + 5:c + 6], AF.Ln, scale=-1.0)
        return r
    pr.s(s2); pr.phase()

    def s3(scalar):
        r = None
        for i in range(ntiles):
            c = i * 8
            r = scalar.activation(stat[:, c + 6:c + 7], stat[:, c + 3:c + 4], AF.Exp, scale=-0.5)
        return r
    pr.s(s3); pr.phase()

    def v4(vector):
        r = None
        for i in range(ntiles):
            c = i * 8
            r = vector.tensor_scalar(xo[:, i * D:(i + 1) * D], xin[:, i * D:(i + 1) * D],
                                     stat[:, c + 7:c + 8], stat[:, c + 6:c + 7], ALU.add, ALU.mult)
            if xo2 is not None:
                r = vector.tensor_copy(xo2[:, i * D:(i + 1) * D], xo[:, i * D:(i + 1) * D])
        return r
    pr.v(v4); pr.phase()

    if gamma_rep is not None:
        def v5(vector):
            r = None
            for i in range(ntiles):
                sl = slice(i * D, (i + 1) * D)
                vector.tensor_tensor(xo[:, sl], xo[:, sl], gamma_rep[:, 0:D], ALU.mult)
                r = vector.tensor_tensor(xo[:, sl], xo[:, sl], beta_rep[:, 0:D], ALU.add)
            return r
        pr.v(v5); pr.phase()


def emit_transpose(pr, pst, blocks):
    group = pst.shape[1] // 128
    ident = pr.ident16[:]
    for i0 in range(0, len(blocks), group):
        n = min(group, len(blocks) - i0)
        def tf(tensor, i0=i0, n=n):
            r = None
            for j in range(n):
                st, sc, _, _ = blocks[i0 + j]
                r = tensor.transpose(pst[:, j * 128:(j + 1) * 128], st[:, sc:sc + 128], ident)
            return r
        pr.t(tf); pr.phase()
        def vf(vector, i0=i0, n=n):
            r = None
            for j in range(n):
                _, _, dt_, dc = blocks[i0 + j]
                r = vector.tensor_copy(dt_[:, dc:dc + 128], pst[:, j * 128:(j + 1) * 128])
            return r
        pr.v(vf); pr.phase()


def _emit_quant_single(pr, job, scr):
    x, F, out = job
    G = F // 16
    bm, rs = scr["bm"][0], scr["rs"][0]
    def p1(vector):
        xg = x[:, 0:F].rearrange("p (g o) -> p g o", o=16)
        return vector.reduce_max(bm[:, 0:G], xg, axis=AX.X, apply_absolute_value=True)
    pr.v(p1); pr.phase()
    pr.v(lambda vector: vector.tensor_scalar(bm[:, 0:G], bm[:, 0:G], 1.0 / 6.0, 1e-12,
                                             ALU.mult, ALU.max))
    pr.phase()
    pr.v(lambda vector: vector.reciprocal(rs[:, 0:G], bm[:, 0:G]))
    pr.phase()
    pr.v(lambda vector: vector.tensor_tensor(scr["q1"][0][:, 0:G], bm[:, 0:G],
                                             rs[:, 0:G], ALU.mult))
    pr.phase()
    pr.v(lambda vector: vector.tensor_scalar(scr["q1"][0][:, 0:G], scr["q1"][0][:, 0:G],
                                             -1.0, 2.0, ALU.mult, ALU.add))
    pr.phase()
    pr.v(lambda vector: vector.tensor_tensor(rs[:, 0:G], rs[:, 0:G],
                                             scr["q1"][0][:, 0:G], ALU.mult))
    pr.phase()
    def p2(vector):
        u, m = scr["u"][0], scr["m"][0]
        q1, qh, mg = scr["q1"][0], scr["qh"][0], scr["mg"][0]
        rsb = rs[:, 0:G].unsqueeze(2).broadcast_to([128, G, 16])
        smb = bm[:, 0:G].unsqueeze(2).broadcast_to([128, G, 16])
        xv = x[:, 0:F].rearrange("p (g o) -> p g o", o=16)
        vector.tensor_copy(u[:, 0:F], x[:, 0:F])
        vector.tensor_single_scalar(m[:, 0:F].bitcast(U16), u[:, 0:F].bitcast(U16),
                                    0x8000, ALU.bitwise_and)
        mgv = mg[:, 0:F].rearrange("p (g o) -> p g o", o=16)
        vector.tensor_tensor(mgv, xv, rsb, ALU.mult)
        vector.tensor_single_scalar(mg[:, 0:F].bitcast(U32), mg[:, 0:F].bitcast(U32),
                                    0x7FFFFFFF, ALU.bitwise_and)
        vector.tensor_scalar(q1[:, 0:F], mg[:, 0:F], 2.0, MAGIC, ALU.mult, ALU.add)
        vector.tensor_scalar(q1[:, 0:F], q1[:, 0:F], MAGIC, 0.5, ALU.subtract, ALU.mult)
        vector.tensor_scalar(qh[:, 0:F], mg[:, 0:F], MAGIC, None, ALU.add)
        vector.tensor_scalar(qh[:, 0:F], qh[:, 0:F], MAGIC, 4.0, ALU.subtract, ALU.min)
        vector.tensor_single_scalar(u[:, 0:F], mg[:, 0:F], 5.0, ALU.is_ge)
        vector.scalar_tensor_tensor(qh[:, 0:F], u[:, 0:F], 2.0, qh[:, 0:F],
                                    ALU.mult, ALU.add)
        vector.tensor_single_scalar(u[:, 0:F], mg[:, 0:F], 1.75, ALU.is_lt)
        vector.tensor_tensor(q1[:, 0:F], q1[:, 0:F], qh[:, 0:F], ALU.subtract)
        vector.tensor_tensor(q1[:, 0:F], q1[:, 0:F], u[:, 0:F], ALU.mult)
        vector.tensor_tensor(q1[:, 0:F], q1[:, 0:F], qh[:, 0:F], ALU.add)
        vector.tensor_copy(u[:, 0:F], q1[:, 0:F])
        vector.tensor_tensor(u[:, 0:F].bitcast(U16), u[:, 0:F].bitcast(U16),
                             m[:, 0:F].bitcast(U16), ALU.bitwise_or)
        uv2 = u[:, 0:F].rearrange("p (g o) -> p g o", o=16)
        ov = out[:, 0:F].rearrange("p (g o) -> p g o", o=16)
        return vector.tensor_tensor(ov, uv2, smb, ALU.mult)
    pr.v(p2); pr.phase()


def emit_quant(pr, jobs, scr):
    npool = len(scr["u"])
    for g0 in range(0, len(jobs), npool):
        grp = jobs[g0:g0 + npool]
        if len(grp) == 1:
            _emit_quant_single(pr, grp[0], scr)
            continue

        def vq(vector):
            for j, (x, F, out) in enumerate(grp):
                G = F // 16
                xg = x[:, 0:F].rearrange("p (g o) -> p g o", o=16)
                vector.reduce_max(scr["bm"][j][:, 0:G], xg, axis=AX.X,
                                  apply_absolute_value=True)
            for j, (x, F, out) in enumerate(grp):
                G = F // 16
                vector.tensor_scalar(scr["bm"][j][:, 0:G], scr["bm"][j][:, 0:G],
                                     1.0 / 6.0, 1e-12, ALU.mult, ALU.max)
            for j, (x, F, out) in enumerate(grp):
                G = F // 16
                vector.reciprocal(scr["rs"][j][:, 0:G], scr["bm"][j][:, 0:G])
            for j, (x, F, out) in enumerate(grp):
                G = F // 16
                vector.tensor_tensor(scr["q1"][j][:, 0:G], scr["bm"][j][:, 0:G],
                                     scr["rs"][j][:, 0:G], ALU.mult)
            for j, (x, F, out) in enumerate(grp):
                G = F // 16
                vector.tensor_scalar(scr["q1"][j][:, 0:G], scr["q1"][j][:, 0:G],
                                     -1.0, 2.0, ALU.mult, ALU.add)
            for j, (x, F, out) in enumerate(grp):
                G = F // 16
                vector.tensor_tensor(scr["rs"][j][:, 0:G], scr["rs"][j][:, 0:G],
                                     scr["q1"][j][:, 0:G], ALU.mult)
            r = None
            for j, (x, F, out) in enumerate(grp):
                G = F // 16
                bm, rs = scr["bm"][j], scr["rs"][j]
                u, m = scr["u"][j], scr["m"][j]
                q1, qh, mg = scr["q1"][j], scr["qh"][j], scr["mg"][j]
                rsb = rs[:, 0:G].unsqueeze(2).broadcast_to([128, G, 16])
                smb = bm[:, 0:G].unsqueeze(2).broadcast_to([128, G, 16])
                xv = x[:, 0:F].rearrange("p (g o) -> p g o", o=16)
                vector.tensor_copy(u[:, 0:F], x[:, 0:F])
                vector.tensor_single_scalar(m[:, 0:F].bitcast(U16), u[:, 0:F].bitcast(U16),
                                            0x8000, ALU.bitwise_and)
                mgv = mg[:, 0:F].rearrange("p (g o) -> p g o", o=16)
                vector.tensor_tensor(mgv, xv, rsb, ALU.mult)
                vector.tensor_single_scalar(mg[:, 0:F].bitcast(U32), mg[:, 0:F].bitcast(U32),
                                            0x7FFFFFFF, ALU.bitwise_and)
                vector.tensor_scalar(q1[:, 0:F], mg[:, 0:F], 2.0, MAGIC, ALU.mult, ALU.add)
                vector.tensor_scalar(q1[:, 0:F], q1[:, 0:F], MAGIC, 0.5, ALU.subtract, ALU.mult)
                vector.tensor_scalar(qh[:, 0:F], mg[:, 0:F], MAGIC, None, ALU.add)
                vector.tensor_scalar(qh[:, 0:F], qh[:, 0:F], MAGIC, 4.0, ALU.subtract, ALU.min)
                vector.tensor_single_scalar(u[:, 0:F], mg[:, 0:F], 5.0, ALU.is_ge)
                vector.scalar_tensor_tensor(qh[:, 0:F], u[:, 0:F], 2.0, qh[:, 0:F],
                                            ALU.mult, ALU.add)
                vector.tensor_single_scalar(u[:, 0:F], mg[:, 0:F], 1.75, ALU.is_lt)
                vector.tensor_tensor(q1[:, 0:F], q1[:, 0:F], qh[:, 0:F], ALU.subtract)
                vector.tensor_tensor(q1[:, 0:F], q1[:, 0:F], u[:, 0:F], ALU.mult)
                vector.tensor_tensor(q1[:, 0:F], q1[:, 0:F], qh[:, 0:F], ALU.add)
                vector.tensor_copy(u[:, 0:F], q1[:, 0:F])
                vector.tensor_tensor(u[:, 0:F].bitcast(U16), u[:, 0:F].bitcast(U16),
                                     m[:, 0:F].bitcast(U16), ALU.bitwise_or)
                uv2 = u[:, 0:F].rearrange("p (g o) -> p g o", o=16)
                ov = out[:, 0:F].rearrange("p (g o) -> p g o", o=16)
                r = vector.tensor_tensor(ov, uv2, smb, ALU.mult)
            return r
        pr.v(vq); pr.phase()


def mk_quant_scr(pr, F, n=1, es=None):
    def alloc(shape, dt_):
        pr._names += 1
        name = f"qscr{pr._names}"
        cm = pr.nc.sbuf_tensor(name, shape, dt_)
        return (es or pr.es).enter_context(cm)
    return {
        "bm": [alloc([128, F // 16], F32) for _ in range(n)],
        "rs": [alloc([128, F // 16], F32) for _ in range(n)],
        "q1": [alloc([128, F], F32) for _ in range(n)],
        "qh": [alloc([128, F], F32) for _ in range(n)],
        "mg": [alloc([128, F], F32) for _ in range(n)],
        "u": [alloc([128, F], BF16) for _ in range(n)],
        "m": [alloc([128, F], BF16) for _ in range(n)],
    }


def emit_silu(pr, tiles, scr_e):
    npool = len(scr_e)
    for g0 in range(0, len(tiles), npool):
        grp = tiles[g0:g0 + npool]
        def s1(scalar):
            r = None
            for j, (x, F) in enumerate(grp):
                r = scalar.activation(scr_e[j][:, 0:F], x[:, 0:F], AF.Exp, scale=-1.0)
            return r
        pr.s(s1); pr.phase()
        def v1(vector):
            r = None
            for j, (x, F) in enumerate(grp):
                e = scr_e[j]
                vector.tensor_scalar(e[:, 0:F], e[:, 0:F], 1.0, None, ALU.add)
                vector.reciprocal(e[:, 0:F], e[:, 0:F])
                r = vector.tensor_tensor(x[:, 0:F], x[:, 0:F], e[:, 0:F], ALU.mult)
            return r
        pr.v(v1); pr.phase()


DBG_FFN = False


def build_ffn():
    I = 2816
    KI = I // 128
    KD = DM // 128
    nc = bass.Bass("TRN2", target_bir_lowering=False)
    xin = nc.dram_tensor("x", [NB, T, DM], F32, kind="ExternalInput").ap()
    wg = nc.dram_tensor("wg", [128, KI * DM], F32, kind="ExternalInput").ap()
    wu = nc.dram_tensor("wu", [128, KI * DM], F32, kind="ExternalInput").ap()
    wd = nc.dram_tensor("wd", [128, KD * I], F32, kind="ExternalInput").ap()
    xout = nc.dram_tensor("xo", [NB, T, DM], F32, kind="ExternalOutput").ap()
    with ExitStack() as es:
        pr = Pr(nc, es)
        load_consts(pr, nc)
        emit_ffn(nc, pr, xin, wg, wu, wd, xout, "F_")
        pr.finish()
    return nc


def emit_ffn(nc, pr, xin, wg, wu, wd, xout, pf):
    I = 2816
    KI = I // 128
    KD = DM // 128
    with ExitStack() as es:
        scr = mk_quant_scr(pr, 512, n=3, es=es)
        pr.scr_sq = scr["q1"][0]

        swgT = es.enter_context(nc.sbuf_tensor(pf + "swgT", [128, KD * I], BF16))
        swuT = es.enter_context(nc.sbuf_tensor(pf + "swuT", [128, KD * I], BF16))
        swdT = es.enter_context(nc.sbuf_tensor(pf + "swdT", [128, KI * DM], BF16))
        pst = es.enter_context(nc.psum_tensor(pf + "pst", [128, 2048], BF16))
        psA = es.enter_context(nc.psum_tensor(pf + "psA", [128, 1024], F32))
        psA2 = es.enter_context(nc.psum_tensor(pf + "psA2", [128, 1024], F32))
        psD = es.enter_context(nc.psum_tensor(pf + "psD", [128, 1024], F32))
        xnT = es.enter_context(nc.sbuf_tensor(pf + "xnT", [128, KD * T], BF16))

        _pc = [0]

        def prep(wdram, n_rowtiles, fcols, dstT, slice_tiles):
            for s0 in range(0, n_rowtiles, slice_tiles):
                ns = min(slice_tiles, n_rowtiles - s0)
                _pc[0] += 1
                with nc.sbuf_tensor(f"{pf}wstg{_pc[0]}", [128, ns * fcols], F32) as stg, \
                     nc.sbuf_tensor(f"{pf}wstq{_pc[0]}", [128, ns * fcols], BF16) as stq:
                    pr.dma(stg[:], wdram[:, s0 * fcols:(s0 + ns) * fcols])
                    pr.phase()
                    qw = 512 if fcols % 512 == 0 else 352
                    qjobs = []
                    for j in range(ns):
                        for c in range(0, fcols, qw):
                            qjobs.append((stg[:, j * fcols + c:j * fcols + c + qw], qw,
                                          stq[:, j * fcols + c:j * fcols + c + qw]))
                    emit_quant(pr, qjobs, scr)
                    blocks = []
                    for j in range(ns):
                        kt = s0 + j
                        for fc in range(fcols // 128):
                            blocks.append((stq, j * fcols + fc * 128,
                                           dstT, fc * (n_rowtiles * 128) + kt * 128))
                    emit_transpose(pr, pst, blocks)

        with nc.named_scope("ffn_wprep"):
            prep(wg, KI, DM, swgT, 3)
            prep(wu, KI, DM, swuT, 3)
            prep(wd, KD, I, swdT, 1)

        stat = es.enter_context(nc.sbuf_tensor(pf + "stat", [128, 8], F32))
        g32 = es.enter_context(nc.sbuf_tensor(pf + "g32", [128, DM], F32))
        u32 = es.enter_context(nc.sbuf_tensor(pf + "u32", [128, DM], F32))
        o_t = u32
        x_t = g32
        pr.scr_sq = u32

        for b in range(NB):
            QT = 1
            for h0 in range(0, NTT, QT):
                with nc.sbuf_tensor(f"{pf}xs{b}_{h0}", [128, QT * DM], F32) as xs, \
                     nc.sbuf_tensor(f"{pf}xq{b}_{h0}", [128, QT * DM], BF16) as xq:
                    for i in range(QT):
                        tt = h0 + i
                        pr.dma(xs[:, i * DM:(i + 1) * DM], xin[b, tt * 128:(tt + 1) * 128, :])
                    pr.phase()
                    emit_ln(pr, xs, xs, QT, DM, stat)
                    emit_quant(pr, [(xs[:, i * DM + c:i * DM + c + 512], 512,
                                     xq[:, i * DM + c:i * DM + c + 512])
                                    for i in range(QT) for c in (0, 512)], scr)
                    emit_transpose(pr, pst,
                                   [(xq, i * DM + kd * 128, xnT, kd * T + (h0 + i) * 128)
                                    for i in range(QT) for kd in range(KD)])


            with nc.sbuf_tensor(f"{pf}h_t{b}", [128, I], BF16) as h_t:
              hT_t = h_t
              for tt in range(NTT):
                  slices = []
                  for c0 in range(0, I, DM):
                      cw = min(DM, I - c0)
                      for i0 in range(c0, c0 + cw, 512):
                          slices.append((c0, i0, min(512, c0 + cw - i0)))
                  nsl = len(slices)

                  def emit_mm(si):
                      c0, i0, iw = slices[si]
                      ps = psA if si % 2 == 0 else psA2
                      def tf(tensor, i0=i0, iw=iw, ps=ps, tt=tt):
                          r = None
                          for half, w in ((0, swgT), (1, swuT)):
                              for kd in range(KD):
                                  r = tensor.matmul(
                                      ps[:, half * 512:half * 512 + iw],
                                      xnT[:, kd * T + tt * 128:kd * T + tt * 128 + 128],
                                      w[:, kd * I + i0:kd * I + i0 + iw],
                                      start=(kd == 0), stop=(kd == KD - 1))
                          return r
                      pr.t(tf)

                  def emit_evac(si):
                      c0, i0, iw = slices[si]
                      ps = psA if si % 2 == 0 else psA2
                      def vf(vector, c0=c0, i0=i0, iw=iw, ps=ps):
                          c = i0 - c0
                          vector.tensor_copy(g32[:, c:c + iw], ps[:, 0:iw])
                          return vector.tensor_copy(u32[:, c:c + iw], ps[:, 512:512 + iw])
                      pr.v(vf)

                  def emit_silu(c0, cw):
                      pr.s(lambda scalar, c0=c0, cw=cw: scalar.activation(
                          g32[:, 0:cw], g32[:, 0:cw], AF.Silu))

                  def emit_mult(c0, cw):
                      pr.v(lambda vector, c0=c0, cw=cw: vector.tensor_tensor(
                          g32[:, 0:cw], g32[:, 0:cw], u32[:, 0:cw], ALU.mult))

                  def emit_q(c0, cw):
                      qjobs = []
                      for q0 in range(0, cw, 512):
                          qw = min(512, cw - q0)
                          qjobs.append((g32[:, q0:q0 + qw], qw,
                                        h_t[:, c0 + q0:c0 + q0 + qw]))
                      emit_quant(pr, qjobs, scr)

                  def emit_tr(j0, nblk):
                      def tfr(tensor, j0=j0, nblk=nblk):
                          r = None
                          for j in range(nblk):
                              r = tensor.transpose(pst[:, j * 128:(j + 1) * 128],
                                                   h_t[:, (j0 + j) * 128:(j0 + j + 1) * 128],
                                                   pr.ident16[:])
                          return r
                      pr.t(tfr)

                  def emit_tre(j0, nblk):
                      def vfr(vector, j0=j0, nblk=nblk):
                          r = None
                          for j in range(nblk):
                              r = vector.tensor_copy(
                                  h_t[:, (j0 + j) * 128:(j0 + j + 1) * 128],
                                  pst[:, j * 128:(j + 1) * 128])
                          return r
                      pr.v(vfr)

                  emit_mm(0); pr.phase()
                  emit_mm(1); emit_evac(0); pr.phase()
                  emit_mm(2); emit_evac(1); pr.phase()
                  emit_mm(3); emit_silu(0, DM); pr.phase()
                  emit_mult(0, DM); pr.phase()
                  emit_q(0, DM)
                  emit_evac(2); pr.phase()
                  emit_mm(4); emit_evac(3); pr.phase()
                  emit_mm(5); emit_silu(DM, DM); pr.phase()
                  emit_mult(DM, DM); pr.phase()
                  emit_q(DM, DM)
                  emit_evac(4); emit_tr(0, 16); pr.phase()
                  emit_evac(5); emit_tre(0, 16); pr.phase()
                  emit_silu(2 * DM, I - 2 * DM); pr.phase()
                  emit_mult(2 * DM, I - 2 * DM); pr.phase()
                  emit_q(2 * DM, I - 2 * DM)
                  emit_tr(16, 6); pr.phase()
                  emit_tre(16, 6); pr.phase()

                  def tf2(tensor, tt=tt):
                      r = None
                      for c0 in range(0, DM, 512):
                          for ki in range(KI):
                              r = tensor.matmul(psD[:, c0:c0 + 512],
                                                hT_t[:, ki * 128:(ki + 1) * 128],
                                                swdT[:, ki * DM + c0:ki * DM + c0 + 512],
                                                start=(ki == 0), stop=(ki == KI - 1))
                      return r
                  pr.t(tf2)
                  pr.dma(x_t[:, 0:DM], xin[b, tt * 128:(tt + 1) * 128, :])
                  pr.phase()

                  def vf2(vector):
                      return vector.tensor_tensor(o_t[:], psD[:, 0:DM], x_t[:, 0:DM], ALU.add)
                  pr.v(vf2); pr.phase()
                  pr.dma(xout[b, tt * 128:(tt + 1) * 128, :], o_t[:])
            pr.phase()
        pr.phase()


def build_l0_attn():
    nc = bass.Bass("TRN2", target_bir_lowering=False)
    xin = nc.dram_tensor("x", [NB, T, DM], F32, kind="ExternalInput").ap()
    xout = nc.dram_tensor("xo", [NB, T, DM], F32, kind="ExternalOutput").ap()
    W = _l0_declare(nc)
    with ExitStack() as es:
        pr = Pr(nc, es)
        load_consts(pr, nc, tri=True, identf=True, self32=True)
        emit_l0(nc, pr, xin, xout, W, "A_")
        pr.finish()
    return nc


def _l0_declare(nc, pf=""):
    names = [("inprojT", [128, 8 * 4384]), ("convw", [128, 18 * 4]),
             ("convbi", [128, 18]), ("dtbias", [32, 1]), ("alogi", [32, 1]),
             ("dvec_rep", [128, 2048]), ("normw_rep", [128, 2048]),
             ("lnw_rep", [128, 1024]), ("lnb_rep", [128, 1024]),
             ("outprojT", [128, 16 * 1024])]
    return {n: nc.dram_tensor(pf + n, sh, F32, kind="ExternalInput").ap()
            for n, sh in names}


def emit_l0(nc, pr, xin, xout, W, pf):
    H = 32
    Q = 128
    NCH = T // Q
    inprojT = W["inprojT"]; convw = W["convw"]; convbi = W["convbi"]
    dtbias = W["dtbias"]; alogi = W["alogi"]; dvec_rep = W["dvec_rep"]
    normw_rep = W["normw_rep"]; lnw_rep = W["lnw_rep"]; lnb_rep = W["lnb_rep"]
    outprojT = W["outprojT"]
    z0_sp = nc.dram_tensor(pf + "z0_sp", [128, 16 * T], BF16, kind="Internal").ap()
    xbc_sp = nc.dram_tensor(pf + "xbc_sp", [128, 16 * T], BF16, kind="Internal").ap()
    y0_sp = nc.dram_tensor(pf + "y0_sp", [128, 16 * DM], BF16, kind="Internal").ap()
    dk_sp = nc.dram_tensor(pf + "dk_sp", [32, 16], F32, kind="Internal").ap()
    cselap = pr.csel
    if True:
        es = ExitStack()

        stat = es.enter_context(nc.sbuf_tensor(pf + "stat", [128, 8], F32))
        scr_sq = es.enter_context(nc.sbuf_tensor(pf + "scr_sq", [128, DM], F32))
        pr.scr_sq = scr_sq
        onesf = es.enter_context(nc.sbuf_tensor(pf + "onesf", [1, 128], F32))
        cw_sb = es.enter_context(nc.sbuf_tensor(pf + "cw_sb", [128, 18 * 4], F32))
        cb_sb = es.enter_context(nc.sbuf_tensor(pf + "cb_sb", [128, 18], F32))
        dtb_sb = es.enter_context(nc.sbuf_tensor(pf + "dtb_sb", [32, 1], F32))
        an_sb = es.enter_context(nc.sbuf_tensor(pf + "an_sb", [32, 1], F32))
        drep = es.enter_context(nc.sbuf_tensor(pf + "drep", [128, 2048], BF16))
        nwrep = es.enter_context(nc.sbuf_tensor(pf + "nwrep", [128, 2048], BF16))
        lwrep = es.enter_context(nc.sbuf_tensor(pf + "lwrep", [128, 1024], BF16))
        lbrep = es.enter_context(nc.sbuf_tensor(pf + "lbrep", [128, 1024], BF16))
        pr.dma(cw_sb[:], convw)
        pr.dma(cb_sb[:], convbi)
        pr.dma(dtb_sb[:], dtbias)
        pr.dma(an_sb[:], alogi)
        pr.dma(drep[:], dvec_rep, queue="g")
        pr.dma(nwrep[:], normw_rep, queue="g")
        pr.dma(lwrep[:], lnw_rep, queue="g")
        pr.dma(lbrep[:], lnb_rep, queue="g")
        pr.v(lambda vector: vector.memset(onesf[:], 1.0))
        pr.phase()
        pr.s(lambda scalar: scalar.activation(an_sb[:], an_sb[:], AF.Exp))
        pr.phase()
        pr.v(lambda vector: vector.tensor_scalar(an_sb[:], an_sb[:], -1.0, None, ALU.mult))
        pr.phase()

        BT_sb = es.enter_context(nc.sbuf_tensor(pf + "BT_sb", [128, T], BF16))
        CT_sb = es.enter_context(nc.sbuf_tensor(pf + "CT_sb", [128, T], BF16))
        dt_sb = es.enter_context(nc.sbuf_tensor(pf + "dt_sb", [32, T], F32))
        ca_sb = es.enter_context(nc.sbuf_tensor(pf + "ca_sb", [32, T], F32))
        det_sb = es.enter_context(nc.sbuf_tensor(pf + "det_sb", [32, T], F32))
        caT = es.enter_context(nc.sbuf_tensor(pf + "caT", [128, NCH * 32], F32))
        dtT = es.enter_context(nc.sbuf_tensor(pf + "dtT", [128, NCH * 32], F32))
        decE = es.enter_context(nc.sbuf_tensor(pf + "decE", [128, NCH * 32], F32))
        expcaT = es.enter_context(nc.sbuf_tensor(pf + "expcaT", [128, NCH * 32], F32))
        dchkr = es.enter_context(nc.sbuf_tensor(pf + "dchkr", [128, 512], F32))
        dchk_row = es.enter_context(nc.sbuf_tensor(pf + "dchk_row", [1, 512], F32))
        S_sb = es.enter_context(nc.sbuf_tensor(pf + "S_sb", [128, H * 64], BF16))
        ones32 = es.enter_context(nc.sbuf_tensor(pf + "ones32", [32, Q], F32))
        ci = es.enter_context(nc.sbuf_tensor(pf + "ci", [128, T + 4], BF16))
        wslc = es.enter_context(nc.sbuf_tensor(pf + "wslc", [128, 8 * 128], BF16))
        pr.v(lambda vector: vector.memset(ones32[:], 1.0))
        pr.phase()

        for b in range(NB):
            cih = [0]
            spl = [0]
            y0h = [0]
            y0hs = []
            with nc.sbuf_tensor(f"{pf}lnx{b}", [128, DM], F32) as xs, \
                 nc.sbuf_tensor(f"{pf}y0t{b}", [128, DM], F32) as y0t, \
                 nc.sbuf_tensor(f"{pf}xnb{b}", [128, DM], BF16) as xnb, \
                 nc.sbuf_tensor(f"{pf}xnT{b}", [128, 8 * T], BF16) as xnT, \
                 nc.sbuf_tensor(f"{pf}e32p{b}", [128, T], F32) as e32, \
                 nc.sbuf_tensor(f"{pf}f32p{b}", [128, T], F32) as f32b:
                with nc.psum_tensor(f"{pf}pstT{b}", [128, 2048], BF16) as pst:
                    for tt in range(NTT):
                        pr.dma(xs[:], xin[b, tt * 128:(tt + 1) * 128, :])
                        pr.phase()
                        pr.need(y0h[0])
                        emit_ln(pr, xs, y0t, 1, DM, stat)
                        y0h[0] = pr.dma(y0_sp[:, tt * DM:(tt + 1) * DM], y0t[:],
                                        queue="g", lazy=True)
                        y0hs.append(y0h[0])
                        emit_ln(pr, y0t, xnb, 1, DM, stat, gamma_rep=lwrep, beta_rep=lbrep)
                        emit_transpose(pr, pst, [(xnb, kd * 128, xnT, kd * T + tt * 128)
                                                 for kd in range(8)])

                with nc.psum_tensor(f"{pf}psA{b}", [128, 2048], F32) as psA:
                    for ot in range(35):
                        orows = 32 if ot == 34 else 128
                        wv = inprojT.rearrange("p (kt o) -> p kt o", o=4384)
                        pr.dma(wslc[:, 0:8 * orows].rearrange("p (kt o) -> p kt o", o=orows),
                               wv[:, :, ot * 128:ot * 128 + orows], queue="g")
                        pr.phase()
                        def tf(tensor, ot=ot, orows=orows):
                            r = None
                            for n0 in range(0, T, 512):
                                for kd in range(8):
                                    r = tensor.matmul(psA[0:orows, n0:n0 + 512],
                                                      wslc[:, kd * orows:(kd + 1) * orows],
                                                      xnT[:, kd * T + n0:kd * T + n0 + 512],
                                                      start=(kd == 0), stop=(kd == 7))
                            return r
                        pr.t(tf); pr.phase()
                        if ot < 16:
                            pr.need(cih[0])
                            pr.v(lambda vector: vector.tensor_copy(ci[:, 4:4 + T],
                                                                   psA[:, 0:T]))
                            pr.phase()
                            h = pr.dma(z0_sp[:, ot * T:(ot + 1) * T], ci[:, 4:4 + T],
                                       queue="g", lazy=True)
                            cih[0] = h; spl[0] = max(spl[0], h)
                            pr.phase()
                        elif ot < 34:
                            cc = ot - 16
                            pr.need(cih[0])
                            def vc0(vector, cc=cc):
                                vector.memset(ci[:, 0:4], 0.0)
                                vector.tensor_copy(ci[:, 4:4 + T], psA[:, 0:T])
                                vector.tensor_scalar(f32b[:, 0:T], ci[:, 1:1 + T],
                                                     cw_sb[:, cc * 4:cc * 4 + 1], None, ALU.mult)
                                for k in (1, 2, 3):
                                    vector.scalar_tensor_tensor(
                                        f32b[:, 0:T], ci[:, 1 + k:1 + k + T],
                                        cw_sb[:, cc * 4 + k:cc * 4 + k + 1],
                                        f32b[:, 0:T], ALU.mult, ALU.add)
                                return vector.tensor_scalar(
                                    f32b[:, 0:T], f32b[:, 0:T], cb_sb[:, cc:cc + 1],
                                    None, ALU.add)
                            pr.v(vc0); pr.phase()
                            if cc < 16:
                                dst0 = ci[:, 4:4 + T]
                            elif cc == 16:
                                dst0 = BT_sb[:, 0:T]
                            else:
                                dst0 = CT_sb[:, 0:T]
                            pr.s(lambda scalar, dst0=dst0: scalar.activation(
                                dst0, f32b[:, 0:T], AF.Silu))
                            pr.phase()
                            if cc < 16:
                                h = pr.dma(xbc_sp[:, cc * T:(cc + 1) * T], ci[:, 4:4 + T],
                                           queue="g", lazy=True)
                                cih[0] = h; spl[0] = max(spl[0], h)
                                pr.phase()
                        else:
                            def vd0(vector):
                                vector.tensor_scalar(f32b[0:32, 0:T], psA[0:32, 0:T],
                                                     dtb_sb[:, 0:1], None, ALU.add)
                                return vector.tensor_single_scalar(
                                    e32[0:32, 0:T].bitcast(U32),
                                    f32b[0:32, 0:T].bitcast(U32),
                                    0x7FFFFFFF, ALU.bitwise_and)
                            pr.v(vd0); pr.phase()
                            def sd1(scalar):
                                scalar.activation(e32[0:32, 0:T], e32[0:32, 0:T],
                                                  AF.Exp, scale=-1.0)
                                return scalar.activation(e32[0:32, 0:T], e32[0:32, 0:T],
                                                         AF.Ln, bias=1.0)
                            pr.s(sd1); pr.phase()
                            def vd2(vector):
                                return vector.scalar_tensor_tensor(
                                    dt_sb[:, 0:T], f32b[0:32, 0:T], 0.0, e32[0:32, 0:T],
                                    ALU.max, ALU.add)
                            pr.v(vd2); pr.phase()

            pr.v(lambda vector: vector.tensor_scalar(ca_sb[:, 0:T], dt_sb[:, 0:T],
                                                     an_sb[:, 0:1], None, ALU.mult))
            pr.phase()
            def vca(vector):
                r = None
                for c in range(NCH):
                    r = vector.tensor_tensor_scan(ca_sb[:, c * Q:(c + 1) * Q], ones32[:],
                                                  ca_sb[:, c * Q:(c + 1) * Q], 0.0,
                                                  ALU.mult, ALU.add)
                return r
            pr.v(vca); pr.phase()
            def vde(vector):
                r = None
                for c in range(NCH):
                    r = vector.tensor_scalar(det_sb[:, c * Q:(c + 1) * Q],
                                             ca_sb[:, c * Q:(c + 1) * Q],
                                             ca_sb[:, (c + 1) * Q - 1:(c + 1) * Q],
                                             None, ALU.subtract)
                return r
            pr.v(vde); pr.phase()
            pr.s(lambda scalar: scalar.activation(det_sb[:, 0:T], det_sb[:, 0:T],
                                                  AF.Exp, scale=-1.0))
            pr.phase()
            with nc.psum_tensor(f"{pf}pstc{b}", [128, 2048], F32) as pstc:
                for src, dst in ((ca_sb, caT), (dt_sb, dtT), (det_sb, decE)):
                    def tca(tensor, src=src):
                        r = None
                        for c in range(NCH):
                            r = tensor.transpose(pstc[:, c * 128:c * 128 + 32],
                                                 src[:, c * Q:(c + 1) * Q],
                                                 pr.identf[0:32, 0:32])
                        return r
                    pr.t(tca); pr.phase()
                    def vca2(vector, dst=dst):
                        r = None
                        for c in range(NCH):
                            r = vector.tensor_copy(dst[:, c * 32:(c + 1) * 32],
                                                   pstc[:, c * 128:c * 128 + 32])
                        return r
                    pr.v(vca2); pr.phase()
            pr.s(lambda scalar: scalar.activation(expcaT[:], caT[:], AF.Exp))
            pr.phase()
            def vdk(vector):
                return vector.tensor_copy(det_sb[:, 0:16], ca_sb[:, Q - 1::Q])
            pr.v(vdk); pr.phase()
            pr.s(lambda scalar: scalar.activation(det_sb[:, 0:16], det_sb[:, 0:16], AF.Exp))
            pr.phase()
            pr.dma(dk_sp, det_sb[:, 0:16])
            pr.phase()
            pr.dma(dchk_row[:], dk_sp.rearrange("p c -> (p c)").unsqueeze(0))
            pr.phase()
            with nc.psum_tensor(f"{pf}psk{b}", [128, 512], F32) as psk:
                pr.t(lambda tensor: tensor.matmul(psk[:], onesf[:, 0:128], dchk_row[:],
                                                  start=True, stop=True))
                pr.phase()
                pr.v(lambda vector: vector.tensor_copy(dchkr[:], psk[:]))
                pr.phase()

            pr.v(lambda vector: vector.memset(S_sb[:], 0.0))
            pr.phase()
            with ExitStack() as ces:
                op_sb = ces.enter_context(nc.sbuf_tensor(f"{pf}op{b}", [128, 16 * 1024], BF16))
                self32x = ces.enter_context(nc.sbuf_tensor(f"{pf}sel32{b}", [32, 32 * 128], F32))
                dif = ces.enter_context(nc.sbuf_tensor(f"{pf}dif{b}", [128, 1024], F32))
                mh = ces.enter_context(nc.sbuf_tensor(f"{pf}mh{b}", [128, 32 * 128], BF16))
                cbm = ces.enter_context(nc.sbuf_tensor(f"{pf}cbm{b}", [128, 128], BF16))
                bts = ces.enter_context(nc.sbuf_tensor(f"{pf}bts{b}", [128, 128], BF16))
                xdt = ces.enter_context(nc.sbuf_tensor(f"{pf}xdt{b}", [128, 2048], BF16))
                xw = ces.enter_context(nc.sbuf_tensor(f"{pf}xw{b}", [128, 2048], BF16))
                xtc = ces.enter_context(nc.sbuf_tensor(f"{pf}xtc{b}", [128, 2048], BF16))
                ygc = ces.enter_context(nc.sbuf_tensor(f"{pf}ygc{b}", [128, 2048], BF16))
                f32s = ces.enter_context(nc.sbuf_tensor(f"{pf}f32s{b}", [128, 2048], F32))
                g32s = ces.enter_context(nc.sbuf_tensor(f"{pf}g32s{b}", [128, 2048], F32))
                ygT = ces.enter_context(nc.sbuf_tensor(f"{pf}ygT{b}", [128, 2048], BF16))
                ot8 = ces.enter_context(nc.sbuf_tensor(f"{pf}ot8{b}", [128, 1024], BF16))
                psY = ces.enter_context(nc.psum_tensor(f"{pf}psY{b}", [128, 1024], F32))
                psY2 = ces.enter_context(nc.psum_tensor(f"{pf}psY2{b}", [128, 1024], F32))
                pstx = ces.enter_context(nc.psum_tensor(f"{pf}pstx{b}", [128, 1024], BF16))
                psS = ces.enter_context(nc.psum_tensor(f"{pf}psS{b}", [128, 512], F32))
                psP = ces.enter_context(nc.psum_tensor(f"{pf}psP{b}", [128, 1024], F32))
                pr.dma(op_sb[:], outprojT, queue="g")
                pr.dma(self32x[:], cselap)
                pr.phase()
                for c in range(NCH):
                    t0 = c * Q
                    for rd in range(4):
                        def tsel(tensor, rd=rd, t0=t0):
                            r = None
                            for i in range(8):
                                h = rd * 8 + i
                                ps = psY if i < 4 else psY2
                                r = tensor.matmul(ps[:, (i % 4) * 128:(i % 4) * 128 + 128],
                                                  self32x[:, h * 128:(h + 1) * 128],
                                                  ca_sb[:, t0:t0 + Q],
                                                  start=True, stop=True)
                            return r
                        pr.t(tsel); pr.phase()
                        def vdif(vector, rd=rd, c=c):
                            r = None
                            for i in range(8):
                                h = rd * 8 + i
                                ps = (psY if i < 4 else psY2)
                                r = vector.tensor_scalar(
                                    dif[:, i * 128:(i + 1) * 128],
                                    ps[:, (i % 4) * 128:(i % 4) * 128 + 128],
                                    caT[:, c * 32 + h:c * 32 + h + 1],
                                    0.0, ALU.subtract, ALU.min)
                            return r
                        pr.v(vdif); pr.phase()
                        def sexp(scalar, rd=rd):
                            return scalar.activation(mh[:, rd * 1024:(rd + 1) * 1024],
                                                     dif[:], AF.Exp)
                        pr.s(sexp); pr.phase()
                    def tcb(tensor, t0=t0):
                        return tensor.matmul(psS[0:128, 0:128], BT_sb[:, t0:t0 + Q],
                                             CT_sb[:, t0:t0 + Q], start=True, stop=True)
                    pr.t(tcb); pr.phase()
                    pr.v(lambda vector: vector.tensor_tensor(cbm[:], psS[:, 0:128],
                                                             pr.tri16[:], ALU.mult))
                    pr.phase()
                    def vmh(vector):
                        r = None
                        for h in range(32):
                            r = vector.tensor_tensor(mh[:, h * 128:(h + 1) * 128],
                                                     mh[:, h * 128:(h + 1) * 128],
                                                     cbm[:], ALU.mult)
                        return r
                    pr.v(vmh); pr.phase()
                    xv2 = xbc_sp.rearrange("p (k t) -> p k t", t=T)
                    pr.need(spl[0])
                    pr.dma(xw[:].rearrange("p (k t) -> p k t", t=128), xv2[:, :, t0:t0 + Q])
                    pr.phase()
                    for rd in range(2):
                        def txt(tensor, rd=rd, t0=t0):
                            r = None
                            for i in range(8):
                                kt = rd * 8 + i
                                r = tensor.transpose(pstx[:, i * 128:(i + 1) * 128],
                                                     xw[:, kt * 128:(kt + 1) * 128],
                                                     pr.ident16[:])
                            return r
                        pr.t(txt); pr.phase()
                        def vxt(vector, rd=rd):
                            return vector.tensor_copy(xtc[:, rd * 1024:(rd + 1) * 1024],
                                                      pstx[:, 0:1024])
                        pr.v(vxt); pr.phase()
                    def vxdt(vector, c=c):
                        dtv = dtT[:, c * 32:(c + 1) * 32]
                        bc = dtv.unsqueeze(2).broadcast_to([128, 32, 64])
                        return vector.tensor_tensor(
                            xdt[:].rearrange("p (h q) -> p h q", q=64),
                            xtc[:].rearrange("p (h q) -> p h q", q=64), bc, ALU.mult)
                    pr.v(vxdt); pr.phase()
                    def vxw(vector, c=c):
                        dv2 = decE[:, c * 32:(c + 1) * 32]
                        bc = dv2.unsqueeze(2).broadcast_to([128, 32, 64])
                        return vector.tensor_tensor(
                            xw[:].rearrange("p (h q) -> p h q", q=64),
                            xdt[:].rearrange("p (h q) -> p h q", q=64), bc, ALU.mult)
                    pr.v(vxw); pr.phase()
                    pr.t(lambda tensor, t0=t0: tensor.transpose(
                        pstx[:, 0:128], BT_sb[:, t0:t0 + Q], pr.ident16[:]))
                    pr.phase()
                    pr.v(lambda vector: vector.tensor_copy(bts[:], pstx[:, 0:128]))
                    pr.phase()
                    for hh in range(2):
                        def tin(tensor, hh=hh, t0=t0):
                            r = None
                            for i in range(16):
                                h = hh * 16 + i
                                tensor.matmul(psY[:, i * 64:(i + 1) * 64],
                                              mh[:, h * 128:(h + 1) * 128],
                                              xdt[:, h * 64:(h + 1) * 64],
                                              start=True, stop=True)
                                r = tensor.matmul(psY2[:, i * 64:(i + 1) * 64],
                                                  CT_sb[:, t0:t0 + Q],
                                                  S_sb[:, h * 64:(h + 1) * 64],
                                                  start=True, stop=True)
                            return r
                        pr.t(tin); pr.phase()
                        def vev(vector, hh=hh, c=c):
                            r = None
                            for i in range(16):
                                h = hh * 16 + i
                                vector.tensor_scalar(
                                    f32s[:, i * 64:(i + 1) * 64],
                                    psY2[:, i * 64:(i + 1) * 64],
                                    expcaT[:, c * 32 + h:c * 32 + h + 1], None, ALU.mult)
                                r = vector.tensor_tensor(
                                    ygc[:, h * 64:(h + 1) * 64],
                                    f32s[:, i * 64:(i + 1) * 64],
                                    psY[:, i * 64:(i + 1) * 64], ALU.add)
                            return r
                        pr.v(vev); pr.phase()
                    for rd in range(4):
                        def tst(tensor, rd=rd):
                            r = None
                            for i in range(8):
                                h = rd * 8 + i
                                r = tensor.matmul(psS[:, i * 64:(i + 1) * 64], bts[:],
                                                  xw[:, h * 64:(h + 1) * 64],
                                                  start=True, stop=True)
                            return r
                        pr.t(tst); pr.phase()
                        def vst(vector, rd=rd, c=c):
                            r = None
                            for i in range(8):
                                h = rd * 8 + i
                                r = vector.scalar_tensor_tensor(
                                    S_sb[:, h * 64:(h + 1) * 64],
                                    S_sb[:, h * 64:(h + 1) * 64],
                                    dchkr[:, h * 16 + c:h * 16 + c + 1],
                                    psS[:, i * 64:(i + 1) * 64], ALU.mult, ALU.add)
                            return r
                        pr.v(vst); pr.phase()
                    def vsk(vector):
                        vector.tensor_tensor(f32s[:], xtc[:], drep[:], ALU.mult)
                        return vector.tensor_tensor(ygc[:], ygc[:], f32s[:], ALU.add)
                    pr.v(vsk); pr.phase()
                    zv = z0_sp.rearrange("p (k t) -> p k t", t=T)
                    pr.need(spl[0])
                    pr.dma(xw[:].rearrange("p (k t) -> p k t", t=128), zv[:, :, t0:t0 + Q])
                    pr.phase()
                    for rd in range(2):
                        def tzt(tensor, rd=rd):
                            r = None
                            for i in range(8):
                                kt = rd * 8 + i
                                r = tensor.transpose(pstx[:, i * 128:(i + 1) * 128],
                                                     xw[:, kt * 128:(kt + 1) * 128],
                                                     pr.ident16[:])
                            return r
                        pr.t(tzt); pr.phase()
                        def vzt(vector, rd=rd):
                            return vector.tensor_copy(xtc[:, rd * 1024:(rd + 1) * 1024],
                                                      pstx[:, 0:1024])
                        pr.v(vzt); pr.phase()
                    pr.s(lambda scalar: scalar.activation(g32s[:], xtc[:], AF.Silu))
                    pr.phase()
                    def vsl(vector):
                        return vector.tensor_tensor(ygc[:], ygc[:], g32s[:], ALU.mult)
                    pr.v(vsl); pr.phase()
                    pr.s(lambda scalar: scalar.activation(g32s[:], ygc[:], AF.Square,
                                                          accum_out=stat[:, 0:1]))
                    pr.phase()
                    pr.v(lambda vector: vector.tensor_scalar(
                        stat[:, 1:2], stat[:, 0:1], 1.0 / 2048.0, EPS, ALU.mult, ALU.add))
                    pr.phase()
                    pr.s(lambda scalar: scalar.activation(stat[:, 2:3], stat[:, 1:2], AF.Ln))
                    pr.phase()
                    pr.s(lambda scalar: scalar.activation(stat[:, 3:4], stat[:, 2:3],
                                                          AF.Exp, scale=-0.5))
                    pr.phase()
                    def vrm2(vector):
                        vector.tensor_scalar(ygc[:], ygc[:], stat[:, 3:4], None, ALU.mult)
                        return vector.tensor_tensor(ygc[:], ygc[:], nwrep[:], ALU.mult)
                    pr.v(vrm2); pr.phase()
                    for rd in range(2):
                        def tyg(tensor, rd=rd):
                            r = None
                            for i in range(8):
                                kt = rd * 8 + i
                                r = tensor.transpose(pstx[:, i * 128:(i + 1) * 128],
                                                     ygc[:, kt * 128:(kt + 1) * 128],
                                                     pr.ident16[:])
                            return r
                        pr.t(tyg); pr.phase()
                        def vyg(vector, rd=rd):
                            return vector.tensor_copy(ygT[:, rd * 1024:(rd + 1) * 1024],
                                                      pstx[:, 0:1024])
                        pr.v(vyg); pr.phase()
                    def top_(tensor):
                        r = None
                        for dm in range(8):
                            for kt in range(16):
                                r = tensor.matmul(
                                    psP[:, dm * 128:(dm + 1) * 128],
                                    op_sb[:, kt * 1024 + dm * 128:kt * 1024 + dm * 128 + 128],
                                    ygT[:, kt * 128:(kt + 1) * 128],
                                    start=(kt == 0), stop=(kt == 15))
                        return r
                    pr.t(top_); pr.phase()
                    pr.v(lambda vector: vector.tensor_copy(ot8[:], psP[:, 0:1024]))
                    pr.phase()
                    def tot(tensor):
                        r = None
                        for dm in range(8):
                            r = tensor.transpose(pstx[:, dm * 128:(dm + 1) * 128],
                                                 ot8[:, dm * 128:(dm + 1) * 128],
                                                 pr.ident16[:])
                        return r
                    pr.t(tot)
                    pr.need(y0hs[c])
                    pr.dma(xtc[:, 0:DM], y0_sp[:, c * DM:(c + 1) * DM])
                    pr.dma(f32s[:, 0:DM], xin[b, c * 128:(c + 1) * 128, :])
                    pr.phase()
                    def vres(vector):
                        vector.tensor_tensor(f32s[:, 0:DM], f32s[:, 0:DM],
                                             xtc[:, 0:DM], ALU.add)
                        return vector.tensor_tensor(f32s[:, 0:DM], f32s[:, 0:DM],
                                                    pstx[:, 0:1024], ALU.add)
                    pr.v(vres); pr.phase()
                    pr.dma(xout[b, c * 128:(c + 1) * 128, :], f32s[:, 0:DM])
                    pr.phase()
    es.close()
    pr.phase()


def build_l1_attn():
    nc = bass.Bass("TRN2", target_bir_lowering=False)
    xin = nc.dram_tensor("x", [NB, T, DM], F32, kind="ExternalInput").ap()
    xout = nc.dram_tensor("xo", [NB, T, DM], F32, kind="ExternalOutput").ap()
    W = _l1_declare(nc)
    with ExitStack() as es:
        pr = Pr(nc, es)
        load_consts(pr, nc, sel=True)
        emit_l1(nc, pr, xin, xout, W, "B_")
        pr.finish()
    return nc


def _l1_declare(nc, pf=""):
    W = {}
    for d in ("fw", "bw"):
        W[d] = dict(
            winT=nc.dram_tensor(f"{pf}{d}_winT", [128, 8 * 4096], F32, kind="ExternalInput").ap(),
            convw=nc.dram_tensor(f"{pf}{d}_convw", [128, 16 * 4], F32, kind="ExternalInput").ap(),
            convb=nc.dram_tensor(f"{pf}{d}_convb", [128, 16], F32, kind="ExternalInput").ap(),
            wxT=nc.dram_tensor(f"{pf}{d}_wxT", [128, 16 * 96], F32, kind="ExternalInput").ap(),
            wdtT=nc.dram_tensor(f"{pf}{d}_wdtT", [64, 2048], F32, kind="ExternalInput").ap(),
            bdt=nc.dram_tensor(f"{pf}{d}_bdt", [128, 16], F32, kind="ExternalInput").ap(),
            alog=nc.dram_tensor(f"{pf}{d}_alog", [128, 16 * 16], F32, kind="ExternalInput").ap(),
            dvec=nc.dram_tensor(f"{pf}{d}_dvec", [128, 16], F32, kind="ExternalInput").ap(),
            woutT=nc.dram_tensor(f"{pf}{d}_woutT", [128, 16 * 1024], F32,
                                 kind="ExternalInput").ap(),
        )
    W["fuseWT"] = nc.dram_tensor(pf + "fuseWT", [128, 16 * 1024], F32,
                                 kind="ExternalInput").ap()
    W["fuseb"] = nc.dram_tensor(pf + "fuseb", [128, 8], F32, kind="ExternalInput").ap()
    return W


def emit_l1(nc, pr, xin, xout, Wd, pf):
    DI = 2048
    NDT = DI // 128
    N = 16
    TH = 512
    NSEG = T // TH
    W = {"fw": Wd["fw"], "bw": Wd["bw"]}
    fuseWT = Wd["fuseWT"]
    fuseb = Wd["fuseb"]
    u_sp = nc.dram_tensor(pf + "u_sp", [128, NDT * T], BF16, kind="Internal").ap()
    dt_sp = nc.dram_tensor(pf + "dt_sp", [128, NDT * T], BF16, kind="Internal").ap()
    z_sp = nc.dram_tensor(pf + "z_sp", [128, NDT * T], BF16, kind="Internal").ap()
    y_sp = nc.dram_tensor(pf + "y_sp", [128, NDT * T], BF16, kind="Internal").ap()
    fo_sp = nc.dram_tensor(pf + "fo_sp", [128, 8 * T], BF16, kind="Internal").ap()
    bo_sp = nc.dram_tensor(pf + "bo_sp", [128, 8 * T], BF16, kind="Internal").ap()
    if True:
        es = ExitStack()
        sel16 = es.enter_context(nc.sbuf_tensor(pf + "sel16", [32, 32 * 128], BF16))
        pr.dma(sel16[:], pr.csel, queue="g")
        pr.phase()
        stat = es.enter_context(nc.sbuf_tensor(pf + "stat", [128, 8], F32))
        xnT = es.enter_context(nc.sbuf_tensor(pf + "xnT", [128, 8 * T], BF16))
        wx_sb = es.enter_context(nc.sbuf_tensor(pf + "wx_sb", [128, 16 * 96], BF16))
        wdt_sb = es.enter_context(nc.sbuf_tensor(pf + "wdt_sb", [64, 2048], BF16))
        cw_sb = es.enter_context(nc.sbuf_tensor(pf + "cw_sb", [128, 16 * 4], F32))
        cb_sb = es.enter_context(nc.sbuf_tensor(pf + "cb_sb", [128, 16], F32))
        bdt_sb = es.enter_context(nc.sbuf_tensor(pf + "bdt_sb", [128, 16], F32))
        dv_sb = es.enter_context(nc.sbuf_tensor(pf + "dv_sb", [128, 16], F32))
        aneg = es.enter_context(nc.sbuf_tensor(pf + "aneg", [128, 16 * 16], F32))
        ci = es.enter_context(nc.sbuf_tensor(pf + "ci", [128, T + 4], BF16))
        wslc = es.enter_context(nc.sbuf_tensor(pf + "wslc", [128, 8 * 128], BF16))
        fuseb_sb = es.enter_context(nc.sbuf_tensor(pf + "fuseb_sb", [128, 8], F32))
        bct_sb = es.enter_context(nc.sbuf_tensor(pf + "bct_sb", [32, T], BF16))
        pr.dma(fuseb_sb[:], fuseb)
        pr.phase()

        for b in range(NB):
            with nc.sbuf_tensor(f"{pf}lnx{b}", [128, DM], F32) as xs, \
                 nc.sbuf_tensor(f"{pf}xnb{b}", [128, DM], BF16) as xnb, \
                 nc.sbuf_tensor(f"{pf}sq{b}", [128, DM], F32) as sqs, \
                 nc.psum_tensor(f"{pf}pstT{b}", [128, 2048], BF16) as pst:
                pr.scr_sq = sqs
                for tt in range(NTT):
                    pr.dma(xs[:], xin[b, tt * 128:(tt + 1) * 128, :])
                    pr.phase()
                    emit_ln(pr, xs, xnb, 1, DM, stat)
                    emit_transpose(pr, pst, [(xnb, kd * 128, xnT, kd * T + tt * 128)
                                             for kd in range(8)])

            foh = [0]
            for di, d in enumerate(("fw", "bw")):
                wt = W[d]
                if di == 1:
                    def rev(vector):
                        r = None
                        for kd in range(8):
                            vector.tensor_copy(ci[:, 0:T], xnT[:, kd * T:(kd + 1) * T])
                            src = ci[:, 0:T]
                            r = vector.tensor_copy(xnT[:, kd * T:(kd + 1) * T], src[:, ::-1])
                        return r
                    pr.v(rev)
                    pr.phase()

                pr.dma(wx_sb[:], wt["wxT"], queue="g")
                pr.dma(wdt_sb[:], wt["wdtT"], queue="g")
                pr.dma(cw_sb[:], wt["convw"])
                pr.dma(cb_sb[:], wt["convb"])
                pr.dma(bdt_sb[:], wt["bdt"])
                pr.dma(dv_sb[:], wt["dvec"])
                pr.dma(aneg[:], wt["alog"])
                pr.phase()
                pr.s(lambda scalar: scalar.activation(aneg[:], aneg[:], AF.Exp))
                pr.phase()
                pr.v(lambda vector: vector.tensor_scalar(aneg[:], aneg[:], -1.0, None, ALU.mult))
                pr.phase()

                l1xz, _ = nc.enter_named_scope(f"l1xz{b}{di}", False)
                cih = [0]
                spl = [0]
                dtscope = ExitStack()
                dtp_sb = dtscope.enter_context(
                    nc.sbuf_tensor(f"{pf}dtp{b}{di}", [64, T], BF16))
                with nc.psum_tensor(f"{pf}psA{b}{di}", [128, 2048], F32) as psA, \
                     nc.psum_tensor(f"{pf}psX{b}{di}", [128, 2048], F32) as psX, \
                     nc.sbuf_tensor(f"{pf}ciB{b}{di}", [128, T + 4], BF16) as ciB, \
                     nc.sbuf_tensor(f"{pf}wslB{b}{di}", [128, 8 * 128], BF16) as wslc2, \
                     nc.sbuf_tensor(f"{pf}e32a{b}{di}", [128, T], F32) as e32, \
                     nc.sbuf_tensor(f"{pf}f32a{b}{di}", [128, T], F32) as f32b:
                    cis = [ci, ciB]
                    fbs = [f32b, e32]
                    wsl = [wslc, wslc2]
                    wh = [0, 0]
                    sph = [0] * 32
                    wv = wt["winT"].rearrange("p (kt o) -> p kt o", o=4096)

                    def ldw(ot):
                        wh[ot % 2] = pr.dma(
                            wsl[ot % 2][:].rearrange("p (kt o) -> p kt o", o=128),
                            wv[:, :, ot * 128:(ot + 1) * 128], queue="g", lazy=True)

                    def mm(ot):
                        pr.need(wh[ot % 2])
                        def tf(tensor, ot=ot):
                            r = None
                            for n0 in range(0, T, 512):
                                for kd in range(8):
                                    r = tensor.matmul(
                                        psA[:, n0:n0 + 512],
                                        wsl[ot % 2][:, kd * 128:(kd + 1) * 128],
                                        xnT[:, kd * T + n0:kd * T + n0 + 512],
                                        start=(kd == 0), stop=(kd == 7))
                            return r
                        pr.t(tf)

                    def evconv(ot):
                        p = ot % 2
                        if ot >= 2:
                            pr.need(sph[ot - 2])
                        if ot < 16:
                            def vc0(vector, ot=ot, p=p):
                                vector.memset(cis[p][:, 0:4], 0.0)
                                vector.tensor_copy(cis[p][:, 4:4 + T], psA[:, 0:T])
                                vector.tensor_scalar(fbs[p][:, 0:T], cis[p][:, 1:1 + T],
                                                     cw_sb[:, ot * 4:ot * 4 + 1], None,
                                                     ALU.mult)
                                for k in (1, 2, 3):
                                    vector.scalar_tensor_tensor(
                                        fbs[p][:, 0:T], cis[p][:, 1 + k:1 + k + T],
                                        cw_sb[:, ot * 4 + k:ot * 4 + k + 1],
                                        fbs[p][:, 0:T], ALU.mult, ALU.add)
                                return vector.tensor_scalar(
                                    fbs[p][:, 0:T], fbs[p][:, 0:T], cb_sb[:, ot:ot + 1],
                                    None, ALU.add)
                            pr.v(vc0)
                        else:
                            pr.v(lambda vector, p=p: vector.tensor_copy(
                                cis[p][:, 4:4 + T], psA[:, 0:T]))

                    def silu(ot):
                        if ot < 16:
                            p = ot % 2
                            pr.s(lambda scalar, p=p: scalar.activation(
                                cis[p][:, 4:4 + T], fbs[p][:, 0:T], AF.Silu))

                    def post(ot):
                        p = ot % 2
                        if ot < 16:
                            def tfx(tensor, ot=ot, p=p):
                                r = None
                                for n0 in range(0, T, 512):
                                    r = tensor.matmul(psX[0:96, n0:n0 + 512],
                                                      wx_sb[:, ot * 96:(ot + 1) * 96],
                                                      cis[p][:, 4 + n0:4 + n0 + 512],
                                                      start=(ot == 0), stop=(ot == 15))
                                return r
                            pr.t(tfx)
                            h = pr.dma(u_sp[:, ot * T:(ot + 1) * T], cis[p][:, 4:4 + T],
                                       queue="g", lazy=True)
                        else:
                            h = pr.dma(z_sp[:, (ot - 16) * T:(ot - 15) * T],
                                       cis[p][:, 4:4 + T], queue="g", lazy=True)
                        sph[ot] = h
                        spl[0] = max(spl[0], h)

                    ldw(0); pr.phase()
                    for ot in range(33):
                        if ot < 32:
                            mm(ot)
                        if ot >= 1:
                            silu(ot - 1)
                        if ot + 1 < 32:
                            ldw(ot + 1)
                        pr.phase()
                        if ot < 32:
                            evconv(ot)
                        if ot >= 1:
                            post(ot - 1)
                        pr.phase()
                    def vx(vector):
                        vector.tensor_copy(dtp_sb[:, 0:T], psX[0:64, 0:T])
                        return vector.tensor_copy(bct_sb[:, 0:T], psX[64:96, 0:T])
                    pr.v(vx); pr.phase()

                with nc.psum_tensor(f"{pf}psB{b}{di}", [128, 2048], F32) as psB, \
                     nc.sbuf_tensor(f"{pf}e32b{b}{di}", [128, T], F32) as e32, \
                     nc.sbuf_tensor(f"{pf}f32c{b}{di}", [128, T], F32) as f32b:
                    for kt in range(NDT):
                        def tf2(tensor, kt=kt):
                            r = None
                            for n0 in range(0, T, 512):
                                r = tensor.matmul(psB[:, n0:n0 + 512],
                                                  wdt_sb[:, kt * 128:(kt + 1) * 128],
                                                  dtp_sb[:, n0:n0 + 512],
                                                  start=True, stop=True)
                            return r
                        pr.t(tf2); pr.phase()
                        def vd0(vector, kt=kt):
                            vector.tensor_scalar(f32b[:, 0:T], psB[:, 0:T],
                                                 bdt_sb[:, kt:kt + 1], None, ALU.add)
                            return vector.tensor_single_scalar(
                                e32[:, 0:T].bitcast(U32), f32b[:, 0:T].bitcast(U32),
                                0x7FFFFFFF, ALU.bitwise_and)
                        pr.v(vd0); pr.phase()
                        def sd1(scalar):
                            scalar.activation(e32[:, 0:T], e32[:, 0:T], AF.Exp, scale=-1.0)
                            return scalar.activation(e32[:, 0:T], e32[:, 0:T], AF.Ln, bias=1.0)
                        pr.s(sd1); pr.phase()
                        pr.need(cih[0])
                        def vd2(vector):
                            return vector.scalar_tensor_tensor(
                                ci[:, 4:4 + T], f32b[:, 0:T], 0.0, e32[:, 0:T],
                                ALU.max, ALU.add)
                        pr.v(vd2)
                        pr.phase()
                        h = pr.dma(dt_sp[:, kt * T:(kt + 1) * T], ci[:, 4:4 + T],
                                   queue="g", lazy=True)
                        cih[0] = h; spl[0] = max(spl[0], h)
                        pr.phase()
                dtscope.close()
                nc.leave_named_scope(f"l1xz{b}{di}", l1xz, False)

                yh = [0]
                l1sc, _ = nc.enter_named_scope(f"l1scan{b}{di}", False)
                with nc.sbuf_tensor(f"{pf}brep{b}{di}", [128, N * TH], BF16) as brep, \
                     nc.sbuf_tensor(f"{pf}crep{b}{di}", [128, N * TH], BF16) as crep, \
                     nc.sbuf_tensor(f"{pf}su{b}{di}", [128, NDT * TH], BF16) as s_u, \
                     nc.sbuf_tensor(f"{pf}sdt{b}{di}", [128, NDT * TH], BF16) as s_dt, \
                     nc.sbuf_tensor(f"{pf}sa0{b}{di}", [128, NDT * TH], BF16) as dA0, \
                     nc.sbuf_tensor(f"{pf}sa1{b}{di}", [128, NDT * TH], BF16) as dA1, \
                     nc.sbuf_tensor(f"{pf}sh{b}{di}", [128, NDT * TH], BF16) as h_h, \
                     nc.sbuf_tensor(f"{pf}shB{b}{di}", [128, NDT * TH], BF16) as h_hB, \
                     nc.sbuf_tensor(f"{pf}sy{b}{di}", [128, NDT * TH], BF16) as ya_h, \
                     nc.sbuf_tensor(f"{pf}hcar{b}{di}", [128, NDT * N], F32) as hcar:
                    for seg in range(NSEG):
                        t0 = seg * TH
                        with nc.psum_tensor(f"{pf}psR{b}{di}{seg}", [128, 2048], F32) as psR:
                            for n0 in range(0, N, 2):
                                def tr2(tensor, n0=n0, t0=t0):
                                    r = None
                                    for i, n in enumerate((n0, n0 + 1)):
                                        tensor.matmul(psR[:, i * 1024:i * 1024 + 512],
                                                      sel16[:, n * 128:(n + 1) * 128],
                                                      bct_sb[:, t0:t0 + TH],
                                                      start=True, stop=True)
                                        r = tensor.matmul(
                                            psR[:, i * 1024 + 512:i * 1024 + 1024],
                                            sel16[:, (16 + n) * 128:(17 + n) * 128],
                                            bct_sb[:, t0:t0 + TH],
                                            start=True, stop=True)
                                    return r
                                pr.t(tr2); pr.phase()
                                def vr2(vector, n0=n0):
                                    r = None
                                    for i, n in enumerate((n0, n0 + 1)):
                                        vector.tensor_copy(brep[:, n * TH:(n + 1) * TH],
                                                           psR[:, i * 1024:i * 1024 + 512])
                                        r = vector.tensor_copy(
                                            crep[:, n * TH:(n + 1) * TH],
                                            psR[:, i * 1024 + 512:i * 1024 + 1024])
                                    return r
                                pr.v(vr2); pr.phase()

                        uv = u_sp.rearrange("p (k t) -> p k t", t=T)
                        dv = dt_sp.rearrange("p (k t) -> p k t", t=T)
                        pr.need(spl[0])
                        pr.dma(s_u[:].rearrange("p (k t) -> p k t", t=TH),
                               uv[:, :, t0:t0 + TH])
                        pr.dma(s_dt[:].rearrange("p (k t) -> p k t", t=TH),
                               dv[:, :, t0:t0 + TH])
                        pr.phase()
                        pr.need(yh[0])
                        def vseed(vector):
                            r = None
                            for j in range(NDT):
                                sl = slice(j * TH, (j + 1) * TH)
                                r = vector.tensor_scalar(ya_h[:, sl], s_u[:, sl],
                                                         dv_sb[:, j:j + 1], None, ALU.mult)
                            return r
                        pr.v(vseed); pr.phase()
                        def vw(vector):
                            return vector.tensor_tensor(s_u[:], s_dt[:], s_u[:], ALU.mult)
                        pr.v(vw)
                        def sa0(scalar):
                            r = None
                            for j in range(NDT):
                                r = scalar.activation(
                                    dA0[:, j * TH:(j + 1) * TH],
                                    s_dt[:, j * TH:(j + 1) * TH], AF.Exp,
                                    scale=aneg[:, j * 16:j * 16 + 1])
                            return r
                        pr.s(sa0); pr.phase()

                        for n in range(N):
                            dAc = dA0 if n % 2 == 0 else dA1
                            dAn = dA1 if n % 2 == 0 else dA0
                            hb = h_h if n % 2 == 0 else h_hB
                            hprev = h_hB if n % 2 == 0 else h_h
                            if n + 1 < N:
                                def sa(scalar, n=n, dAn=dAn):
                                    r = None
                                    for j in range(NDT):
                                        r = scalar.activation(
                                            dAn[:, j * TH:(j + 1) * TH],
                                            s_dt[:, j * TH:(j + 1) * TH], AF.Exp,
                                            scale=aneg[:, j * 16 + n + 1:j * 16 + n + 2])
                                    return r
                                pr.s(sa)
                            if n >= 1:
                                def gacc(vector, hprev=hprev):
                                    return vector.tensor_tensor(ya_h[:], ya_h[:], hprev[:],
                                                                ALU.add)
                                pr.v(gacc)
                            def vscan(vector, n=n, dAc=dAc, seg=seg, hb=hb):
                                bv = brep[:, n * TH:(n + 1) * TH].unsqueeze(1) \
                                    .broadcast_to([128, NDT, TH])
                                cv = crep[:, n * TH:(n + 1) * TH].unsqueeze(1) \
                                    .broadcast_to([128, NDT, TH])
                                hv = hb[:].rearrange("p (k t) -> p k t", t=TH)
                                uv3 = s_u[:].rearrange("p (k t) -> p k t", t=TH)
                                vector.tensor_tensor(hv, uv3, bv, ALU.mult)
                                for j in range(NDT):
                                    sl = slice(j * TH, (j + 1) * TH)
                                    col = j * N + n
                                    ini = 0.0 if seg == 0 else hcar[:, col:col + 1]
                                    vector.tensor_tensor_scan(hb[:, sl], dAc[:, sl],
                                                              hb[:, sl], ini,
                                                              ALU.mult, ALU.add)
                                vector.tensor_copy(hcar[:, n::N], hb[:, TH - 1::TH])
                                return vector.tensor_tensor(hv, hv, cv, ALU.mult)
                            pr.v(vscan); pr.phase()
                        def gacc2(vector):
                            return vector.tensor_tensor(ya_h[:], ya_h[:],
                                                        (h_hB if (N - 1) % 2 else h_h)[:],
                                                        ALU.add)
                        pr.v(gacc2); pr.phase()

                        yv = y_sp.rearrange("p (k t) -> p k t", t=T)
                        yh[0] = pr.dma(yv[:, :, t0:t0 + TH],
                                       ya_h[:].rearrange("p (k t) -> p k t", t=TH),
                                       queue="g", lazy=True)
                        pr.phase()

                nc.leave_named_scope(f"l1scan{b}{di}", l1sc, False)
                l1ga, _ = nc.enter_named_scope(f"l1gate{b}{di}", False)
                osp = fo_sp if di == 0 else bo_sp
                with nc.sbuf_tensor(f"{pf}gy{b}{di}", [128, 16 * 512], BF16) as gy, \
                     nc.sbuf_tensor(f"{pf}gz{b}{di}", [128, 16 * 512], BF16) as gz, \
                     nc.sbuf_tensor(f"{pf}wo{b}{di}", [128, 8 * 1024], BF16) as wo_sb, \
                     nc.sbuf_tensor(f"{pf}e32g{b}{di}", [128, T], F32) as e32, \
                     nc.sbuf_tensor(f"{pf}f32g{b}{di}", [128, T], F32) as f32b, \
                     nc.psum_tensor(f"{pf}psO{b}{di}", [128, 8 * 512], F32) as psO:
                    for q0 in range(0, T, 512):
                        yv = y_sp.rearrange("p (k t) -> p k t", t=T)
                        zv = z_sp.rearrange("p (k t) -> p k t", t=T)
                        pr.need(yh[0]); pr.need(spl[0]); pr.need(foh[0])
                        pr.dma(gy[:].rearrange("p (k t) -> p k t", t=512),
                               yv[:, :, q0:q0 + 512])
                        pr.dma(gz[:].rearrange("p (k t) -> p k t", t=512),
                               zv[:, :, q0:q0 + 512])
                        pr.phase()
                        def sg(scalar):
                            return scalar.activation(gz[:], gz[:], AF.Silu)
                        pr.s(sg); pr.phase()
                        def vg2(vector):
                            return vector.tensor_tensor(gy[:], gy[:], gz[:], ALU.mult)
                        pr.v(vg2); pr.phase()
                        for hh in range(2):
                            wv2 = wt["woutT"].rearrange("p (kt o) -> p kt o", o=1024)
                            pr.dma(wo_sb[:].rearrange("p (kt o) -> p kt o", o=1024),
                                   wv2[:, hh * 8:(hh + 1) * 8, :], queue="g")
                            pr.phase()
                            def tw(tensor, hh=hh):
                                r = None
                                for k in range(8):
                                    kt = hh * 8 + k
                                    for dm in range(8):
                                        r = tensor.matmul(
                                            psO[:, dm * 512:(dm + 1) * 512],
                                            wo_sb[:, k * 1024 + dm * 128:
                                                  k * 1024 + dm * 128 + 128],
                                            gy[:, kt * 512:(kt + 1) * 512],
                                            start=(kt == 0), stop=(kt == NDT - 1))
                                return r
                            pr.t(tw); pr.phase()
                        def ve2(vector, di=di):
                            r = None
                            for dm in range(8):
                                dst = gy[:, dm * 512:(dm + 1) * 512]
                                if di == 1:
                                    dst = dst[:, ::-1]
                                r = vector.tensor_copy(dst, psO[:, dm * 512:(dm + 1) * 512])
                            return r
                        pr.v(ve2); pr.phase()
                        for dm in range(8):
                            if di == 0:
                                h = pr.dma(osp[:, dm * T + q0:dm * T + q0 + 512],
                                           gy[:, dm * 512:(dm + 1) * 512], queue="g",
                                           lazy=True)
                            else:
                                h = pr.dma(osp[:, dm * T + T - q0 - 512:dm * T + T - q0],
                                           gy[:, dm * 512:(dm + 1) * 512], queue="g",
                                           lazy=True)
                            foh[0] = h
                        pr.phase()

                nc.leave_named_scope(f"l1gate{b}{di}", l1ga, False)
            with nc.sbuf_tensor(f"{pf}o_t{b}", [128, DM], F32) as o_t, \
                 nc.sbuf_tensor(f"{pf}fsw{b}", [128, 16 * 1024], BF16) as fw_sb, \
                 nc.sbuf_tensor(f"{pf}fio{b}", [128, 16 * 512], BF16) as fio, \
                 nc.sbuf_tensor(f"{pf}fot{b}", [128, 8 * 512], BF16) as fot, \
                 nc.psum_tensor(f"{pf}psF{b}", [128, 4 * 512], F32) as psF, \
                 nc.psum_tensor(f"{pf}pstF{b}", [128, 1024], BF16) as pstF:
                pr.dma(fw_sb[:], fuseWT, queue="g")
                pr.phase()
                for q0 in range(0, T, 512):
                    fv = fo_sp.rearrange("p (k t) -> p k t", t=T)
                    bv = bo_sp.rearrange("p (k t) -> p k t", t=T)
                    pr.need(foh[0])
                    pr.dma(fio[:, 0:8 * 512].rearrange("p (k t) -> p k t", t=512),
                           fv[:, :, q0:q0 + 512])
                    pr.dma(fio[:, 8 * 512:].rearrange("p (k t) -> p k t", t=512),
                           bv[:, :, q0:q0 + 512])
                    pr.phase()
                    for dh in range(2):
                        def tff(tensor, dh=dh):
                            r = None
                            for dmo in range(dh * 4, dh * 4 + 4):
                                for kt in range(16):
                                    r = tensor.matmul(
                                        psF[:, (dmo % 4) * 512:(dmo % 4) * 512 + 512],
                                        fw_sb[:, kt * 1024 + dmo * 128:
                                              kt * 1024 + dmo * 128 + 128],
                                        fio[:, kt * 512:(kt + 1) * 512],
                                        start=(kt == 0), stop=(kt == 15))
                            return r
                        pr.t(tff); pr.phase()
                        def vff(vector, dh=dh):
                            r = None
                            for dmo in range(dh * 4, dh * 4 + 4):
                                r = vector.tensor_scalar(
                                    fot[:, dmo * 512:(dmo + 1) * 512],
                                    psF[:, (dmo % 4) * 512:(dmo % 4) * 512 + 512],
                                    fuseb_sb[:, dmo:dmo + 1], None, ALU.add)
                            return r
                        pr.v(vff); pr.phase()
                    for st in range(4):
                        def tft(tensor, st=st):
                            r = None
                            for dmo in range(8):
                                r = tensor.transpose(pstF[:, dmo * 128:(dmo + 1) * 128],
                                                     fot[:, dmo * 512 + st * 128:
                                                         dmo * 512 + st * 128 + 128],
                                                     pr.ident16[:])
                            return r
                        pr.t(tft)
                        tt = (q0 + st * 128) // 128
                        pr.dma(o_t[:], xin[b, tt * 128:(tt + 1) * 128, :])
                        pr.phase()
                        def vres(vector):
                            return vector.tensor_tensor(o_t[:], o_t[:], pstF[:, 0:1024],
                                                        ALU.add)
                        pr.v(vres); pr.phase()
                        pr.dma(xout[b, tt * 128:(tt + 1) * 128, :], o_t[:])
                        pr.phase()
    es.close()
    pr.phase()


_PROGS = {}


def _tile_rows(w):
    K_, M_ = w.shape[0] // 128, w.shape[1]
    return np.ascontiguousarray(w.reshape(K_, 128, M_).transpose(1, 0, 2).reshape(128, K_ * M_))


def _get_prog(name):
    if name not in _PROGS:
        _PROGS[name] = {"ffn": build_ffn, "l1": build_l1_attn, "l0": build_l0_attn,
                         "full": build_full}[name]()
    return _PROGS[name]


def _run_ffn(x, wg, wu, wd):
    nc = _get_prog("ffn")
    consts = host_consts()
    in_maps = []
    for c in range(NCORES):
        m = {"c_ident": consts["c_ident"]}
        m["x"] = np.ascontiguousarray(x[c * NB:(c + 1) * NB])
        m["wg"] = _tile_rows(wg)
        m["wu"] = _tile_rows(wu)
        m["wd"] = _tile_rows(wd)
        in_maps.append(m)
    res = run_bass_kernel_spmd(nc, in_maps, core_ids=list(range(NCORES)))
    return np.concatenate([r["xo"] for r in res.results], axis=0)


def _tile_vec(v):
    return np.ascontiguousarray(v.reshape(-1, 128).T)


def _l1_weight_maps(I):
    m = {}
    for d in ("fw", "bw"):
        m[f"{d}_winT"] = _tile_rows(np.ascontiguousarray(I[f"L1_attn_{d}_Win"].T))
        m[f"{d}_convw"] = _tile_rows(np.ascontiguousarray(I[f"L1_attn_{d}_convw"][:, 0, :]))
        m[f"{d}_convb"] = _tile_vec(I[f"L1_attn_{d}_convb"])
        m[f"{d}_wxT"] = _tile_rows(np.ascontiguousarray(I[f"L1_attn_{d}_Wx"].T))
        m[f"{d}_wdtT"] = np.ascontiguousarray(I[f"L1_attn_{d}_Wdt"].T)
        m[f"{d}_bdt"] = _tile_vec(I[f"L1_attn_{d}_bdt"])
        m[f"{d}_alog"] = _tile_rows(np.ascontiguousarray(I[f"L1_attn_{d}_Alog"]))
        m[f"{d}_dvec"] = _tile_vec(I[f"L1_attn_{d}_D"])
        m[f"{d}_woutT"] = _tile_rows(np.ascontiguousarray(I[f"L1_attn_{d}_Wout"].T))
    m["fuseWT"] = _tile_rows(np.ascontiguousarray(I["L1_attn_fuse_W"].T))
    m["fuseb"] = _tile_vec(I["L1_attn_fuse_b"])
    return m


def _l0_weight_maps(I):
    H = 32
    D = I["L0_attn_D"]
    hp = np.repeat(D, 64)
    return {
        "inprojT": _tile_rows(np.ascontiguousarray(I["L0_attn_in_proj_w"].T)),
        "convw": _tile_rows(np.ascontiguousarray(I["L0_attn_conv_w"])),
        "convbi": _tile_vec(I["L0_attn_conv_b"]),
        "dtbias": np.ascontiguousarray(I["L0_attn_dt_bias"].reshape(32, 1)),
        "alogi": np.ascontiguousarray(I["L0_attn_A_log"].reshape(32, 1)),
        "dvec_rep": np.ascontiguousarray(np.broadcast_to(hp, (128, 2048))),
        "normw_rep": np.ascontiguousarray(np.broadcast_to(I["L0_attn_norm_w"], (128, 2048))),
        "lnw_rep": np.ascontiguousarray(np.broadcast_to(I["L0_attn_ln_w"], (128, 1024))),
        "lnb_rep": np.ascontiguousarray(np.broadcast_to(I["L0_attn_ln_b"], (128, 1024))),
        "outprojT": _tile_rows(np.ascontiguousarray(I["L0_attn_out_proj_w"].T)),
    }


def _run_l0(x, I):
    nc = _get_prog("l0")
    consts = host_consts()
    wm = _l0_weight_maps(I)
    in_maps = []
    for c in range(NCORES):
        m = {"c_ident": consts["c_ident"], "c_sel": consts["c_sel"],
             "c_tri": consts["c_tri"]}
        m.update(wm)
        m["x"] = np.ascontiguousarray(x[c * NB:(c + 1) * NB])
        in_maps.append(m)
    res = run_bass_kernel_spmd(nc, in_maps, core_ids=list(range(NCORES)))
    return np.concatenate([r["xo"] for r in res.results], axis=0)


def _run_l1(x, I):
    nc = _get_prog("l1")
    consts = host_consts()
    wm = _l1_weight_maps(I)
    in_maps = []
    for c in range(NCORES):
        m = {"c_ident": consts["c_ident"], "c_sel": consts["c_sel"]}
        m.update(wm)
        m["x"] = np.ascontiguousarray(x[c * NB:(c + 1) * NB])
        in_maps.append(m)
    res = run_bass_kernel_spmd(nc, in_maps, core_ids=list(range(NCORES)))
    return np.concatenate([r["xo"] for r in res.results], axis=0)


def build_full():
    I = 2816
    KI = I // 128
    KD = DM // 128
    nc = bass.Bass("TRN2", target_bir_lowering=False)
    xin = nc.dram_tensor("x", [NB, T, DM], F32, kind="ExternalInput").ap()
    xout = nc.dram_tensor("xo", [NB, T, DM], F32, kind="ExternalOutput").ap()
    x1 = nc.dram_tensor("x1", [NB, T, DM], F32, kind="Internal").ap()
    x2 = nc.dram_tensor("x2", [NB, T, DM], F32, kind="Internal").ap()
    x3 = nc.dram_tensor("x3", [NB, T, DM], F32, kind="Internal").ap()
    WL0 = _l0_declare(nc)
    WL1 = _l1_declare(nc)
    ffw = {}
    for t_ in ("0", "1"):
        ffw[t_] = (nc.dram_tensor(f"wg{t_}", [128, KI * DM], F32, kind="ExternalInput").ap(),
                   nc.dram_tensor(f"wu{t_}", [128, KI * DM], F32, kind="ExternalInput").ap(),
                   nc.dram_tensor(f"wd{t_}", [128, KD * I], F32, kind="ExternalInput").ap())
    with ExitStack() as es:
        pr = Pr(nc, es)
        load_consts(pr, nc, sel=True, tri=True, identf=True)
        with nc.named_scope("L0"):
            emit_l0(nc, pr, xin, x1, WL0, "A_")
        with nc.named_scope("FFN0"):
            emit_ffn(nc, pr, x1, ffw["0"][0], ffw["0"][1], ffw["0"][2], x2, "C_")
        with nc.named_scope("L1"):
            emit_l1(nc, pr, x2, x3, WL1, "B_")
        with nc.named_scope("FFN1"):
            emit_ffn(nc, pr, x3, ffw["1"][0], ffw["1"][1], ffw["1"][2], xout, "D_")
        pr.finish()
    return nc


def kernel(**inputs):
    nc = _get_prog("full")
    consts = host_consts()
    m0 = _l0_weight_maps(inputs)
    m1 = _l1_weight_maps(inputs)
    base = {"c_ident": consts["c_ident"], "c_sel": consts["c_sel"],
            "c_tri": consts["c_tri"]}
    base.update(m0)
    base.update(m1)
    for t_, pre in (("0", "L0_ffn"), ("1", "L1_ffn")):
        base[f"wg{t_}"] = _tile_rows(inputs[f"{pre}_w_gate"])
        base[f"wu{t_}"] = _tile_rows(inputs[f"{pre}_w_up"])
        base[f"wd{t_}"] = _tile_rows(inputs[f"{pre}_w_down"])
    x = np.ascontiguousarray(inputs["L0_attn_x"], dtype=np.float32)
    in_maps = []
    for c in range(NCORES):
        m = dict(base)
        m["x"] = np.ascontiguousarray(x[c * NB:(c + 1) * NB])
        in_maps.append(m)
    res = run_bass_kernel_spmd(nc, in_maps, core_ids=list(range(NCORES)))
    return np.concatenate([r["xo"] for r in res.results], axis=0)


def _kernel_unfused(**inputs):
    x = np.ascontiguousarray(inputs["L0_attn_x"], dtype=np.float32)
    x = _run_l0(x, inputs)
    x = _run_ffn(x, inputs["L0_ffn_w_gate"], inputs["L0_ffn_w_up"], inputs["L0_ffn_w_down"])
    x = _run_l1(x, inputs)
    x = _run_ffn(x, inputs["L1_ffn_w_gate"], inputs["L1_ffn_w_up"], inputs["L1_ffn_w_down"])
    return x


# version 16
# speedup vs baseline: 1.2240x; 1.2240x over previous
import numpy as np
import ml_dtypes
from contextlib import ExitStack

import concourse.bass as bass
import concourse.mybir as mybir
from concourse.bass_utils import run_bass_kernel_spmd

F32 = mybir.dt.float32
BF16 = mybir.dt.bfloat16
U8 = mybir.dt.uint8
U16 = mybir.dt.uint16
U32 = mybir.dt.uint32
AF = mybir.ActivationFunctionType
ALU = mybir.AluOpType
AX = mybir.AxisListType
BFNP = ml_dtypes.bfloat16

T = 2048
DM = 1024
NB = 2
NCORES = 8
NTT = T // 128
EPS = 1e-5
MAGIC = 12582912.0


class Pr:
    ENGS = ("v", "s", "t", "g", "y")

    def __init__(self, nc, es):
        self.nc = nc
        self.es = es
        self.sems = {e: es.enter_context(nc.semaphore("sem_" + e)) for e in self.ENGS}
        self.dsem = es.enter_context(nc.semaphore("sem_dma"))
        self.lzsem = es.enter_context(nc.semaphore("sem_lzdma"))
        self.cnt = {e: 0 for e in self.ENGS}
        self.dcnt = 0
        self.pend_dma = 0
        self.lzcnt = 0
        self.need_lz = 0
        self.lz_seen = {e: 0 for e in self.ENGS}
        self.seen = {e: {o: 0 for o in self.ENGS} for e in self.ENGS}
        self.dseen = {e: 0 for e in self.ENGS}
        self.blk = es.enter_context(nc.Block())
        self.q = {e: [] for e in self.ENGS}
        self._names = 0

    def v(self, fn): self.q["v"].append(fn)
    def s(self, fn): self.q["s"].append(fn)
    def t(self, fn): self.q["t"].append(fn)
    def g(self, fn): self.q["g"].append(fn)
    def y(self, fn): self.q["y"].append(fn)

    def dma(self, out, in_, queue="y", lazy=False, **kw):
        if lazy:
            lzsem = self.lzsem
            def f(eng):
                eng.dma_start(out=out, in_=in_, **kw).then_inc(lzsem, 16)
                return None
            self.q[queue].append(f)
            self.lzcnt += 1
            return self.lzcnt
        dsem = self.dsem
        def f(eng):
            eng.dma_start(out=out, in_=in_, **kw).then_inc(dsem, 16)
            return None
        self.q[queue].append(f)
        self.pend_dma += 1
        return None

    def need(self, h):
        if h:
            self.need_lz = max(self.need_lz, h)

    def sbuf(self, shape, dtype=F32, name=None):
        if name is None:
            self._names += 1
            name = f"sb{self._names}"
        return self.es.enter_context(self.nc.sbuf_tensor(name, shape, dtype))

    def psum(self, shape, dtype=F32, name=None):
        if name is None:
            self._names += 1
            name = f"ps{self._names}"
        return self.es.enter_context(self.nc.psum_tensor(name, shape, dtype))

    def phase(self):
        waits = {}
        for e in self.ENGS:
            if self.q[e]:
                w = []
                for oe in self.ENGS:
                    c = self.cnt[oe]
                    if c > 0 and (oe == e or c > self.seen[e][oe]):
                        if c > self.seen[e][oe]:
                            w.append((self.sems[oe], c))
                            self.seen[e][oe] = c
                waits[e] = w
        dwait = self.dcnt
        for e in self.ENGS:
            ops = self.q[e]
            if not ops:
                continue
            sem = self.sems[e]
            mywaits = waits[e]
            dsem = self.dsem
            dw_eff = dwait if dwait > self.dseen[e] else 0
            if dw_eff:
                self.dseen[e] = dwait
            lzw = self.need_lz if self.need_lz > self.lz_seen[e] else 0
            if lzw:
                self.lz_seen[e] = lzw
            lzsem = self.lzsem
            def body(eng, ops=ops, mywaits=mywaits, dwait=dw_eff, sem=sem, dsem=dsem,
                     lzw=lzw, lzsem=lzsem):
                for s_, c_ in mywaits:
                    eng.wait_ge(s_, c_)
                if dwait:
                    eng.wait_ge(dsem, 16 * dwait)
                if lzw:
                    eng.wait_ge(lzsem, 16 * lzw)
                last = None
                for fn in ops:
                    r = fn(eng)
                    if r is not None:
                        last = r
                if last is not None:
                    last.then_inc(sem, 1)
                else:
                    eng.nop().then_inc(sem, 1)
            getattr(self.blk, {"v": "vector", "s": "scalar", "t": "tensor",
                               "g": "gpsimd", "y": "sync"}[e])(body)
            self.cnt[e] += 1
            self.seen[e][e] = self.cnt[e]
            self.q[e] = []
        self.dcnt += self.pend_dma
        self.pend_dma = 0

    def finish(self):
        self.phase()
        dw = self.dcnt
        lw = self.lzcnt
        waits = [(self.sems[e], self.cnt[e]) for e in self.ENGS if self.cnt[e] > 0]
        dsem = self.dsem
        lzsem = self.lzsem
        def body(eng):
            for s_, c_ in waits:
                eng.wait_ge(s_, c_)
            eng.wait_ge(dsem, 16 * dw)
            if lw:
                eng.wait_ge(lzsem, 16 * lw)
        self.blk.sync(body)


def load_consts(pr, nc, sel=False, tri=False, ones=False, identf=False, self32=False):
    cid = nc.dram_tensor("c_ident", [128, 128], F32, kind="ExternalInput").ap()
    pr.ident16 = pr.sbuf([128, 128], BF16, "identb")
    pr.dma(pr.ident16[:], cid, queue="g")
    if identf:
        pr.identf = pr.sbuf([128, 128], F32, "identf")
        pr.dma(pr.identf[:], cid)
    if sel or self32:
        csel = nc.dram_tensor("c_sel", [32, 32 * 128], F32, kind="ExternalInput").ap()
        pr.csel = csel
        if sel:
            pass
        if self32:
            pr.self32 = pr.sbuf([32, 32 * 128], F32, "self32")
            pr.dma(pr.self32[:], csel)
    if tri:
        ctri = nc.dram_tensor("c_tri", [128, 128], F32, kind="ExternalInput").ap()
        pr.tri16 = pr.sbuf([128, 128], BF16, "trib")
        pr.dma(pr.tri16[:], ctri, queue="g")
    if ones:
        pr.ones1f = pr.sbuf([1, 512], F32, "ones1f")
        def vf(vector):
            return vector.memset(pr.ones1f[:], 1.0)
        pr.v(vf)
    pr.phase()


def host_consts():
    ident = np.eye(128, dtype=np.float32)
    sel = np.zeros((32, 32 * 128), dtype=np.float32)
    for j in range(32):
        sel[j, j * 128:(j + 1) * 128] = 1.0
    tri = np.triu(np.ones((128, 128), dtype=np.float32))
    return {"c_ident": ident, "c_sel": sel, "c_tri": tri}


def emit_ln(pr, xin, xo, ntiles, D, stat, gamma_rep=None, beta_rep=None, xo2=None):
    sq = pr.scr_sq

    def s1(scalar):
        r = None
        for i in range(ntiles):
            r = scalar.activation(sq[:, 0:D], xin[:, i * D:(i + 1) * D], AF.Square,
                                  accum_out=stat[:, i * 8 + 1:i * 8 + 2])
        return r
    def v1(vector):
        r = None
        for i in range(ntiles):
            r = vector.reduce_sum(stat[:, i * 8:i * 8 + 1], xin[:, i * D:(i + 1) * D], axis=AX.X)
        return r
    pr.s(s1); pr.v(v1); pr.phase()

    def v2(vector):
        r = None
        for i in range(ntiles):
            c = i * 8
            vector.tensor_scalar(stat[:, c + 2:c + 3], stat[:, c:c + 1], 1.0 / D, None, ALU.mult)
            r = vector.tensor_scalar(stat[:, c + 4:c + 5], stat[:, c + 1:c + 2],
                                     1.0 / D, EPS, ALU.mult, ALU.add)
        return r
    pr.v(v2); pr.phase()

    def v3(vector):
        r = None
        for i in range(ntiles):
            c = i * 8
            vector.scalar_tensor_tensor(stat[:, c + 5:c + 6], stat[:, c + 2:c + 3],
                                        stat[:, c + 2:c + 3], stat[:, c + 4:c + 5],
                                        ALU.mult, ALU.subtract)
            r = vector.tensor_scalar(stat[:, c + 7:c + 8], stat[:, c + 2:c + 3],
                                     -1.0, None, ALU.mult)
        return r
    pr.v(v3); pr.phase()

    def s2(scalar):
        r = None
        for i in range(ntiles):
            c = i * 8
            r = scalar.activation(stat[:, c + 3:c + 4], stat[:, c + 5:c + 6], AF.Ln, scale=-1.0)
        return r
    pr.s(s2); pr.phase()

    def s3(scalar):
        r = None
        for i in range(ntiles):
            c = i * 8
            r = scalar.activation(stat[:, c + 6:c + 7], stat[:, c + 3:c + 4], AF.Exp, scale=-0.5)
        return r
    pr.s(s3); pr.phase()

    def v4(vector):
        r = None
        for i in range(ntiles):
            c = i * 8
            r = vector.tensor_scalar(xo[:, i * D:(i + 1) * D], xin[:, i * D:(i + 1) * D],
                                     stat[:, c + 7:c + 8], stat[:, c + 6:c + 7], ALU.add, ALU.mult)
            if xo2 is not None:
                r = vector.tensor_copy(xo2[:, i * D:(i + 1) * D], xo[:, i * D:(i + 1) * D])
        return r
    pr.v(v4); pr.phase()

    if gamma_rep is not None:
        def v5(vector):
            r = None
            for i in range(ntiles):
                sl = slice(i * D, (i + 1) * D)
                vector.tensor_tensor(xo[:, sl], xo[:, sl], gamma_rep[:, 0:D], ALU.mult)
                r = vector.tensor_tensor(xo[:, sl], xo[:, sl], beta_rep[:, 0:D], ALU.add)
            return r
        pr.v(v5); pr.phase()


def emit_transpose(pr, pst, blocks):
    group = pst.shape[1] // 128
    ident = pr.ident16[:]
    for i0 in range(0, len(blocks), group):
        n = min(group, len(blocks) - i0)
        def tf(tensor, i0=i0, n=n):
            r = None
            for j in range(n):
                st, sc, _, _ = blocks[i0 + j]
                r = tensor.transpose(pst[:, j * 128:(j + 1) * 128], st[:, sc:sc + 128], ident)
            return r
        pr.t(tf); pr.phase()
        def vf(vector, i0=i0, n=n):
            r = None
            for j in range(n):
                _, _, dt_, dc = blocks[i0 + j]
                r = vector.tensor_copy(dt_[:, dc:dc + 128], pst[:, j * 128:(j + 1) * 128])
            return r
        pr.v(vf); pr.phase()


def _emit_quant_single(pr, job, scr):
    x, F, out = job
    G = F // 16
    bm, rs = scr["bm"][0], scr["rs"][0]
    def p1(vector):
        xg = x[:, 0:F].rearrange("p (g o) -> p g o", o=16)
        return vector.reduce_max(bm[:, 0:G], xg, axis=AX.X, apply_absolute_value=True)
    pr.v(p1); pr.phase()
    pr.v(lambda vector: vector.tensor_scalar(bm[:, 0:G], bm[:, 0:G], 1.0 / 6.0, 1e-12,
                                             ALU.mult, ALU.max))
    pr.phase()
    pr.v(lambda vector: vector.reciprocal(rs[:, 0:G], bm[:, 0:G]))
    pr.phase()
    pr.v(lambda vector: vector.tensor_tensor(scr["q1"][0][:, 0:G], bm[:, 0:G],
                                             rs[:, 0:G], ALU.mult))
    pr.phase()
    pr.v(lambda vector: vector.tensor_scalar(scr["q1"][0][:, 0:G], scr["q1"][0][:, 0:G],
                                             -1.0, 2.0, ALU.mult, ALU.add))
    pr.phase()
    pr.v(lambda vector: vector.tensor_tensor(rs[:, 0:G], rs[:, 0:G],
                                             scr["q1"][0][:, 0:G], ALU.mult))
    pr.phase()
    def p2(vector):
        u, m = scr["u"][0], scr["m"][0]
        q1, qh, mg = scr["q1"][0], scr["qh"][0], scr["mg"][0]
        rsb = rs[:, 0:G].unsqueeze(2).broadcast_to([128, G, 16])
        smb = bm[:, 0:G].unsqueeze(2).broadcast_to([128, G, 16])
        xv = x[:, 0:F].rearrange("p (g o) -> p g o", o=16)
        vector.tensor_copy(u[:, 0:F], x[:, 0:F])
        vector.tensor_single_scalar(m[:, 0:F].bitcast(U16), u[:, 0:F].bitcast(U16),
                                    0x8000, ALU.bitwise_and)
        mgv = mg[:, 0:F].rearrange("p (g o) -> p g o", o=16)
        vector.tensor_tensor(mgv, xv, rsb, ALU.mult)
        vector.tensor_single_scalar(mg[:, 0:F].bitcast(U32), mg[:, 0:F].bitcast(U32),
                                    0x7FFFFFFF, ALU.bitwise_and)
        vector.tensor_scalar(q1[:, 0:F], mg[:, 0:F], 2.0, MAGIC, ALU.mult, ALU.add)
        vector.tensor_scalar(q1[:, 0:F], q1[:, 0:F], MAGIC, 0.5, ALU.subtract, ALU.mult)
        vector.tensor_scalar(qh[:, 0:F], mg[:, 0:F], MAGIC, None, ALU.add)
        vector.tensor_scalar(qh[:, 0:F], qh[:, 0:F], MAGIC, 4.0, ALU.subtract, ALU.min)
        vector.tensor_single_scalar(u[:, 0:F], mg[:, 0:F], 5.0, ALU.is_ge)
        vector.scalar_tensor_tensor(qh[:, 0:F], u[:, 0:F], 2.0, qh[:, 0:F],
                                    ALU.mult, ALU.add)
        vector.tensor_single_scalar(u[:, 0:F], mg[:, 0:F], 1.75, ALU.is_lt)
        vector.tensor_tensor(q1[:, 0:F], q1[:, 0:F], qh[:, 0:F], ALU.subtract)
        vector.tensor_tensor(q1[:, 0:F], q1[:, 0:F], u[:, 0:F], ALU.mult)
        vector.tensor_tensor(q1[:, 0:F], q1[:, 0:F], qh[:, 0:F], ALU.add)
        vector.tensor_copy(u[:, 0:F], q1[:, 0:F])
        vector.tensor_tensor(u[:, 0:F].bitcast(U16), u[:, 0:F].bitcast(U16),
                             m[:, 0:F].bitcast(U16), ALU.bitwise_or)
        uv2 = u[:, 0:F].rearrange("p (g o) -> p g o", o=16)
        ov = out[:, 0:F].rearrange("p (g o) -> p g o", o=16)
        return vector.tensor_tensor(ov, uv2, smb, ALU.mult)
    pr.v(p2); pr.phase()


def emit_quant(pr, jobs, scr):
    npool = len(scr["u"])
    for g0 in range(0, len(jobs), npool):
        grp = jobs[g0:g0 + npool]
        if len(grp) == 1:
            _emit_quant_single(pr, grp[0], scr)
            continue

        def vq(vector):
            for j, (x, F, out) in enumerate(grp):
                G = F // 16
                xg = x[:, 0:F].rearrange("p (g o) -> p g o", o=16)
                vector.reduce_max(scr["bm"][j][:, 0:G], xg, axis=AX.X,
                                  apply_absolute_value=True)
            for j, (x, F, out) in enumerate(grp):
                G = F // 16
                vector.tensor_scalar(scr["bm"][j][:, 0:G], scr["bm"][j][:, 0:G],
                                     1.0 / 6.0, 1e-12, ALU.mult, ALU.max)
            for j, (x, F, out) in enumerate(grp):
                G = F // 16
                vector.reciprocal(scr["rs"][j][:, 0:G], scr["bm"][j][:, 0:G])
            for j, (x, F, out) in enumerate(grp):
                G = F // 16
                vector.tensor_tensor(scr["q1"][j][:, 0:G], scr["bm"][j][:, 0:G],
                                     scr["rs"][j][:, 0:G], ALU.mult)
            for j, (x, F, out) in enumerate(grp):
                G = F // 16
                vector.tensor_scalar(scr["q1"][j][:, 0:G], scr["q1"][j][:, 0:G],
                                     -1.0, 2.0, ALU.mult, ALU.add)
            for j, (x, F, out) in enumerate(grp):
                G = F // 16
                vector.tensor_tensor(scr["rs"][j][:, 0:G], scr["rs"][j][:, 0:G],
                                     scr["q1"][j][:, 0:G], ALU.mult)
            r = None
            for j, (x, F, out) in enumerate(grp):
                G = F // 16
                bm, rs = scr["bm"][j], scr["rs"][j]
                u, m = scr["u"][j], scr["m"][j]
                q1, qh, mg = scr["q1"][j], scr["qh"][j], scr["mg"][j]
                rsb = rs[:, 0:G].unsqueeze(2).broadcast_to([128, G, 16])
                smb = bm[:, 0:G].unsqueeze(2).broadcast_to([128, G, 16])
                xv = x[:, 0:F].rearrange("p (g o) -> p g o", o=16)
                vector.tensor_copy(u[:, 0:F], x[:, 0:F])
                vector.tensor_single_scalar(m[:, 0:F].bitcast(U16), u[:, 0:F].bitcast(U16),
                                            0x8000, ALU.bitwise_and)
                mgv = mg[:, 0:F].rearrange("p (g o) -> p g o", o=16)
                vector.tensor_tensor(mgv, xv, rsb, ALU.mult)
                vector.tensor_single_scalar(mg[:, 0:F].bitcast(U32), mg[:, 0:F].bitcast(U32),
                                            0x7FFFFFFF, ALU.bitwise_and)
                vector.tensor_scalar(q1[:, 0:F], mg[:, 0:F], 2.0, MAGIC, ALU.mult, ALU.add)
                vector.tensor_scalar(q1[:, 0:F], q1[:, 0:F], MAGIC, 0.5, ALU.subtract, ALU.mult)
                vector.tensor_scalar(qh[:, 0:F], mg[:, 0:F], MAGIC, None, ALU.add)
                vector.tensor_scalar(qh[:, 0:F], qh[:, 0:F], MAGIC, 4.0, ALU.subtract, ALU.min)
                vector.tensor_single_scalar(u[:, 0:F], mg[:, 0:F], 5.0, ALU.is_ge)
                vector.scalar_tensor_tensor(qh[:, 0:F], u[:, 0:F], 2.0, qh[:, 0:F],
                                            ALU.mult, ALU.add)
                vector.tensor_single_scalar(u[:, 0:F], mg[:, 0:F], 1.75, ALU.is_lt)
                vector.tensor_tensor(q1[:, 0:F], q1[:, 0:F], qh[:, 0:F], ALU.subtract)
                vector.tensor_tensor(q1[:, 0:F], q1[:, 0:F], u[:, 0:F], ALU.mult)
                vector.tensor_tensor(q1[:, 0:F], q1[:, 0:F], qh[:, 0:F], ALU.add)
                vector.tensor_copy(u[:, 0:F], q1[:, 0:F])
                vector.tensor_tensor(u[:, 0:F].bitcast(U16), u[:, 0:F].bitcast(U16),
                                     m[:, 0:F].bitcast(U16), ALU.bitwise_or)
                uv2 = u[:, 0:F].rearrange("p (g o) -> p g o", o=16)
                ov = out[:, 0:F].rearrange("p (g o) -> p g o", o=16)
                r = vector.tensor_tensor(ov, uv2, smb, ALU.mult)
            return r
        pr.v(vq); pr.phase()


def mk_quant_scr(pr, F, n=1, es=None):
    def alloc(shape, dt_):
        pr._names += 1
        name = f"qscr{pr._names}"
        cm = pr.nc.sbuf_tensor(name, shape, dt_)
        return (es or pr.es).enter_context(cm)
    return {
        "bm": [alloc([128, F // 16], F32) for _ in range(n)],
        "rs": [alloc([128, F // 16], F32) for _ in range(n)],
        "q1": [alloc([128, F], F32) for _ in range(n)],
        "qh": [alloc([128, F], F32) for _ in range(n)],
        "mg": [alloc([128, F], F32) for _ in range(n)],
        "u": [alloc([128, F], BF16) for _ in range(n)],
        "m": [alloc([128, F], BF16) for _ in range(n)],
    }


def emit_silu(pr, tiles, scr_e):
    npool = len(scr_e)
    for g0 in range(0, len(tiles), npool):
        grp = tiles[g0:g0 + npool]
        def s1(scalar):
            r = None
            for j, (x, F) in enumerate(grp):
                r = scalar.activation(scr_e[j][:, 0:F], x[:, 0:F], AF.Exp, scale=-1.0)
            return r
        pr.s(s1); pr.phase()
        def v1(vector):
            r = None
            for j, (x, F) in enumerate(grp):
                e = scr_e[j]
                vector.tensor_scalar(e[:, 0:F], e[:, 0:F], 1.0, None, ALU.add)
                vector.reciprocal(e[:, 0:F], e[:, 0:F])
                r = vector.tensor_tensor(x[:, 0:F], x[:, 0:F], e[:, 0:F], ALU.mult)
            return r
        pr.v(v1); pr.phase()


DBG_FFN = False


def build_ffn():
    I = 2816
    KI = I // 128
    KD = DM // 128
    nc = bass.Bass("TRN2", target_bir_lowering=False)
    xin = nc.dram_tensor("x", [NB, T, DM], F32, kind="ExternalInput").ap()
    wg = nc.dram_tensor("wg", [128, KI * DM], F32, kind="ExternalInput").ap()
    wu = nc.dram_tensor("wu", [128, KI * DM], F32, kind="ExternalInput").ap()
    wd = nc.dram_tensor("wd", [128, KD * I], F32, kind="ExternalInput").ap()
    xout = nc.dram_tensor("xo", [NB, T, DM], F32, kind="ExternalOutput").ap()
    with ExitStack() as es:
        pr = Pr(nc, es)
        load_consts(pr, nc)
        emit_ffn(nc, pr, xin, wg, wu, wd, xout, "F_")
        pr.finish()
    return nc


def emit_ffn(nc, pr, xin, wg, wu, wd, xout, pf):
    I = 2816
    KI = I // 128
    KD = DM // 128
    with ExitStack() as es:
        scr = mk_quant_scr(pr, 512, n=3, es=es)
        pr.scr_sq = scr["q1"][0]

        swgT = es.enter_context(nc.sbuf_tensor(pf + "swgT", [128, KD * I], BF16))
        swuT = es.enter_context(nc.sbuf_tensor(pf + "swuT", [128, KD * I], BF16))
        swdT = es.enter_context(nc.sbuf_tensor(pf + "swdT", [128, KI * DM], BF16))
        pst = es.enter_context(nc.psum_tensor(pf + "pst", [128, 2048], BF16))
        psA = es.enter_context(nc.psum_tensor(pf + "psA", [128, 1024], F32))
        psA2 = es.enter_context(nc.psum_tensor(pf + "psA2", [128, 1024], F32))
        psD = es.enter_context(nc.psum_tensor(pf + "psD", [128, 1024], F32))
        xnT = es.enter_context(nc.sbuf_tensor(pf + "xnT", [128, KD * T], BF16))

        _pc = [0]

        def prep(wdram, n_rowtiles, fcols, dstT, slice_tiles):
            for s0 in range(0, n_rowtiles, slice_tiles):
                ns = min(slice_tiles, n_rowtiles - s0)
                _pc[0] += 1
                with nc.sbuf_tensor(f"{pf}wstg{_pc[0]}", [128, ns * fcols], F32) as stg, \
                     nc.sbuf_tensor(f"{pf}wstq{_pc[0]}", [128, ns * fcols], BF16) as stq:
                    pr.dma(stg[:], wdram[:, s0 * fcols:(s0 + ns) * fcols])
                    pr.phase()
                    qw = 512 if fcols % 512 == 0 else 352
                    qjobs = []
                    for j in range(ns):
                        for c in range(0, fcols, qw):
                            qjobs.append((stg[:, j * fcols + c:j * fcols + c + qw], qw,
                                          stq[:, j * fcols + c:j * fcols + c + qw]))
                    emit_quant(pr, qjobs, scr)
                    blocks = []
                    for j in range(ns):
                        kt = s0 + j
                        for fc in range(fcols // 128):
                            blocks.append((stq, j * fcols + fc * 128,
                                           dstT, fc * (n_rowtiles * 128) + kt * 128))
                    emit_transpose(pr, pst, blocks)

        with nc.named_scope("ffn_wprep"):
            prep(wg, KI, DM, swgT, 3)
            prep(wu, KI, DM, swuT, 3)
            prep(wd, KD, I, swdT, 1)

        stat = es.enter_context(nc.sbuf_tensor(pf + "stat", [128, 8], F32))
        g32 = es.enter_context(nc.sbuf_tensor(pf + "g32", [128, DM], F32))
        u32 = es.enter_context(nc.sbuf_tensor(pf + "u32", [128, DM], F32))
        o_t = u32
        x_t = g32
        pr.scr_sq = u32

        for b in range(NB):
            QT = 1
            for h0 in range(0, NTT, QT):
                with nc.sbuf_tensor(f"{pf}xs{b}_{h0}", [128, QT * DM], F32) as xs, \
                     nc.sbuf_tensor(f"{pf}xq{b}_{h0}", [128, QT * DM], BF16) as xq:
                    for i in range(QT):
                        tt = h0 + i
                        pr.dma(xs[:, i * DM:(i + 1) * DM], xin[b, tt * 128:(tt + 1) * 128, :])
                    pr.phase()
                    emit_ln(pr, xs, xs, QT, DM, stat)
                    emit_quant(pr, [(xs[:, i * DM + c:i * DM + c + 512], 512,
                                     xq[:, i * DM + c:i * DM + c + 512])
                                    for i in range(QT) for c in (0, 512)], scr)
                    emit_transpose(pr, pst,
                                   [(xq, i * DM + kd * 128, xnT, kd * T + (h0 + i) * 128)
                                    for i in range(QT) for kd in range(KD)])


            with nc.sbuf_tensor(f"{pf}h_t{b}", [128, I], BF16) as h_t:
              hT_t = h_t
              for tt in range(NTT):
                  slices = []
                  for c0 in range(0, I, DM):
                      cw = min(DM, I - c0)
                      for i0 in range(c0, c0 + cw, 512):
                          slices.append((c0, i0, min(512, c0 + cw - i0)))
                  nsl = len(slices)

                  def emit_mm(si):
                      c0, i0, iw = slices[si]
                      ps = psA if si % 2 == 0 else psA2
                      def tf(tensor, i0=i0, iw=iw, ps=ps, tt=tt):
                          r = None
                          for half, w in ((0, swgT), (1, swuT)):
                              for kd in range(KD):
                                  r = tensor.matmul(
                                      ps[:, half * 512:half * 512 + iw],
                                      xnT[:, kd * T + tt * 128:kd * T + tt * 128 + 128],
                                      w[:, kd * I + i0:kd * I + i0 + iw],
                                      start=(kd == 0), stop=(kd == KD - 1))
                          return r
                      pr.t(tf)

                  def emit_evac(si):
                      c0, i0, iw = slices[si]
                      ps = psA if si % 2 == 0 else psA2
                      def vf(vector, c0=c0, i0=i0, iw=iw, ps=ps):
                          c = i0 - c0
                          vector.tensor_copy(g32[:, c:c + iw], ps[:, 0:iw])
                          return vector.tensor_copy(u32[:, c:c + iw], ps[:, 512:512 + iw])
                      pr.v(vf)

                  def emit_silu(c0, cw):
                      pr.s(lambda scalar, c0=c0, cw=cw: scalar.activation(
                          g32[:, 0:cw], g32[:, 0:cw], AF.Silu))

                  def emit_mult(c0, cw):
                      pr.v(lambda vector, c0=c0, cw=cw: vector.tensor_tensor(
                          g32[:, 0:cw], g32[:, 0:cw], u32[:, 0:cw], ALU.mult))

                  def emit_q(c0, cw):
                      qjobs = []
                      for q0 in range(0, cw, 512):
                          qw = min(512, cw - q0)
                          qjobs.append((g32[:, q0:q0 + qw], qw,
                                        h_t[:, c0 + q0:c0 + q0 + qw]))
                      emit_quant(pr, qjobs, scr)

                  def emit_tr(j0, nblk):
                      def tfr(tensor, j0=j0, nblk=nblk):
                          r = None
                          for j in range(nblk):
                              r = tensor.transpose(pst[:, j * 128:(j + 1) * 128],
                                                   h_t[:, (j0 + j) * 128:(j0 + j + 1) * 128],
                                                   pr.ident16[:])
                          return r
                      pr.t(tfr)

                  def emit_tre(j0, nblk):
                      def vfr(vector, j0=j0, nblk=nblk):
                          r = None
                          for j in range(nblk):
                              r = vector.tensor_copy(
                                  h_t[:, (j0 + j) * 128:(j0 + j + 1) * 128],
                                  pst[:, j * 128:(j + 1) * 128])
                          return r
                      pr.v(vfr)

                  emit_mm(0); pr.phase()
                  emit_mm(1); emit_evac(0); pr.phase()
                  emit_mm(2); emit_evac(1); pr.phase()
                  emit_mm(3); emit_silu(0, DM); pr.phase()
                  emit_mult(0, DM); pr.phase()
                  emit_q(0, DM)
                  emit_evac(2); pr.phase()
                  emit_mm(4); emit_evac(3); pr.phase()
                  emit_mm(5); emit_silu(DM, DM); pr.phase()
                  emit_mult(DM, DM); pr.phase()
                  emit_q(DM, DM)
                  emit_evac(4); emit_tr(0, 16); pr.phase()
                  emit_evac(5); emit_tre(0, 16); pr.phase()
                  emit_silu(2 * DM, I - 2 * DM); pr.phase()
                  emit_mult(2 * DM, I - 2 * DM); pr.phase()
                  emit_q(2 * DM, I - 2 * DM)
                  emit_tr(16, 6); pr.phase()
                  emit_tre(16, 6); pr.phase()

                  def tf2(tensor, tt=tt):
                      r = None
                      for c0 in range(0, DM, 512):
                          for ki in range(KI):
                              r = tensor.matmul(psD[:, c0:c0 + 512],
                                                hT_t[:, ki * 128:(ki + 1) * 128],
                                                swdT[:, ki * DM + c0:ki * DM + c0 + 512],
                                                start=(ki == 0), stop=(ki == KI - 1))
                      return r
                  pr.t(tf2)
                  pr.dma(x_t[:, 0:DM], xin[b, tt * 128:(tt + 1) * 128, :])
                  pr.phase()

                  def vf2(vector):
                      return vector.tensor_tensor(o_t[:], psD[:, 0:DM], x_t[:, 0:DM], ALU.add)
                  pr.v(vf2); pr.phase()
                  pr.dma(xout[b, tt * 128:(tt + 1) * 128, :], o_t[:])
            pr.phase()
        pr.phase()


def build_l0_attn():
    nc = bass.Bass("TRN2", target_bir_lowering=False)
    xin = nc.dram_tensor("x", [NB, T, DM], F32, kind="ExternalInput").ap()
    xout = nc.dram_tensor("xo", [NB, T, DM], F32, kind="ExternalOutput").ap()
    W = _l0_declare(nc)
    with ExitStack() as es:
        pr = Pr(nc, es)
        load_consts(pr, nc, tri=True, identf=True, self32=True)
        emit_l0(nc, pr, xin, xout, W, "A_")
        pr.finish()
    return nc


def _l0_declare(nc, pf=""):
    names = [("inprojT", [128, 8 * 4384]), ("convw", [128, 18 * 4]),
             ("convbi", [128, 18]), ("dtbias", [32, 1]), ("alogi", [32, 1]),
             ("dvec_rep", [128, 2048]), ("normw_rep", [128, 2048]),
             ("lnw_rep", [128, 1024]), ("lnb_rep", [128, 1024]),
             ("outprojT", [128, 16 * 1024])]
    return {n: nc.dram_tensor(pf + n, sh, F32, kind="ExternalInput").ap()
            for n, sh in names}


L0SER = {"a": 0, "b": 0, "c": 0, "d": 0, "e": 0, "f": 1}


def emit_l0(nc, pr, xin, xout, W, pf):
    def SP(g="a"):
        if L0SER[g]:
            pr.phase()
    H = 32
    Q = 128
    NCH = T // Q
    inprojT = W["inprojT"]; convw = W["convw"]; convbi = W["convbi"]
    dtbias = W["dtbias"]; alogi = W["alogi"]; dvec_rep = W["dvec_rep"]
    normw_rep = W["normw_rep"]; lnw_rep = W["lnw_rep"]; lnb_rep = W["lnb_rep"]
    outprojT = W["outprojT"]
    z0_sp = nc.dram_tensor(pf + "z0_sp", [128, 16 * T], BF16, kind="Internal").ap()
    xbc_sp = nc.dram_tensor(pf + "xbc_sp", [128, 16 * T], BF16, kind="Internal").ap()
    y0_sp = nc.dram_tensor(pf + "y0_sp", [128, 16 * DM], BF16, kind="Internal").ap()
    dk_sp = nc.dram_tensor(pf + "dk_sp", [32, 16], F32, kind="Internal").ap()
    cselap = pr.csel
    if True:
        es = ExitStack()

        stat = es.enter_context(nc.sbuf_tensor(pf + "stat", [128, 8], F32))
        scr_sq = es.enter_context(nc.sbuf_tensor(pf + "scr_sq", [128, DM], F32))
        pr.scr_sq = scr_sq
        onesf = es.enter_context(nc.sbuf_tensor(pf + "onesf", [1, 128], F32))
        cw_sb = es.enter_context(nc.sbuf_tensor(pf + "cw_sb", [128, 18 * 4], F32))
        cb_sb = es.enter_context(nc.sbuf_tensor(pf + "cb_sb", [128, 18], F32))
        dtb_sb = es.enter_context(nc.sbuf_tensor(pf + "dtb_sb", [32, 1], F32))
        an_sb = es.enter_context(nc.sbuf_tensor(pf + "an_sb", [32, 1], F32))
        drep = es.enter_context(nc.sbuf_tensor(pf + "drep", [128, 2048], BF16))
        nwrep = es.enter_context(nc.sbuf_tensor(pf + "nwrep", [128, 2048], BF16))
        lwrep = es.enter_context(nc.sbuf_tensor(pf + "lwrep", [128, 1024], BF16))
        lbrep = es.enter_context(nc.sbuf_tensor(pf + "lbrep", [128, 1024], BF16))
        pr.dma(cw_sb[:], convw)
        pr.dma(cb_sb[:], convbi)
        pr.dma(dtb_sb[:], dtbias)
        pr.dma(an_sb[:], alogi)
        pr.dma(drep[:], dvec_rep, queue="g")
        pr.dma(nwrep[:], normw_rep, queue="g")
        pr.dma(lwrep[:], lnw_rep, queue="g")
        pr.dma(lbrep[:], lnb_rep, queue="g")
        pr.v(lambda vector: vector.memset(onesf[:], 1.0))
        pr.phase()
        pr.s(lambda scalar: scalar.activation(an_sb[:], an_sb[:], AF.Exp))
        pr.phase()
        pr.v(lambda vector: vector.tensor_scalar(an_sb[:], an_sb[:], -1.0, None, ALU.mult))
        pr.phase()

        BT_sb = es.enter_context(nc.sbuf_tensor(pf + "BT_sb", [128, T], BF16))
        CT_sb = es.enter_context(nc.sbuf_tensor(pf + "CT_sb", [128, T], BF16))
        dt_sb = es.enter_context(nc.sbuf_tensor(pf + "dt_sb", [32, T], F32))
        ca_sb = es.enter_context(nc.sbuf_tensor(pf + "ca_sb", [32, T], F32))
        det_sb = es.enter_context(nc.sbuf_tensor(pf + "det_sb", [32, T], F32))
        caT = es.enter_context(nc.sbuf_tensor(pf + "caT", [128, NCH * 32], F32))
        dtT = es.enter_context(nc.sbuf_tensor(pf + "dtT", [128, NCH * 32], F32))
        decE = es.enter_context(nc.sbuf_tensor(pf + "decE", [128, NCH * 32], F32))
        expcaT = es.enter_context(nc.sbuf_tensor(pf + "expcaT", [128, NCH * 32], F32))
        dchkr = es.enter_context(nc.sbuf_tensor(pf + "dchkr", [128, 512], F32))
        dchk_row = es.enter_context(nc.sbuf_tensor(pf + "dchk_row", [1, 512], F32))
        S_sb = es.enter_context(nc.sbuf_tensor(pf + "S_sb", [128, H * 64], BF16))
        ones32 = es.enter_context(nc.sbuf_tensor(pf + "ones32", [32, Q], F32))
        ci = es.enter_context(nc.sbuf_tensor(pf + "ci", [128, T + 4], BF16))
        wslc = es.enter_context(nc.sbuf_tensor(pf + "wslc", [128, 8 * 128], BF16))
        pr.v(lambda vector: vector.memset(ones32[:], 1.0))
        pr.phase()

        for b in range(NB):
            cih = [0]
            spl = [0]
            y0h = [0]
            y0hs = []
            with nc.sbuf_tensor(f"{pf}lnx{b}", [128, DM], F32) as xs, \
                 nc.sbuf_tensor(f"{pf}y0t{b}", [128, DM], F32) as y0t, \
                 nc.sbuf_tensor(f"{pf}xnb{b}", [128, DM], BF16) as xnb, \
                 nc.sbuf_tensor(f"{pf}xnT{b}", [128, 8 * T], BF16) as xnT, \
                 nc.sbuf_tensor(f"{pf}e32p{b}", [128, T], F32) as e32, \
                 nc.sbuf_tensor(f"{pf}f32p{b}", [128, T], F32) as f32b:
                with nc.psum_tensor(f"{pf}pstT{b}", [128, 2048], BF16) as pst:
                    for tt in range(NTT):
                        pr.dma(xs[:], xin[b, tt * 128:(tt + 1) * 128, :])
                        pr.phase()
                        pr.need(y0h[0])
                        emit_ln(pr, xs, y0t, 1, DM, stat)
                        y0h[0] = pr.dma(y0_sp[:, tt * DM:(tt + 1) * DM], y0t[:],
                                        queue="g", lazy=True)
                        y0hs.append(y0h[0])
                        emit_ln(pr, y0t, xnb, 1, DM, stat, gamma_rep=lwrep, beta_rep=lbrep)
                        emit_transpose(pr, pst, [(xnb, kd * 128, xnT, kd * T + tt * 128)
                                                 for kd in range(8)])

                with nc.psum_tensor(f"{pf}psA{b}", [128, 2048], F32) as psA:
                    for ot in range(35):
                        orows = 32 if ot == 34 else 128
                        wv = inprojT.rearrange("p (kt o) -> p kt o", o=4384)
                        pr.dma(wslc[:, 0:8 * orows].rearrange("p (kt o) -> p kt o", o=orows),
                               wv[:, :, ot * 128:ot * 128 + orows], queue="g")
                        pr.phase()
                        def tf(tensor, ot=ot, orows=orows):
                            r = None
                            for n0 in range(0, T, 512):
                                for kd in range(8):
                                    r = tensor.matmul(psA[0:orows, n0:n0 + 512],
                                                      wslc[:, kd * orows:(kd + 1) * orows],
                                                      xnT[:, kd * T + n0:kd * T + n0 + 512],
                                                      start=(kd == 0), stop=(kd == 7))
                            return r
                        pr.t(tf); pr.phase()
                        if ot < 16:
                            pr.need(cih[0])
                            pr.v(lambda vector: vector.tensor_copy(ci[:, 4:4 + T],
                                                                   psA[:, 0:T]))
                            pr.phase()
                            h = pr.dma(z0_sp[:, ot * T:(ot + 1) * T], ci[:, 4:4 + T],
                                       queue="g", lazy=True)
                            cih[0] = h; spl[0] = max(spl[0], h)
                            pr.phase()
                        elif ot < 34:
                            cc = ot - 16
                            pr.need(cih[0])
                            def vc0(vector, cc=cc):
                                vector.memset(ci[:, 0:4], 0.0)
                                vector.tensor_copy(ci[:, 4:4 + T], psA[:, 0:T])
                                vector.tensor_scalar(f32b[:, 0:T], ci[:, 1:1 + T],
                                                     cw_sb[:, cc * 4:cc * 4 + 1], None, ALU.mult)
                                for k in (1, 2, 3):
                                    vector.scalar_tensor_tensor(
                                        f32b[:, 0:T], ci[:, 1 + k:1 + k + T],
                                        cw_sb[:, cc * 4 + k:cc * 4 + k + 1],
                                        f32b[:, 0:T], ALU.mult, ALU.add)
                                return vector.tensor_scalar(
                                    f32b[:, 0:T], f32b[:, 0:T], cb_sb[:, cc:cc + 1],
                                    None, ALU.add)
                            pr.v(vc0); pr.phase()
                            if cc < 16:
                                dst0 = ci[:, 4:4 + T]
                            elif cc == 16:
                                dst0 = BT_sb[:, 0:T]
                            else:
                                dst0 = CT_sb[:, 0:T]
                            pr.s(lambda scalar, dst0=dst0: scalar.activation(
                                dst0, f32b[:, 0:T], AF.Silu))
                            pr.phase()
                            if cc < 16:
                                h = pr.dma(xbc_sp[:, cc * T:(cc + 1) * T], ci[:, 4:4 + T],
                                           queue="g", lazy=True)
                                cih[0] = h; spl[0] = max(spl[0], h)
                                pr.phase()
                        else:
                            def vd0(vector):
                                vector.tensor_scalar(f32b[0:32, 0:T], psA[0:32, 0:T],
                                                     dtb_sb[:, 0:1], None, ALU.add)
                                return vector.tensor_single_scalar(
                                    e32[0:32, 0:T].bitcast(U32),
                                    f32b[0:32, 0:T].bitcast(U32),
                                    0x7FFFFFFF, ALU.bitwise_and)
                            pr.v(vd0); pr.phase()
                            def sd1(scalar):
                                scalar.activation(e32[0:32, 0:T], e32[0:32, 0:T],
                                                  AF.Exp, scale=-1.0)
                                return scalar.activation(e32[0:32, 0:T], e32[0:32, 0:T],
                                                         AF.Ln, bias=1.0)
                            pr.s(sd1); pr.phase()
                            def vd2(vector):
                                return vector.scalar_tensor_tensor(
                                    dt_sb[:, 0:T], f32b[0:32, 0:T], 0.0, e32[0:32, 0:T],
                                    ALU.max, ALU.add)
                            pr.v(vd2); pr.phase()

            pr.v(lambda vector: vector.tensor_scalar(ca_sb[:, 0:T], dt_sb[:, 0:T],
                                                     an_sb[:, 0:1], None, ALU.mult))
            pr.phase()
            def vca(vector):
                r = None
                for c in range(NCH):
                    r = vector.tensor_tensor_scan(ca_sb[:, c * Q:(c + 1) * Q], ones32[:],
                                                  ca_sb[:, c * Q:(c + 1) * Q], 0.0,
                                                  ALU.mult, ALU.add)
                return r
            pr.v(vca); pr.phase()
            def vde(vector):
                r = None
                for c in range(NCH):
                    r = vector.tensor_scalar(det_sb[:, c * Q:(c + 1) * Q],
                                             ca_sb[:, c * Q:(c + 1) * Q],
                                             ca_sb[:, (c + 1) * Q - 1:(c + 1) * Q],
                                             None, ALU.subtract)
                return r
            pr.v(vde); pr.phase()
            pr.s(lambda scalar: scalar.activation(det_sb[:, 0:T], det_sb[:, 0:T],
                                                  AF.Exp, scale=-1.0))
            pr.phase()
            with nc.psum_tensor(f"{pf}pstc{b}", [128, 2048], F32) as pstc:
                for src, dst in ((ca_sb, caT), (dt_sb, dtT), (det_sb, decE)):
                    def tca(tensor, src=src):
                        r = None
                        for c in range(NCH):
                            r = tensor.transpose(pstc[:, c * 128:c * 128 + 32],
                                                 src[:, c * Q:(c + 1) * Q],
                                                 pr.identf[0:32, 0:32])
                        return r
                    pr.t(tca); pr.phase()
                    def vca2(vector, dst=dst):
                        r = None
                        for c in range(NCH):
                            r = vector.tensor_copy(dst[:, c * 32:(c + 1) * 32],
                                                   pstc[:, c * 128:c * 128 + 32])
                        return r
                    pr.v(vca2); pr.phase()
            pr.s(lambda scalar: scalar.activation(expcaT[:], caT[:], AF.Exp))
            pr.phase()
            def vdk(vector):
                return vector.tensor_copy(det_sb[:, 0:16], ca_sb[:, Q - 1::Q])
            pr.v(vdk); pr.phase()
            pr.s(lambda scalar: scalar.activation(det_sb[:, 0:16], det_sb[:, 0:16], AF.Exp))
            pr.phase()
            pr.dma(dk_sp, det_sb[:, 0:16])
            pr.phase()
            pr.dma(dchk_row[:], dk_sp.rearrange("p c -> (p c)").unsqueeze(0))
            pr.phase()
            with nc.psum_tensor(f"{pf}psk{b}", [128, 512], F32) as psk:
                pr.t(lambda tensor: tensor.matmul(psk[:], onesf[:, 0:128], dchk_row[:],
                                                  start=True, stop=True))
                pr.phase()
                pr.v(lambda vector: vector.tensor_copy(dchkr[:], psk[:]))
                pr.phase()

            pr.v(lambda vector: vector.memset(S_sb[:], 0.0))
            pr.phase()
            with ExitStack() as ces:
                op_sb = ces.enter_context(nc.sbuf_tensor(f"{pf}op{b}", [128, 16 * 1024], BF16))
                self32x = ces.enter_context(nc.sbuf_tensor(f"{pf}sel32{b}", [32, 32 * 128], F32))
                dif = ces.enter_context(nc.sbuf_tensor(f"{pf}dif{b}", [128, 2048], F32))
                mh = ces.enter_context(nc.sbuf_tensor(f"{pf}mh{b}", [128, 32 * 128], BF16))
                cbm = ces.enter_context(nc.sbuf_tensor(f"{pf}cbm{b}", [128, 128], BF16))
                bts = ces.enter_context(nc.sbuf_tensor(f"{pf}bts{b}", [128, 128], BF16))
                xdt = ces.enter_context(nc.sbuf_tensor(f"{pf}xdt{b}", [128, 2048], BF16))
                xw = ces.enter_context(nc.sbuf_tensor(f"{pf}xw{b}", [128, 2048], BF16))
                xtc = ces.enter_context(nc.sbuf_tensor(f"{pf}xtc{b}", [128, 2048], BF16))
                ygc = ces.enter_context(nc.sbuf_tensor(f"{pf}ygc{b}", [128, 2048], BF16))
                f32s = ces.enter_context(nc.sbuf_tensor(f"{pf}f32s{b}", [128, 2048], F32))
                g32s = ces.enter_context(nc.sbuf_tensor(f"{pf}g32s{b}", [128, 2048], F32))
                ygT = ces.enter_context(nc.sbuf_tensor(f"{pf}ygT{b}", [128, 2048], BF16))
                ot8 = ces.enter_context(nc.sbuf_tensor(f"{pf}ot8{b}", [128, 1024], BF16))
                psY = ces.enter_context(nc.psum_tensor(f"{pf}psY{b}", [128, 1024], F32))
                psY2 = ces.enter_context(nc.psum_tensor(f"{pf}psY2{b}", [128, 1024], F32))
                pstx = ces.enter_context(nc.psum_tensor(f"{pf}pstx{b}", [128, 1024], BF16))
                psS = ces.enter_context(nc.psum_tensor(f"{pf}psS{b}", [128, 1024], F32))
                pr.dma(op_sb[:], outprojT, queue="g")
                pr.dma(self32x[:], cselap)
                pr.phase()
                for c in range(NCH):
                    t0 = c * Q
                    xstrh = [0]

                    def tsel(r):
                        ps = psY if r % 2 == 0 else psY2
                        def tf_(tensor, r=r, ps=ps):
                            rr = None
                            for i in range(8):
                                h = r * 8 + i
                                rr = tensor.matmul(ps[:, i * 128:(i + 1) * 128],
                                                   self32x[:, h * 128:(h + 1) * 128],
                                                   ca_sb[:, t0:t0 + Q],
                                                   start=True, stop=True)
                            return rr
                        pr.t(tf_)

                    def vdif(r):
                        ps = psY if r % 2 == 0 else psY2
                        dh = (r % 2) * 1024
                        def vf_(vector, r=r, ps=ps, dh=dh, c=c):
                            rr = None
                            for i in range(8):
                                h = r * 8 + i
                                rr = vector.tensor_scalar(
                                    dif[:, dh + i * 128:dh + (i + 1) * 128],
                                    ps[:, i * 128:(i + 1) * 128],
                                    caT[:, c * 32 + h:c * 32 + h + 1],
                                    0.0, ALU.subtract, ALU.min)
                            return rr
                        pr.v(vf_)

                    def sexp(r):
                        dh = (r % 2) * 1024
                        pr.s(lambda scalar, r=r, dh=dh: scalar.activation(
                            mh[:, r * 1024:(r + 1) * 1024],
                            dif[:, dh:dh + 1024], AF.Exp))

                    tsel(0); pr.phase()
                    tsel(1); SP("a"); vdif(0); pr.phase()
                    tsel(2); SP("a"); vdif(1); SP("a"); sexp(0); pr.phase()
                    tsel(3); SP("a"); vdif(2); SP("a"); sexp(1); pr.phase()
                    pr.t(lambda tensor: tensor.matmul(psS[0:128, 0:128], BT_sb[:, t0:t0 + Q],
                                                      CT_sb[:, t0:t0 + Q],
                                                      start=True, stop=True))
                    SP("a"); vdif(3); SP("a"); sexp(2); pr.phase()
                    pr.v(lambda vector: vector.tensor_tensor(cbm[:], psS[:, 0:128],
                                                             pr.tri16[:], ALU.mult))
                    SP("a")
                    sexp(3)
                    xv2 = xbc_sp.rearrange("p (k t) -> p k t", t=T)
                    pr.need(spl[0])
                    xstrh[0] = pr.dma(xw[:].rearrange("p (k t) -> p k t", t=128),
                                      xv2[:, :, t0:t0 + Q], lazy=True)
                    pr.phase()
                    def vmh(vector):
                        return vector.tensor_tensor(
                            mh[:].rearrange("p (h s) -> p h s", s=128),
                            mh[:].rearrange("p (h s) -> p h s", s=128),
                            cbm[:].unsqueeze(1).broadcast_to([128, 32, 128]), ALU.mult)
                    pr.need(xstrh[0])
                    pr.v(vmh)
                    SP("b")
                    def txt(tensor):
                        r = None
                        for i in range(8):
                            r = tensor.transpose(pstx[:, i * 128:(i + 1) * 128],
                                                 xw[:, i * 128:(i + 1) * 128],
                                                 pr.ident16[:])
                        return r
                    pr.t(txt); pr.phase()
                    pr.v(lambda vector: vector.tensor_copy(xtc[:, 0:1024], pstx[:, 0:1024]))
                    pr.phase()
                    def txt2(tensor):
                        r = None
                        for i in range(8):
                            r = tensor.transpose(pstx[:, i * 128:(i + 1) * 128],
                                                 xw[:, (8 + i) * 128:(9 + i) * 128],
                                                 pr.ident16[:])
                        return r
                    pr.t(txt2); pr.phase()
                    pr.v(lambda vector: vector.tensor_copy(xtc[:, 1024:2048], pstx[:, 0:1024]))
                    pr.phase()
                    def vxdt(vector, c=c):
                        dtv = dtT[:, c * 32:(c + 1) * 32]
                        bc = dtv.unsqueeze(2).broadcast_to([128, 32, 64])
                        return vector.tensor_tensor(
                            xdt[:].rearrange("p (h q) -> p h q", q=64),
                            xtc[:].rearrange("p (h q) -> p h q", q=64), bc, ALU.mult)
                    pr.v(vxdt)
                    SP("c")
                    pr.t(lambda tensor: tensor.transpose(
                        pstx[:, 0:128], BT_sb[:, t0:t0 + Q], pr.ident16[:]))
                    pr.phase()

                    def tin(hh):
                        def tf_(tensor, hh=hh, t0=t0):
                            r = None
                            for i in range(16):
                                h = hh * 16 + i
                                tensor.matmul(psY[:, i * 64:(i + 1) * 64],
                                              mh[:, h * 128:(h + 1) * 128],
                                              xdt[:, h * 64:(h + 1) * 64],
                                              start=True, stop=True)
                                r = tensor.matmul(psY2[:, i * 64:(i + 1) * 64],
                                                  CT_sb[:, t0:t0 + Q],
                                                  S_sb[:, h * 64:(h + 1) * 64],
                                                  start=True, stop=True)
                            return r
                        pr.t(tf_)

                    def vev(hh):
                        def vf_(vector, hh=hh, c=c):
                            r = None
                            for i in range(16):
                                h = hh * 16 + i
                                vector.tensor_scalar(
                                    f32s[:, i * 64:(i + 1) * 64],
                                    psY2[:, i * 64:(i + 1) * 64],
                                    expcaT[:, c * 32 + h:c * 32 + h + 1], None, ALU.mult)
                                r = vector.tensor_tensor(
                                    ygc[:, h * 64:(h + 1) * 64],
                                    f32s[:, i * 64:(i + 1) * 64],
                                    psY[:, i * 64:(i + 1) * 64], ALU.add)
                            return r
                        pr.v(vf_)

                    def tst(r):
                        half = (r % 2) * 512
                        def tf_(tensor, r=r, half=half):
                            rr = None
                            for i in range(8):
                                h = r * 8 + i
                                rr = tensor.matmul(psS[:, half + i * 64:half + (i + 1) * 64],
                                                   bts[:], xw[:, h * 64:(h + 1) * 64],
                                                   start=True, stop=True)
                            return rr
                        pr.t(tf_)

                    def vst(r):
                        half = (r % 2) * 512
                        def vf_(vector, r=r, half=half, c=c):
                            rr = None
                            for i in range(8):
                                h = r * 8 + i
                                rr = vector.scalar_tensor_tensor(
                                    S_sb[:, h * 64:(h + 1) * 64],
                                    S_sb[:, h * 64:(h + 1) * 64],
                                    dchkr[:, h * 16 + c:h * 16 + c + 1],
                                    psS[:, half + i * 64:half + (i + 1) * 64],
                                    ALU.mult, ALU.add)
                            return rr
                        pr.v(vf_)

                    def vxw(vector, c=c):
                        dv2 = decE[:, c * 32:(c + 1) * 32]
                        bc = dv2.unsqueeze(2).broadcast_to([128, 32, 64])
                        vector.tensor_tensor(
                            xw[:].rearrange("p (h q) -> p h q", q=64),
                            xdt[:].rearrange("p (h q) -> p h q", q=64), bc, ALU.mult)
                        return vector.tensor_copy(bts[:], pstx[:, 0:128])
                    pr.v(vxw); SP("d"); tin(0); pr.phase()
                    vev(0); SP("e"); tst(0); pr.phase()
                    vst(0); SP("e"); tin(1); pr.phase()
                    vev(1); SP("e"); tst(1); pr.phase()
                    vst(1); SP("e"); tst(2); pr.phase()
                    vst(2); SP("e"); tst(3); pr.phase()
                    vst(3); SP("e")
                    def vsk(vector):
                        vector.tensor_tensor(f32s[:], xtc[:], drep[:], ALU.mult)
                        return vector.tensor_tensor(ygc[:], ygc[:], f32s[:], ALU.add)
                    pr.v(vsk)
                    zv = z0_sp.rearrange("p (k t) -> p k t", t=T)
                    pr.need(spl[0])
                    xstrh[0] = pr.dma(xw[:].rearrange("p (k t) -> p k t", t=128),
                                      zv[:, :, t0:t0 + Q], lazy=True)
                    pr.phase()

                    pr.need(xstrh[0])
                    def tzt(tensor):
                        r = None
                        for i in range(8):
                            r = tensor.transpose(pstx[:, i * 128:(i + 1) * 128],
                                                 xw[:, i * 128:(i + 1) * 128],
                                                 pr.ident16[:])
                        return r
                    pr.t(tzt); pr.phase()
                    pr.v(lambda vector: vector.tensor_copy(xtc[:, 0:1024], pstx[:, 0:1024]))
                    pr.phase()
                    def tzt2(tensor):
                        r = None
                        for i in range(8):
                            r = tensor.transpose(pstx[:, i * 128:(i + 1) * 128],
                                                 xw[:, (8 + i) * 128:(9 + i) * 128],
                                                 pr.ident16[:])
                        return r
                    pr.t(tzt2); pr.phase()
                    pr.v(lambda vector: vector.tensor_copy(xtc[:, 1024:2048], pstx[:, 0:1024]))
                    pr.phase()
                    pr.s(lambda scalar: scalar.activation(g32s[:], xtc[:], AF.Silu))
                    pr.phase()
                    def vsl(vector):
                        return vector.tensor_tensor(ygc[:], ygc[:], g32s[:], ALU.mult)
                    pr.v(vsl); pr.phase()
                    pr.s(lambda scalar: scalar.activation(g32s[:], ygc[:], AF.Square,
                                                          accum_out=stat[:, 0:1]))
                    pr.phase()
                    pr.v(lambda vector: vector.tensor_scalar(
                        stat[:, 1:2], stat[:, 0:1], 1.0 / 2048.0, EPS, ALU.mult, ALU.add))
                    pr.phase()
                    pr.s(lambda scalar: scalar.activation(stat[:, 2:3], stat[:, 1:2], AF.Ln))
                    pr.phase()
                    pr.s(lambda scalar: scalar.activation(stat[:, 3:4], stat[:, 2:3],
                                                          AF.Exp, scale=-0.5))
                    pr.phase()
                    def vrm2(vector):
                        vector.tensor_scalar(ygc[:], ygc[:], stat[:, 3:4], None, ALU.mult)
                        return vector.tensor_tensor(ygc[:], ygc[:], nwrep[:], ALU.mult)
                    pr.v(vrm2); pr.phase()
                    def tyg(tensor):
                        r = None
                        for i in range(8):
                            r = tensor.transpose(pstx[:, i * 128:(i + 1) * 128],
                                                 ygc[:, i * 128:(i + 1) * 128],
                                                 pr.ident16[:])
                        return r
                    pr.t(tyg); pr.phase()
                    pr.v(lambda vector: vector.tensor_copy(ygT[:, 0:1024], pstx[:, 0:1024]))
                    pr.phase()
                    def tyg2(tensor):
                        r = None
                        for i in range(8):
                            r = tensor.transpose(pstx[:, i * 128:(i + 1) * 128],
                                                 ygc[:, (8 + i) * 128:(9 + i) * 128],
                                                 pr.ident16[:])
                        return r
                    def topk(k0, k1):
                        def tf_(tensor, k0=k0, k1=k1):
                            r = None
                            for dm in range(8):
                                ps = psY if dm < 4 else psY2
                                o = (dm % 4) * 128
                                for kt in range(k0, k1):
                                    r = tensor.matmul(
                                        ps[:, o:o + 128],
                                        op_sb[:, kt * 1024 + dm * 128:kt * 1024 + dm * 128 + 128],
                                        ygT[:, kt * 128:(kt + 1) * 128],
                                        start=(kt == 0), stop=(kt == 15))
                            return r
                        pr.t(tf_)
                    if L0SER["f"]:
                        pr.t(tyg2); pr.phase()
                        pr.v(lambda vector: vector.tensor_copy(ygT[:, 1024:2048],
                                                               pstx[:, 0:1024]))
                        pr.phase()
                        topk(0, 16); pr.phase()
                    else:
                        pr.t(tyg2); topk(0, 8); pr.phase()
                        pr.v(lambda vector: vector.tensor_copy(ygT[:, 1024:2048],
                                                               pstx[:, 0:1024]))
                        pr.phase()
                        topk(8, 16); pr.phase()
                    def vot(vector):
                        r = None
                        for dm in range(8):
                            ps = psY if dm < 4 else psY2
                            o = (dm % 4) * 128
                            r = vector.tensor_copy(ot8[:, dm * 128:(dm + 1) * 128],
                                                   ps[:, o:o + 128])
                        return r
                    pr.v(vot); pr.phase()
                    def tot(tensor):
                        r = None
                        for dm in range(8):
                            r = tensor.transpose(pstx[:, dm * 128:(dm + 1) * 128],
                                                 ot8[:, dm * 128:(dm + 1) * 128],
                                                 pr.ident16[:])
                        return r
                    pr.t(tot)
                    pr.need(y0hs[c])
                    pr.dma(xtc[:, 0:DM], y0_sp[:, c * DM:(c + 1) * DM])
                    pr.dma(f32s[:, 0:DM], xin[b, c * 128:(c + 1) * 128, :])
                    pr.phase()
                    def vres(vector):
                        vector.tensor_tensor(f32s[:, 0:DM], f32s[:, 0:DM],
                                             xtc[:, 0:DM], ALU.add)
                        return vector.tensor_tensor(f32s[:, 0:DM], f32s[:, 0:DM],
                                                    pstx[:, 0:1024], ALU.add)
                    pr.v(vres); pr.phase()
                    pr.dma(xout[b, c * 128:(c + 1) * 128, :], f32s[:, 0:DM])
                    pr.phase()
    es.close()
    pr.phase()


def build_l1_attn():
    nc = bass.Bass("TRN2", target_bir_lowering=False)
    xin = nc.dram_tensor("x", [NB, T, DM], F32, kind="ExternalInput").ap()
    xout = nc.dram_tensor("xo", [NB, T, DM], F32, kind="ExternalOutput").ap()
    W = _l1_declare(nc)
    with ExitStack() as es:
        pr = Pr(nc, es)
        load_consts(pr, nc, sel=True)
        emit_l1(nc, pr, xin, xout, W, "B_")
        pr.finish()
    return nc


def _l1_declare(nc, pf=""):
    W = {}
    for d in ("fw", "bw"):
        W[d] = dict(
            winT=nc.dram_tensor(f"{pf}{d}_winT", [128, 8 * 4096], F32, kind="ExternalInput").ap(),
            convw=nc.dram_tensor(f"{pf}{d}_convw", [128, 16 * 4], F32, kind="ExternalInput").ap(),
            convb=nc.dram_tensor(f"{pf}{d}_convb", [128, 16], F32, kind="ExternalInput").ap(),
            wxT=nc.dram_tensor(f"{pf}{d}_wxT", [128, 16 * 96], F32, kind="ExternalInput").ap(),
            wdtT=nc.dram_tensor(f"{pf}{d}_wdtT", [64, 2048], F32, kind="ExternalInput").ap(),
            bdt=nc.dram_tensor(f"{pf}{d}_bdt", [128, 16], F32, kind="ExternalInput").ap(),
            alog=nc.dram_tensor(f"{pf}{d}_alog", [128, 16 * 16], F32, kind="ExternalInput").ap(),
            dvec=nc.dram_tensor(f"{pf}{d}_dvec", [128, 16], F32, kind="ExternalInput").ap(),
            woutT=nc.dram_tensor(f"{pf}{d}_woutT", [128, 16 * 1024], F32,
                                 kind="ExternalInput").ap(),
        )
    W["fuseWT"] = nc.dram_tensor(pf + "fuseWT", [128, 16 * 1024], F32,
                                 kind="ExternalInput").ap()
    W["fuseb"] = nc.dram_tensor(pf + "fuseb", [128, 8], F32, kind="ExternalInput").ap()
    return W


def emit_l1(nc, pr, xin, xout, Wd, pf):
    DI = 2048
    NDT = DI // 128
    N = 16
    TH = 512
    NSEG = T // TH
    W = {"fw": Wd["fw"], "bw": Wd["bw"]}
    fuseWT = Wd["fuseWT"]
    fuseb = Wd["fuseb"]
    u_sp = nc.dram_tensor(pf + "u_sp", [128, NDT * T], BF16, kind="Internal").ap()
    dt_sp = nc.dram_tensor(pf + "dt_sp", [128, NDT * T], BF16, kind="Internal").ap()
    z_sp = nc.dram_tensor(pf + "z_sp", [128, NDT * T], BF16, kind="Internal").ap()
    y_sp = nc.dram_tensor(pf + "y_sp", [128, NDT * T], BF16, kind="Internal").ap()
    fo_sp = nc.dram_tensor(pf + "fo_sp", [128, 8 * T], BF16, kind="Internal").ap()
    bo_sp = nc.dram_tensor(pf + "bo_sp", [128, 8 * T], BF16, kind="Internal").ap()
    if True:
        es = ExitStack()
        sel16 = es.enter_context(nc.sbuf_tensor(pf + "sel16", [32, 32 * 128], BF16))
        pr.dma(sel16[:], pr.csel, queue="g")
        pr.phase()
        stat = es.enter_context(nc.sbuf_tensor(pf + "stat", [128, 8], F32))
        xnT = es.enter_context(nc.sbuf_tensor(pf + "xnT", [128, 8 * T], BF16))
        wx_sb = es.enter_context(nc.sbuf_tensor(pf + "wx_sb", [128, 16 * 96], BF16))
        wdt_sb = es.enter_context(nc.sbuf_tensor(pf + "wdt_sb", [64, 2048], BF16))
        cw_sb = es.enter_context(nc.sbuf_tensor(pf + "cw_sb", [128, 16 * 4], F32))
        cb_sb = es.enter_context(nc.sbuf_tensor(pf + "cb_sb", [128, 16], F32))
        bdt_sb = es.enter_context(nc.sbuf_tensor(pf + "bdt_sb", [128, 16], F32))
        dv_sb = es.enter_context(nc.sbuf_tensor(pf + "dv_sb", [128, 16], F32))
        aneg = es.enter_context(nc.sbuf_tensor(pf + "aneg", [128, 16 * 16], F32))
        ci = es.enter_context(nc.sbuf_tensor(pf + "ci", [128, T + 4], BF16))
        wslc = es.enter_context(nc.sbuf_tensor(pf + "wslc", [128, 8 * 128], BF16))
        fuseb_sb = es.enter_context(nc.sbuf_tensor(pf + "fuseb_sb", [128, 8], F32))
        bct_sb = es.enter_context(nc.sbuf_tensor(pf + "bct_sb", [32, T], BF16))
        pr.dma(fuseb_sb[:], fuseb)
        pr.phase()

        for b in range(NB):
            with nc.sbuf_tensor(f"{pf}lnx{b}", [128, DM], F32) as xs, \
                 nc.sbuf_tensor(f"{pf}xnb{b}", [128, DM], BF16) as xnb, \
                 nc.sbuf_tensor(f"{pf}sq{b}", [128, DM], F32) as sqs, \
                 nc.psum_tensor(f"{pf}pstT{b}", [128, 2048], BF16) as pst:
                pr.scr_sq = sqs
                for tt in range(NTT):
                    pr.dma(xs[:], xin[b, tt * 128:(tt + 1) * 128, :])
                    pr.phase()
                    emit_ln(pr, xs, xnb, 1, DM, stat)
                    emit_transpose(pr, pst, [(xnb, kd * 128, xnT, kd * T + tt * 128)
                                             for kd in range(8)])

            foh = [0]
            for di, d in enumerate(("fw", "bw")):
                wt = W[d]
                if di == 1:
                    def rev(vector):
                        r = None
                        for kd in range(8):
                            vector.tensor_copy(ci[:, 0:T], xnT[:, kd * T:(kd + 1) * T])
                            src = ci[:, 0:T]
                            r = vector.tensor_copy(xnT[:, kd * T:(kd + 1) * T], src[:, ::-1])
                        return r
                    pr.v(rev)
                    pr.phase()

                pr.dma(wx_sb[:], wt["wxT"], queue="g")
                pr.dma(wdt_sb[:], wt["wdtT"], queue="g")
                pr.dma(cw_sb[:], wt["convw"])
                pr.dma(cb_sb[:], wt["convb"])
                pr.dma(bdt_sb[:], wt["bdt"])
                pr.dma(dv_sb[:], wt["dvec"])
                pr.dma(aneg[:], wt["alog"])
                pr.phase()
                pr.s(lambda scalar: scalar.activation(aneg[:], aneg[:], AF.Exp))
                pr.phase()
                pr.v(lambda vector: vector.tensor_scalar(aneg[:], aneg[:], -1.0, None, ALU.mult))
                pr.phase()

                l1xz, _ = nc.enter_named_scope(f"l1xz{b}{di}", False)
                cih = [0]
                spl = [0]
                dtscope = ExitStack()
                dtp_sb = dtscope.enter_context(
                    nc.sbuf_tensor(f"{pf}dtp{b}{di}", [64, T], BF16))
                with nc.psum_tensor(f"{pf}psA{b}{di}", [128, 2048], F32) as psA, \
                     nc.psum_tensor(f"{pf}psX{b}{di}", [128, 2048], F32) as psX, \
                     nc.sbuf_tensor(f"{pf}ciB{b}{di}", [128, T + 4], BF16) as ciB, \
                     nc.sbuf_tensor(f"{pf}wslB{b}{di}", [128, 8 * 128], BF16) as wslc2, \
                     nc.sbuf_tensor(f"{pf}e32a{b}{di}", [128, T], F32) as e32, \
                     nc.sbuf_tensor(f"{pf}f32a{b}{di}", [128, T], F32) as f32b:
                    cis = [ci, ciB]
                    fbs = [f32b, e32]
                    wsl = [wslc, wslc2]
                    wh = [0, 0]
                    sph = [0] * 32
                    wv = wt["winT"].rearrange("p (kt o) -> p kt o", o=4096)

                    def ldw(ot):
                        wh[ot % 2] = pr.dma(
                            wsl[ot % 2][:].rearrange("p (kt o) -> p kt o", o=128),
                            wv[:, :, ot * 128:(ot + 1) * 128], queue="g", lazy=True)

                    def mm(ot):
                        pr.need(wh[ot % 2])
                        def tf(tensor, ot=ot):
                            r = None
                            for n0 in range(0, T, 512):
                                for kd in range(8):
                                    r = tensor.matmul(
                                        psA[:, n0:n0 + 512],
                                        wsl[ot % 2][:, kd * 128:(kd + 1) * 128],
                                        xnT[:, kd * T + n0:kd * T + n0 + 512],
                                        start=(kd == 0), stop=(kd == 7))
                            return r
                        pr.t(tf)

                    def evconv(ot):
                        p = ot % 2
                        if ot >= 2:
                            pr.need(sph[ot - 2])
                        if ot < 16:
                            def vc0(vector, ot=ot, p=p):
                                vector.memset(cis[p][:, 0:4], 0.0)
                                vector.tensor_copy(cis[p][:, 4:4 + T], psA[:, 0:T])
                                vector.tensor_scalar(fbs[p][:, 0:T], cis[p][:, 1:1 + T],
                                                     cw_sb[:, ot * 4:ot * 4 + 1], None,
                                                     ALU.mult)
                                for k in (1, 2, 3):
                                    vector.scalar_tensor_tensor(
                                        fbs[p][:, 0:T], cis[p][:, 1 + k:1 + k + T],
                                        cw_sb[:, ot * 4 + k:ot * 4 + k + 1],
                                        fbs[p][:, 0:T], ALU.mult, ALU.add)
                                return vector.tensor_scalar(
                                    fbs[p][:, 0:T], fbs[p][:, 0:T], cb_sb[:, ot:ot + 1],
                                    None, ALU.add)
                            pr.v(vc0)
                        else:
                            pr.v(lambda vector, p=p: vector.tensor_copy(
                                cis[p][:, 4:4 + T], psA[:, 0:T]))

                    def silu(ot):
                        if ot < 16:
                            p = ot % 2
                            pr.s(lambda scalar, p=p: scalar.activation(
                                cis[p][:, 4:4 + T], fbs[p][:, 0:T], AF.Silu))

                    def post(ot):
                        p = ot % 2
                        if ot < 16:
                            def tfx(tensor, ot=ot, p=p):
                                r = None
                                for n0 in range(0, T, 512):
                                    r = tensor.matmul(psX[0:96, n0:n0 + 512],
                                                      wx_sb[:, ot * 96:(ot + 1) * 96],
                                                      cis[p][:, 4 + n0:4 + n0 + 512],
                                                      start=(ot == 0), stop=(ot == 15))
                                return r
                            pr.t(tfx)
                            h = pr.dma(u_sp[:, ot * T:(ot + 1) * T], cis[p][:, 4:4 + T],
                                       queue="g", lazy=True)
                        else:
                            h = pr.dma(z_sp[:, (ot - 16) * T:(ot - 15) * T],
                                       cis[p][:, 4:4 + T], queue="g", lazy=True)
                        sph[ot] = h
                        spl[0] = max(spl[0], h)

                    ldw(0); pr.phase()
                    for ot in range(33):
                        if ot < 32:
                            mm(ot)
                        if ot >= 1:
                            silu(ot - 1)
                        if ot + 1 < 32:
                            ldw(ot + 1)
                        pr.phase()
                        if ot < 32:
                            evconv(ot)
                        if ot >= 1:
                            post(ot - 1)
                        pr.phase()
                    def vx(vector):
                        vector.tensor_copy(dtp_sb[:, 0:T], psX[0:64, 0:T])
                        return vector.tensor_copy(bct_sb[:, 0:T], psX[64:96, 0:T])
                    pr.v(vx); pr.phase()

                with nc.psum_tensor(f"{pf}psB{b}{di}", [128, 2048], F32) as psB, \
                     nc.sbuf_tensor(f"{pf}e32b{b}{di}", [128, T], F32) as e32, \
                     nc.sbuf_tensor(f"{pf}f32c{b}{di}", [128, T], F32) as f32b:
                    for kt in range(NDT):
                        def tf2(tensor, kt=kt):
                            r = None
                            for n0 in range(0, T, 512):
                                r = tensor.matmul(psB[:, n0:n0 + 512],
                                                  wdt_sb[:, kt * 128:(kt + 1) * 128],
                                                  dtp_sb[:, n0:n0 + 512],
                                                  start=True, stop=True)
                            return r
                        pr.t(tf2); pr.phase()
                        def vd0(vector, kt=kt):
                            vector.tensor_scalar(f32b[:, 0:T], psB[:, 0:T],
                                                 bdt_sb[:, kt:kt + 1], None, ALU.add)
                            return vector.tensor_single_scalar(
                                e32[:, 0:T].bitcast(U32), f32b[:, 0:T].bitcast(U32),
                                0x7FFFFFFF, ALU.bitwise_and)
                        pr.v(vd0); pr.phase()
                        def sd1(scalar):
                            scalar.activation(e32[:, 0:T], e32[:, 0:T], AF.Exp, scale=-1.0)
                            return scalar.activation(e32[:, 0:T], e32[:, 0:T], AF.Ln, bias=1.0)
                        pr.s(sd1); pr.phase()
                        pr.need(cih[0])
                        def vd2(vector):
                            return vector.scalar_tensor_tensor(
                                ci[:, 4:4 + T], f32b[:, 0:T], 0.0, e32[:, 0:T],
                                ALU.max, ALU.add)
                        pr.v(vd2)
                        pr.phase()
                        h = pr.dma(dt_sp[:, kt * T:(kt + 1) * T], ci[:, 4:4 + T],
                                   queue="g", lazy=True)
                        cih[0] = h; spl[0] = max(spl[0], h)
                        pr.phase()
                dtscope.close()
                nc.leave_named_scope(f"l1xz{b}{di}", l1xz, False)

                yh = [0]
                l1sc, _ = nc.enter_named_scope(f"l1scan{b}{di}", False)
                with nc.sbuf_tensor(f"{pf}brep{b}{di}", [128, N * TH], BF16) as brep, \
                     nc.sbuf_tensor(f"{pf}crep{b}{di}", [128, N * TH], BF16) as crep, \
                     nc.sbuf_tensor(f"{pf}su{b}{di}", [128, NDT * TH], BF16) as s_u, \
                     nc.sbuf_tensor(f"{pf}sdt{b}{di}", [128, NDT * TH], BF16) as s_dt, \
                     nc.sbuf_tensor(f"{pf}sa0{b}{di}", [128, NDT * TH], BF16) as dA0, \
                     nc.sbuf_tensor(f"{pf}sa1{b}{di}", [128, NDT * TH], BF16) as dA1, \
                     nc.sbuf_tensor(f"{pf}sh{b}{di}", [128, NDT * TH], BF16) as h_h, \
                     nc.sbuf_tensor(f"{pf}shB{b}{di}", [128, NDT * TH], BF16) as h_hB, \
                     nc.sbuf_tensor(f"{pf}sy{b}{di}", [128, NDT * TH], BF16) as ya_h, \
                     nc.sbuf_tensor(f"{pf}hcar{b}{di}", [128, NDT * N], F32) as hcar:
                    for seg in range(NSEG):
                        t0 = seg * TH
                        with nc.psum_tensor(f"{pf}psR{b}{di}{seg}", [128, 2048], F32) as psR:
                            for n0 in range(0, N, 2):
                                def tr2(tensor, n0=n0, t0=t0):
                                    r = None
                                    for i, n in enumerate((n0, n0 + 1)):
                                        tensor.matmul(psR[:, i * 1024:i * 1024 + 512],
                                                      sel16[:, n * 128:(n + 1) * 128],
                                                      bct_sb[:, t0:t0 + TH],
                                                      start=True, stop=True)
                                        r = tensor.matmul(
                                            psR[:, i * 1024 + 512:i * 1024 + 1024],
                                            sel16[:, (16 + n) * 128:(17 + n) * 128],
                                            bct_sb[:, t0:t0 + TH],
                                            start=True, stop=True)
                                    return r
                                pr.t(tr2); pr.phase()
                                def vr2(vector, n0=n0):
                                    r = None
                                    for i, n in enumerate((n0, n0 + 1)):
                                        vector.tensor_copy(brep[:, n * TH:(n + 1) * TH],
                                                           psR[:, i * 1024:i * 1024 + 512])
                                        r = vector.tensor_copy(
                                            crep[:, n * TH:(n + 1) * TH],
                                            psR[:, i * 1024 + 512:i * 1024 + 1024])
                                    return r
                                pr.v(vr2); pr.phase()

                        uv = u_sp.rearrange("p (k t) -> p k t", t=T)
                        dv = dt_sp.rearrange("p (k t) -> p k t", t=T)
                        pr.need(spl[0])
                        pr.dma(s_u[:].rearrange("p (k t) -> p k t", t=TH),
                               uv[:, :, t0:t0 + TH])
                        pr.dma(s_dt[:].rearrange("p (k t) -> p k t", t=TH),
                               dv[:, :, t0:t0 + TH])
                        pr.phase()
                        pr.need(yh[0])
                        def vseed(vector):
                            r = None
                            for j in range(NDT):
                                sl = slice(j * TH, (j + 1) * TH)
                                r = vector.tensor_scalar(ya_h[:, sl], s_u[:, sl],
                                                         dv_sb[:, j:j + 1], None, ALU.mult)
                            return r
                        pr.v(vseed); pr.phase()
                        def vw(vector):
                            return vector.tensor_tensor(s_u[:], s_dt[:], s_u[:], ALU.mult)
                        pr.v(vw)
                        def sa0(scalar):
                            r = None
                            for j in range(NDT):
                                r = scalar.activation(
                                    dA0[:, j * TH:(j + 1) * TH],
                                    s_dt[:, j * TH:(j + 1) * TH], AF.Exp,
                                    scale=aneg[:, j * 16:j * 16 + 1])
                            return r
                        pr.s(sa0); pr.phase()

                        for n in range(N):
                            dAc = dA0 if n % 2 == 0 else dA1
                            dAn = dA1 if n % 2 == 0 else dA0
                            hb = h_h if n % 2 == 0 else h_hB
                            hprev = h_hB if n % 2 == 0 else h_h
                            if n + 1 < N:
                                def sa(scalar, n=n, dAn=dAn):
                                    r = None
                                    for j in range(NDT):
                                        r = scalar.activation(
                                            dAn[:, j * TH:(j + 1) * TH],
                                            s_dt[:, j * TH:(j + 1) * TH], AF.Exp,
                                            scale=aneg[:, j * 16 + n + 1:j * 16 + n + 2])
                                    return r
                                pr.s(sa)
                            if n >= 1:
                                def gacc(vector, hprev=hprev):
                                    return vector.tensor_tensor(ya_h[:], ya_h[:], hprev[:],
                                                                ALU.add)
                                pr.v(gacc)
                            def vscan(vector, n=n, dAc=dAc, seg=seg, hb=hb):
                                bv = brep[:, n * TH:(n + 1) * TH].unsqueeze(1) \
                                    .broadcast_to([128, NDT, TH])
                                cv = crep[:, n * TH:(n + 1) * TH].unsqueeze(1) \
                                    .broadcast_to([128, NDT, TH])
                                hv = hb[:].rearrange("p (k t) -> p k t", t=TH)
                                uv3 = s_u[:].rearrange("p (k t) -> p k t", t=TH)
                                vector.tensor_tensor(hv, uv3, bv, ALU.mult)
                                for j in range(NDT):
                                    sl = slice(j * TH, (j + 1) * TH)
                                    col = j * N + n
                                    ini = 0.0 if seg == 0 else hcar[:, col:col + 1]
                                    vector.tensor_tensor_scan(hb[:, sl], dAc[:, sl],
                                                              hb[:, sl], ini,
                                                              ALU.mult, ALU.add)
                                vector.tensor_copy(hcar[:, n::N], hb[:, TH - 1::TH])
                                return vector.tensor_tensor(hv, hv, cv, ALU.mult)
                            pr.v(vscan); pr.phase()
                        def gacc2(vector):
                            return vector.tensor_tensor(ya_h[:], ya_h[:],
                                                        (h_hB if (N - 1) % 2 else h_h)[:],
                                                        ALU.add)
                        pr.v(gacc2); pr.phase()

                        yv = y_sp.rearrange("p (k t) -> p k t", t=T)
                        yh[0] = pr.dma(yv[:, :, t0:t0 + TH],
                                       ya_h[:].rearrange("p (k t) -> p k t", t=TH),
                                       queue="g", lazy=True)
                        pr.phase()

                nc.leave_named_scope(f"l1scan{b}{di}", l1sc, False)
                l1ga, _ = nc.enter_named_scope(f"l1gate{b}{di}", False)
                osp = fo_sp if di == 0 else bo_sp
                with nc.sbuf_tensor(f"{pf}gy{b}{di}", [128, 16 * 512], BF16) as gy, \
                     nc.sbuf_tensor(f"{pf}gz{b}{di}", [128, 16 * 512], BF16) as gz, \
                     nc.sbuf_tensor(f"{pf}wo{b}{di}", [128, 8 * 1024], BF16) as wo_sb, \
                     nc.sbuf_tensor(f"{pf}e32g{b}{di}", [128, T], F32) as e32, \
                     nc.sbuf_tensor(f"{pf}f32g{b}{di}", [128, T], F32) as f32b, \
                     nc.psum_tensor(f"{pf}psO{b}{di}", [128, 8 * 512], F32) as psO:
                    for q0 in range(0, T, 512):
                        yv = y_sp.rearrange("p (k t) -> p k t", t=T)
                        zv = z_sp.rearrange("p (k t) -> p k t", t=T)
                        pr.need(yh[0]); pr.need(spl[0]); pr.need(foh[0])
                        pr.dma(gy[:].rearrange("p (k t) -> p k t", t=512),
                               yv[:, :, q0:q0 + 512])
                        pr.dma(gz[:].rearrange("p (k t) -> p k t", t=512),
                               zv[:, :, q0:q0 + 512])
                        pr.phase()
                        def sg(scalar):
                            return scalar.activation(gz[:], gz[:], AF.Silu)
                        pr.s(sg); pr.phase()
                        def vg2(vector):
                            return vector.tensor_tensor(gy[:], gy[:], gz[:], ALU.mult)
                        pr.v(vg2); pr.phase()
                        for hh in range(2):
                            wv2 = wt["woutT"].rearrange("p (kt o) -> p kt o", o=1024)
                            pr.dma(wo_sb[:].rearrange("p (kt o) -> p kt o", o=1024),
                                   wv2[:, hh * 8:(hh + 1) * 8, :], queue="g")
                            pr.phase()
                            def tw(tensor, hh=hh):
                                r = None
                                for k in range(8):
                                    kt = hh * 8 + k
                                    for dm in range(8):
                                        r = tensor.matmul(
                                            psO[:, dm * 512:(dm + 1) * 512],
                                            wo_sb[:, k * 1024 + dm * 128:
                                                  k * 1024 + dm * 128 + 128],
                                            gy[:, kt * 512:(kt + 1) * 512],
                                            start=(kt == 0), stop=(kt == NDT - 1))
                                return r
                            pr.t(tw); pr.phase()
                        def ve2(vector, di=di):
                            r = None
                            for dm in range(8):
                                dst = gy[:, dm * 512:(dm + 1) * 512]
                                if di == 1:
                                    dst = dst[:, ::-1]
                                r = vector.tensor_copy(dst, psO[:, dm * 512:(dm + 1) * 512])
                            return r
                        pr.v(ve2); pr.phase()
                        for dm in range(8):
                            if di == 0:
                                h = pr.dma(osp[:, dm * T + q0:dm * T + q0 + 512],
                                           gy[:, dm * 512:(dm + 1) * 512], queue="g",
                                           lazy=True)
                            else:
                                h = pr.dma(osp[:, dm * T + T - q0 - 512:dm * T + T - q0],
                                           gy[:, dm * 512:(dm + 1) * 512], queue="g",
                                           lazy=True)
                            foh[0] = h
                        pr.phase()

                nc.leave_named_scope(f"l1gate{b}{di}", l1ga, False)
            with nc.sbuf_tensor(f"{pf}o_t{b}", [128, DM], F32) as o_t, \
                 nc.sbuf_tensor(f"{pf}fsw{b}", [128, 16 * 1024], BF16) as fw_sb, \
                 nc.sbuf_tensor(f"{pf}fio{b}", [128, 16 * 512], BF16) as fio, \
                 nc.sbuf_tensor(f"{pf}fot{b}", [128, 8 * 512], BF16) as fot, \
                 nc.psum_tensor(f"{pf}psF{b}", [128, 4 * 512], F32) as psF, \
                 nc.psum_tensor(f"{pf}pstF{b}", [128, 1024], BF16) as pstF:
                pr.dma(fw_sb[:], fuseWT, queue="g")
                pr.phase()
                for q0 in range(0, T, 512):
                    fv = fo_sp.rearrange("p (k t) -> p k t", t=T)
                    bv = bo_sp.rearrange("p (k t) -> p k t", t=T)
                    pr.need(foh[0])
                    pr.dma(fio[:, 0:8 * 512].rearrange("p (k t) -> p k t", t=512),
                           fv[:, :, q0:q0 + 512])
                    pr.dma(fio[:, 8 * 512:].rearrange("p (k t) -> p k t", t=512),
                           bv[:, :, q0:q0 + 512])
                    pr.phase()
                    for dh in range(2):
                        def tff(tensor, dh=dh):
                            r = None
                            for dmo in range(dh * 4, dh * 4 + 4):
                                for kt in range(16):
                                    r = tensor.matmul(
                                        psF[:, (dmo % 4) * 512:(dmo % 4) * 512 + 512],
                                        fw_sb[:, kt * 1024 + dmo * 128:
                                              kt * 1024 + dmo * 128 + 128],
                                        fio[:, kt * 512:(kt + 1) * 512],
                                        start=(kt == 0), stop=(kt == 15))
                            return r
                        pr.t(tff); pr.phase()
                        def vff(vector, dh=dh):
                            r = None
                            for dmo in range(dh * 4, dh * 4 + 4):
                                r = vector.tensor_scalar(
                                    fot[:, dmo * 512:(dmo + 1) * 512],
                                    psF[:, (dmo % 4) * 512:(dmo % 4) * 512 + 512],
                                    fuseb_sb[:, dmo:dmo + 1], None, ALU.add)
                            return r
                        pr.v(vff); pr.phase()
                    for st in range(4):
                        def tft(tensor, st=st):
                            r = None
                            for dmo in range(8):
                                r = tensor.transpose(pstF[:, dmo * 128:(dmo + 1) * 128],
                                                     fot[:, dmo * 512 + st * 128:
                                                         dmo * 512 + st * 128 + 128],
                                                     pr.ident16[:])
                            return r
                        pr.t(tft)
                        tt = (q0 + st * 128) // 128
                        pr.dma(o_t[:], xin[b, tt * 128:(tt + 1) * 128, :])
                        pr.phase()
                        def vres(vector):
                            return vector.tensor_tensor(o_t[:], o_t[:], pstF[:, 0:1024],
                                                        ALU.add)
                        pr.v(vres); pr.phase()
                        pr.dma(xout[b, tt * 128:(tt + 1) * 128, :], o_t[:])
                        pr.phase()
    es.close()
    pr.phase()


_PROGS = {}


def _tile_rows(w):
    K_, M_ = w.shape[0] // 128, w.shape[1]
    return np.ascontiguousarray(w.reshape(K_, 128, M_).transpose(1, 0, 2).reshape(128, K_ * M_))


def _get_prog(name):
    if name not in _PROGS:
        _PROGS[name] = {"ffn": build_ffn, "l1": build_l1_attn, "l0": build_l0_attn,
                         "full": build_full}[name]()
    return _PROGS[name]


def _run_ffn(x, wg, wu, wd):
    nc = _get_prog("ffn")
    consts = host_consts()
    in_maps = []
    for c in range(NCORES):
        m = {"c_ident": consts["c_ident"]}
        m["x"] = np.ascontiguousarray(x[c * NB:(c + 1) * NB])
        m["wg"] = _tile_rows(wg)
        m["wu"] = _tile_rows(wu)
        m["wd"] = _tile_rows(wd)
        in_maps.append(m)
    res = run_bass_kernel_spmd(nc, in_maps, core_ids=list(range(NCORES)))
    return np.concatenate([r["xo"] for r in res.results], axis=0)


def _tile_vec(v):
    return np.ascontiguousarray(v.reshape(-1, 128).T)


def _l1_weight_maps(I):
    m = {}
    for d in ("fw", "bw"):
        m[f"{d}_winT"] = _tile_rows(np.ascontiguousarray(I[f"L1_attn_{d}_Win"].T))
        m[f"{d}_convw"] = _tile_rows(np.ascontiguousarray(I[f"L1_attn_{d}_convw"][:, 0, :]))
        m[f"{d}_convb"] = _tile_vec(I[f"L1_attn_{d}_convb"])
        m[f"{d}_wxT"] = _tile_rows(np.ascontiguousarray(I[f"L1_attn_{d}_Wx"].T))
        m[f"{d}_wdtT"] = np.ascontiguousarray(I[f"L1_attn_{d}_Wdt"].T)
        m[f"{d}_bdt"] = _tile_vec(I[f"L1_attn_{d}_bdt"])
        m[f"{d}_alog"] = _tile_rows(np.ascontiguousarray(I[f"L1_attn_{d}_Alog"]))
        m[f"{d}_dvec"] = _tile_vec(I[f"L1_attn_{d}_D"])
        m[f"{d}_woutT"] = _tile_rows(np.ascontiguousarray(I[f"L1_attn_{d}_Wout"].T))
    m["fuseWT"] = _tile_rows(np.ascontiguousarray(I["L1_attn_fuse_W"].T))
    m["fuseb"] = _tile_vec(I["L1_attn_fuse_b"])
    return m


def _l0_weight_maps(I):
    H = 32
    D = I["L0_attn_D"]
    hp = np.repeat(D, 64)
    return {
        "inprojT": _tile_rows(np.ascontiguousarray(I["L0_attn_in_proj_w"].T)),
        "convw": _tile_rows(np.ascontiguousarray(I["L0_attn_conv_w"])),
        "convbi": _tile_vec(I["L0_attn_conv_b"]),
        "dtbias": np.ascontiguousarray(I["L0_attn_dt_bias"].reshape(32, 1)),
        "alogi": np.ascontiguousarray(I["L0_attn_A_log"].reshape(32, 1)),
        "dvec_rep": np.ascontiguousarray(np.broadcast_to(hp, (128, 2048))),
        "normw_rep": np.ascontiguousarray(np.broadcast_to(I["L0_attn_norm_w"], (128, 2048))),
        "lnw_rep": np.ascontiguousarray(np.broadcast_to(I["L0_attn_ln_w"], (128, 1024))),
        "lnb_rep": np.ascontiguousarray(np.broadcast_to(I["L0_attn_ln_b"], (128, 1024))),
        "outprojT": _tile_rows(np.ascontiguousarray(I["L0_attn_out_proj_w"].T)),
    }


def _run_l0(x, I):
    nc = _get_prog("l0")
    consts = host_consts()
    wm = _l0_weight_maps(I)
    in_maps = []
    for c in range(NCORES):
        m = {"c_ident": consts["c_ident"], "c_sel": consts["c_sel"],
             "c_tri": consts["c_tri"]}
        m.update(wm)
        m["x"] = np.ascontiguousarray(x[c * NB:(c + 1) * NB])
        in_maps.append(m)
    res = run_bass_kernel_spmd(nc, in_maps, core_ids=list(range(NCORES)))
    return np.concatenate([r["xo"] for r in res.results], axis=0)


def _run_l1(x, I):
    nc = _get_prog("l1")
    consts = host_consts()
    wm = _l1_weight_maps(I)
    in_maps = []
    for c in range(NCORES):
        m = {"c_ident": consts["c_ident"], "c_sel": consts["c_sel"]}
        m.update(wm)
        m["x"] = np.ascontiguousarray(x[c * NB:(c + 1) * NB])
        in_maps.append(m)
    res = run_bass_kernel_spmd(nc, in_maps, core_ids=list(range(NCORES)))
    return np.concatenate([r["xo"] for r in res.results], axis=0)


def build_full():
    I = 2816
    KI = I // 128
    KD = DM // 128
    nc = bass.Bass("TRN2", target_bir_lowering=False)
    xin = nc.dram_tensor("x", [NB, T, DM], F32, kind="ExternalInput").ap()
    xout = nc.dram_tensor("xo", [NB, T, DM], F32, kind="ExternalOutput").ap()
    x1 = nc.dram_tensor("x1", [NB, T, DM], F32, kind="Internal").ap()
    x2 = nc.dram_tensor("x2", [NB, T, DM], F32, kind="Internal").ap()
    x3 = nc.dram_tensor("x3", [NB, T, DM], F32, kind="Internal").ap()
    WL0 = _l0_declare(nc)
    WL1 = _l1_declare(nc)
    ffw = {}
    for t_ in ("0", "1"):
        ffw[t_] = (nc.dram_tensor(f"wg{t_}", [128, KI * DM], F32, kind="ExternalInput").ap(),
                   nc.dram_tensor(f"wu{t_}", [128, KI * DM], F32, kind="ExternalInput").ap(),
                   nc.dram_tensor(f"wd{t_}", [128, KD * I], F32, kind="ExternalInput").ap())
    with ExitStack() as es:
        pr = Pr(nc, es)
        load_consts(pr, nc, sel=True, tri=True, identf=True)
        with nc.named_scope("L0"):
            emit_l0(nc, pr, xin, x1, WL0, "A_")
        with nc.named_scope("FFN0"):
            emit_ffn(nc, pr, x1, ffw["0"][0], ffw["0"][1], ffw["0"][2], x2, "C_")
        with nc.named_scope("L1"):
            emit_l1(nc, pr, x2, x3, WL1, "B_")
        with nc.named_scope("FFN1"):
            emit_ffn(nc, pr, x3, ffw["1"][0], ffw["1"][1], ffw["1"][2], xout, "D_")
        pr.finish()
    return nc


def kernel(**inputs):
    nc = _get_prog("full")
    consts = host_consts()
    m0 = _l0_weight_maps(inputs)
    m1 = _l1_weight_maps(inputs)
    base = {"c_ident": consts["c_ident"], "c_sel": consts["c_sel"],
            "c_tri": consts["c_tri"]}
    base.update(m0)
    base.update(m1)
    for t_, pre in (("0", "L0_ffn"), ("1", "L1_ffn")):
        base[f"wg{t_}"] = _tile_rows(inputs[f"{pre}_w_gate"])
        base[f"wu{t_}"] = _tile_rows(inputs[f"{pre}_w_up"])
        base[f"wd{t_}"] = _tile_rows(inputs[f"{pre}_w_down"])
    x = np.ascontiguousarray(inputs["L0_attn_x"], dtype=np.float32)
    in_maps = []
    for c in range(NCORES):
        m = dict(base)
        m["x"] = np.ascontiguousarray(x[c * NB:(c + 1) * NB])
        in_maps.append(m)
    res = run_bass_kernel_spmd(nc, in_maps, core_ids=list(range(NCORES)))
    return np.concatenate([r["xo"] for r in res.results], axis=0)


def _kernel_unfused(**inputs):
    x = np.ascontiguousarray(inputs["L0_attn_x"], dtype=np.float32)
    x = _run_l0(x, inputs)
    x = _run_ffn(x, inputs["L0_ffn_w_gate"], inputs["L0_ffn_w_up"], inputs["L0_ffn_w_down"])
    x = _run_l1(x, inputs)
    x = _run_ffn(x, inputs["L1_ffn_w_gate"], inputs["L1_ffn_w_up"], inputs["L1_ffn_w_down"])
    return x


# version 17
# speedup vs baseline: 1.2447x; 1.0169x over previous
import numpy as np
import ml_dtypes
from contextlib import ExitStack

import concourse.bass as bass
import concourse.mybir as mybir
from concourse.bass_utils import run_bass_kernel_spmd

F32 = mybir.dt.float32
BF16 = mybir.dt.bfloat16
U8 = mybir.dt.uint8
U16 = mybir.dt.uint16
U32 = mybir.dt.uint32
AF = mybir.ActivationFunctionType
ALU = mybir.AluOpType
AX = mybir.AxisListType
BFNP = ml_dtypes.bfloat16

T = 2048
DM = 1024
NB = 2
NCORES = 8
NTT = T // 128
EPS = 1e-5
MAGIC = 12582912.0


class Pr:
    ENGS = ("v", "s", "t", "g", "y")

    def __init__(self, nc, es):
        self.nc = nc
        self.es = es
        self.sems = {e: es.enter_context(nc.semaphore("sem_" + e)) for e in self.ENGS}
        self.dsem = es.enter_context(nc.semaphore("sem_dma"))
        self.lzsem = es.enter_context(nc.semaphore("sem_lzdma"))
        self.cnt = {e: 0 for e in self.ENGS}
        self.dcnt = 0
        self.pend_dma = 0
        self.lzcnt = 0
        self.need_lz = 0
        self.lz_seen = {e: 0 for e in self.ENGS}
        self.seen = {e: {o: 0 for o in self.ENGS} for e in self.ENGS}
        self.dseen = {e: 0 for e in self.ENGS}
        self.blk = es.enter_context(nc.Block())
        self.q = {e: [] for e in self.ENGS}
        self._names = 0

    def v(self, fn): self.q["v"].append(fn)
    def s(self, fn): self.q["s"].append(fn)
    def t(self, fn): self.q["t"].append(fn)
    def g(self, fn): self.q["g"].append(fn)
    def y(self, fn): self.q["y"].append(fn)

    def dma(self, out, in_, queue="y", lazy=False, **kw):
        if lazy:
            lzsem = self.lzsem
            def f(eng):
                eng.dma_start(out=out, in_=in_, **kw).then_inc(lzsem, 16)
                return None
            self.q[queue].append(f)
            self.lzcnt += 1
            return self.lzcnt
        dsem = self.dsem
        def f(eng):
            eng.dma_start(out=out, in_=in_, **kw).then_inc(dsem, 16)
            return None
        self.q[queue].append(f)
        self.pend_dma += 1
        return None

    def need(self, h):
        if h:
            self.need_lz = max(self.need_lz, h)

    def sbuf(self, shape, dtype=F32, name=None):
        if name is None:
            self._names += 1
            name = f"sb{self._names}"
        return self.es.enter_context(self.nc.sbuf_tensor(name, shape, dtype))

    def psum(self, shape, dtype=F32, name=None):
        if name is None:
            self._names += 1
            name = f"ps{self._names}"
        return self.es.enter_context(self.nc.psum_tensor(name, shape, dtype))

    def phase(self):
        waits = {}
        for e in self.ENGS:
            if self.q[e]:
                w = []
                for oe in self.ENGS:
                    c = self.cnt[oe]
                    if c > 0 and (oe == e or c > self.seen[e][oe]):
                        if c > self.seen[e][oe]:
                            w.append((self.sems[oe], c))
                            self.seen[e][oe] = c
                waits[e] = w
        dwait = self.dcnt
        for e in self.ENGS:
            ops = self.q[e]
            if not ops:
                continue
            sem = self.sems[e]
            mywaits = waits[e]
            dsem = self.dsem
            dw_eff = dwait if dwait > self.dseen[e] else 0
            if dw_eff:
                self.dseen[e] = dwait
            lzw = self.need_lz if self.need_lz > self.lz_seen[e] else 0
            if lzw:
                self.lz_seen[e] = lzw
            lzsem = self.lzsem
            def body(eng, ops=ops, mywaits=mywaits, dwait=dw_eff, sem=sem, dsem=dsem,
                     lzw=lzw, lzsem=lzsem):
                for s_, c_ in mywaits:
                    eng.wait_ge(s_, c_)
                if dwait:
                    eng.wait_ge(dsem, 16 * dwait)
                if lzw:
                    eng.wait_ge(lzsem, 16 * lzw)
                last = None
                for fn in ops:
                    r = fn(eng)
                    if r is not None:
                        last = r
                if last is not None:
                    last.then_inc(sem, 1)
                else:
                    eng.nop().then_inc(sem, 1)
            getattr(self.blk, {"v": "vector", "s": "scalar", "t": "tensor",
                               "g": "gpsimd", "y": "sync"}[e])(body)
            self.cnt[e] += 1
            self.seen[e][e] = self.cnt[e]
            self.q[e] = []
        self.dcnt += self.pend_dma
        self.pend_dma = 0

    def finish(self):
        self.phase()
        dw = self.dcnt
        lw = self.lzcnt
        waits = [(self.sems[e], self.cnt[e]) for e in self.ENGS if self.cnt[e] > 0]
        dsem = self.dsem
        lzsem = self.lzsem
        def body(eng):
            for s_, c_ in waits:
                eng.wait_ge(s_, c_)
            eng.wait_ge(dsem, 16 * dw)
            if lw:
                eng.wait_ge(lzsem, 16 * lw)
        self.blk.sync(body)


def load_consts(pr, nc, sel=False, tri=False, ones=False, identf=False, self32=False):
    cid = nc.dram_tensor("c_ident", [128, 128], F32, kind="ExternalInput").ap()
    pr.ident16 = pr.sbuf([128, 128], BF16, "identb")
    pr.dma(pr.ident16[:], cid, queue="g")
    if identf:
        pr.identf = pr.sbuf([128, 128], F32, "identf")
        pr.dma(pr.identf[:], cid)
    if sel or self32:
        csel = nc.dram_tensor("c_sel", [32, 32 * 128], F32, kind="ExternalInput").ap()
        pr.csel = csel
        if sel:
            pass
        if self32:
            pr.self32 = pr.sbuf([32, 32 * 128], F32, "self32")
            pr.dma(pr.self32[:], csel)
    if tri:
        ctri = nc.dram_tensor("c_tri", [128, 128], F32, kind="ExternalInput").ap()
        pr.tri16 = pr.sbuf([128, 128], BF16, "trib")
        pr.dma(pr.tri16[:], ctri, queue="g")
    if ones:
        pr.ones1f = pr.sbuf([1, 512], F32, "ones1f")
        def vf(vector):
            return vector.memset(pr.ones1f[:], 1.0)
        pr.v(vf)
    pr.phase()


def host_consts():
    ident = np.eye(128, dtype=np.float32)
    sel = np.zeros((32, 32 * 128), dtype=np.float32)
    for j in range(32):
        sel[j, j * 128:(j + 1) * 128] = 1.0
    tri = np.triu(np.ones((128, 128), dtype=np.float32))
    return {"c_ident": ident, "c_sel": sel, "c_tri": tri}


def emit_ln(pr, xin, xo, ntiles, D, stat, gamma_rep=None, beta_rep=None, xo2=None):
    sq = pr.scr_sq

    def s1(scalar):
        r = None
        for i in range(ntiles):
            r = scalar.activation(sq[:, 0:D], xin[:, i * D:(i + 1) * D], AF.Square,
                                  accum_out=stat[:, i * 8 + 1:i * 8 + 2])
        return r
    def v1(vector):
        r = None
        for i in range(ntiles):
            r = vector.reduce_sum(stat[:, i * 8:i * 8 + 1], xin[:, i * D:(i + 1) * D], axis=AX.X)
        return r
    pr.s(s1); pr.v(v1); pr.phase()

    def v2(vector):
        r = None
        for i in range(ntiles):
            c = i * 8
            vector.tensor_scalar(stat[:, c + 2:c + 3], stat[:, c:c + 1], 1.0 / D, None, ALU.mult)
            r = vector.tensor_scalar(stat[:, c + 4:c + 5], stat[:, c + 1:c + 2],
                                     1.0 / D, EPS, ALU.mult, ALU.add)
        return r
    pr.v(v2); pr.phase()

    def v3(vector):
        r = None
        for i in range(ntiles):
            c = i * 8
            vector.scalar_tensor_tensor(stat[:, c + 5:c + 6], stat[:, c + 2:c + 3],
                                        stat[:, c + 2:c + 3], stat[:, c + 4:c + 5],
                                        ALU.mult, ALU.subtract)
            r = vector.tensor_scalar(stat[:, c + 7:c + 8], stat[:, c + 2:c + 3],
                                     -1.0, None, ALU.mult)
        return r
    pr.v(v3); pr.phase()

    def s2(scalar):
        r = None
        for i in range(ntiles):
            c = i * 8
            r = scalar.activation(stat[:, c + 3:c + 4], stat[:, c + 5:c + 6], AF.Ln, scale=-1.0)
        return r
    pr.s(s2); pr.phase()

    def s3(scalar):
        r = None
        for i in range(ntiles):
            c = i * 8
            r = scalar.activation(stat[:, c + 6:c + 7], stat[:, c + 3:c + 4], AF.Exp, scale=-0.5)
        return r
    pr.s(s3); pr.phase()

    def v4(vector):
        r = None
        for i in range(ntiles):
            c = i * 8
            r = vector.tensor_scalar(xo[:, i * D:(i + 1) * D], xin[:, i * D:(i + 1) * D],
                                     stat[:, c + 7:c + 8], stat[:, c + 6:c + 7], ALU.add, ALU.mult)
            if xo2 is not None:
                r = vector.tensor_copy(xo2[:, i * D:(i + 1) * D], xo[:, i * D:(i + 1) * D])
        return r
    pr.v(v4); pr.phase()

    if gamma_rep is not None:
        def v5(vector):
            r = None
            for i in range(ntiles):
                sl = slice(i * D, (i + 1) * D)
                vector.tensor_tensor(xo[:, sl], xo[:, sl], gamma_rep[:, 0:D], ALU.mult)
                r = vector.tensor_tensor(xo[:, sl], xo[:, sl], beta_rep[:, 0:D], ALU.add)
            return r
        pr.v(v5); pr.phase()


def emit_transpose(pr, pst, blocks):
    group = pst.shape[1] // 128
    ident = pr.ident16[:]
    for i0 in range(0, len(blocks), group):
        n = min(group, len(blocks) - i0)
        def tf(tensor, i0=i0, n=n):
            r = None
            for j in range(n):
                st, sc, _, _ = blocks[i0 + j]
                r = tensor.transpose(pst[:, j * 128:(j + 1) * 128], st[:, sc:sc + 128], ident)
            return r
        pr.t(tf); pr.phase()
        def vf(vector, i0=i0, n=n):
            r = None
            for j in range(n):
                _, _, dt_, dc = blocks[i0 + j]
                r = vector.tensor_copy(dt_[:, dc:dc + 128], pst[:, j * 128:(j + 1) * 128])
            return r
        pr.v(vf); pr.phase()


def _emit_quant_single(pr, job, scr):
    x, F, out = job
    G = F // 16
    bm, rs = scr["bm"][0], scr["rs"][0]
    def p1(vector):
        xg = x[:, 0:F].rearrange("p (g o) -> p g o", o=16)
        return vector.reduce_max(bm[:, 0:G], xg, axis=AX.X, apply_absolute_value=True)
    pr.v(p1); pr.phase()
    pr.v(lambda vector: vector.tensor_scalar(bm[:, 0:G], bm[:, 0:G], 1.0 / 6.0, 1e-12,
                                             ALU.mult, ALU.max))
    pr.phase()
    pr.v(lambda vector: vector.reciprocal(rs[:, 0:G], bm[:, 0:G]))
    pr.phase()
    pr.v(lambda vector: vector.tensor_tensor(scr["q1"][0][:, 0:G], bm[:, 0:G],
                                             rs[:, 0:G], ALU.mult))
    pr.phase()
    pr.v(lambda vector: vector.tensor_scalar(scr["q1"][0][:, 0:G], scr["q1"][0][:, 0:G],
                                             -1.0, 2.0, ALU.mult, ALU.add))
    pr.phase()
    pr.v(lambda vector: vector.tensor_tensor(rs[:, 0:G], rs[:, 0:G],
                                             scr["q1"][0][:, 0:G], ALU.mult))
    pr.phase()
    def p2(vector):
        u, m = scr["u"][0], scr["m"][0]
        q1, qh, mg = scr["q1"][0], scr["qh"][0], scr["mg"][0]
        rsb = rs[:, 0:G].unsqueeze(2).broadcast_to([128, G, 16])
        smb = bm[:, 0:G].unsqueeze(2).broadcast_to([128, G, 16])
        xv = x[:, 0:F].rearrange("p (g o) -> p g o", o=16)
        vector.tensor_copy(u[:, 0:F], x[:, 0:F])
        vector.tensor_single_scalar(m[:, 0:F].bitcast(U16), u[:, 0:F].bitcast(U16),
                                    0x8000, ALU.bitwise_and)
        mgv = mg[:, 0:F].rearrange("p (g o) -> p g o", o=16)
        vector.tensor_tensor(mgv, xv, rsb, ALU.mult)
        vector.tensor_single_scalar(mg[:, 0:F].bitcast(U32), mg[:, 0:F].bitcast(U32),
                                    0x7FFFFFFF, ALU.bitwise_and)
        vector.tensor_scalar(q1[:, 0:F], mg[:, 0:F], 2.0, MAGIC, ALU.mult, ALU.add)
        vector.tensor_scalar(q1[:, 0:F], q1[:, 0:F], MAGIC, 0.5, ALU.subtract, ALU.mult)
        vector.tensor_scalar(qh[:, 0:F], mg[:, 0:F], MAGIC, None, ALU.add)
        vector.tensor_scalar(qh[:, 0:F], qh[:, 0:F], MAGIC, 4.0, ALU.subtract, ALU.min)
        vector.tensor_single_scalar(u[:, 0:F], mg[:, 0:F], 5.0, ALU.is_ge)
        vector.scalar_tensor_tensor(qh[:, 0:F], u[:, 0:F], 2.0, qh[:, 0:F],
                                    ALU.mult, ALU.add)
        vector.tensor_single_scalar(u[:, 0:F], mg[:, 0:F], 1.75, ALU.is_lt)
        vector.tensor_tensor(q1[:, 0:F], q1[:, 0:F], qh[:, 0:F], ALU.subtract)
        vector.tensor_tensor(q1[:, 0:F], q1[:, 0:F], u[:, 0:F], ALU.mult)
        vector.tensor_tensor(q1[:, 0:F], q1[:, 0:F], qh[:, 0:F], ALU.add)
        vector.tensor_copy(u[:, 0:F], q1[:, 0:F])
        vector.tensor_tensor(u[:, 0:F].bitcast(U16), u[:, 0:F].bitcast(U16),
                             m[:, 0:F].bitcast(U16), ALU.bitwise_or)
        uv2 = u[:, 0:F].rearrange("p (g o) -> p g o", o=16)
        ov = out[:, 0:F].rearrange("p (g o) -> p g o", o=16)
        return vector.tensor_tensor(ov, uv2, smb, ALU.mult)
    pr.v(p2); pr.phase()


def emit_quant(pr, jobs, scr):
    npool = len(scr["u"])
    for g0 in range(0, len(jobs), npool):
        grp = jobs[g0:g0 + npool]
        if len(grp) == 1:
            _emit_quant_single(pr, grp[0], scr)
            continue

        def vq(vector):
            for j, (x, F, out) in enumerate(grp):
                G = F // 16
                xg = x[:, 0:F].rearrange("p (g o) -> p g o", o=16)
                vector.reduce_max(scr["bm"][j][:, 0:G], xg, axis=AX.X,
                                  apply_absolute_value=True)
            for j, (x, F, out) in enumerate(grp):
                G = F // 16
                vector.tensor_scalar(scr["bm"][j][:, 0:G], scr["bm"][j][:, 0:G],
                                     1.0 / 6.0, 1e-12, ALU.mult, ALU.max)
            for j, (x, F, out) in enumerate(grp):
                G = F // 16
                vector.reciprocal(scr["rs"][j][:, 0:G], scr["bm"][j][:, 0:G])
            for j, (x, F, out) in enumerate(grp):
                G = F // 16
                vector.tensor_tensor(scr["q1"][j][:, 0:G], scr["bm"][j][:, 0:G],
                                     scr["rs"][j][:, 0:G], ALU.mult)
            for j, (x, F, out) in enumerate(grp):
                G = F // 16
                vector.tensor_scalar(scr["q1"][j][:, 0:G], scr["q1"][j][:, 0:G],
                                     -1.0, 2.0, ALU.mult, ALU.add)
            for j, (x, F, out) in enumerate(grp):
                G = F // 16
                vector.tensor_tensor(scr["rs"][j][:, 0:G], scr["rs"][j][:, 0:G],
                                     scr["q1"][j][:, 0:G], ALU.mult)
            r = None
            for j, (x, F, out) in enumerate(grp):
                G = F // 16
                bm, rs = scr["bm"][j], scr["rs"][j]
                u, m = scr["u"][j], scr["m"][j]
                q1, qh, mg = scr["q1"][j], scr["qh"][j], scr["mg"][j]
                rsb = rs[:, 0:G].unsqueeze(2).broadcast_to([128, G, 16])
                smb = bm[:, 0:G].unsqueeze(2).broadcast_to([128, G, 16])
                xv = x[:, 0:F].rearrange("p (g o) -> p g o", o=16)
                vector.tensor_copy(u[:, 0:F], x[:, 0:F])
                vector.tensor_single_scalar(m[:, 0:F].bitcast(U16), u[:, 0:F].bitcast(U16),
                                            0x8000, ALU.bitwise_and)
                mgv = mg[:, 0:F].rearrange("p (g o) -> p g o", o=16)
                vector.tensor_tensor(mgv, xv, rsb, ALU.mult)
                vector.tensor_single_scalar(mg[:, 0:F].bitcast(U32), mg[:, 0:F].bitcast(U32),
                                            0x7FFFFFFF, ALU.bitwise_and)
                vector.tensor_scalar(q1[:, 0:F], mg[:, 0:F], 2.0, MAGIC, ALU.mult, ALU.add)
                vector.tensor_scalar(q1[:, 0:F], q1[:, 0:F], MAGIC, 0.5, ALU.subtract, ALU.mult)
                vector.tensor_scalar(qh[:, 0:F], mg[:, 0:F], MAGIC, None, ALU.add)
                vector.tensor_scalar(qh[:, 0:F], qh[:, 0:F], MAGIC, 4.0, ALU.subtract, ALU.min)
                vector.tensor_single_scalar(u[:, 0:F], mg[:, 0:F], 5.0, ALU.is_ge)
                vector.scalar_tensor_tensor(qh[:, 0:F], u[:, 0:F], 2.0, qh[:, 0:F],
                                            ALU.mult, ALU.add)
                vector.tensor_single_scalar(u[:, 0:F], mg[:, 0:F], 1.75, ALU.is_lt)
                vector.tensor_tensor(q1[:, 0:F], q1[:, 0:F], qh[:, 0:F], ALU.subtract)
                vector.tensor_tensor(q1[:, 0:F], q1[:, 0:F], u[:, 0:F], ALU.mult)
                vector.tensor_tensor(q1[:, 0:F], q1[:, 0:F], qh[:, 0:F], ALU.add)
                vector.tensor_copy(u[:, 0:F], q1[:, 0:F])
                vector.tensor_tensor(u[:, 0:F].bitcast(U16), u[:, 0:F].bitcast(U16),
                                     m[:, 0:F].bitcast(U16), ALU.bitwise_or)
                uv2 = u[:, 0:F].rearrange("p (g o) -> p g o", o=16)
                ov = out[:, 0:F].rearrange("p (g o) -> p g o", o=16)
                r = vector.tensor_tensor(ov, uv2, smb, ALU.mult)
            return r
        pr.v(vq); pr.phase()


def mk_quant_scr(pr, F, n=1, es=None):
    def alloc(shape, dt_):
        pr._names += 1
        name = f"qscr{pr._names}"
        cm = pr.nc.sbuf_tensor(name, shape, dt_)
        return (es or pr.es).enter_context(cm)
    return {
        "bm": [alloc([128, F // 16], F32) for _ in range(n)],
        "rs": [alloc([128, F // 16], F32) for _ in range(n)],
        "q1": [alloc([128, F], F32) for _ in range(n)],
        "qh": [alloc([128, F], F32) for _ in range(n)],
        "mg": [alloc([128, F], F32) for _ in range(n)],
        "u": [alloc([128, F], BF16) for _ in range(n)],
        "m": [alloc([128, F], BF16) for _ in range(n)],
    }


def emit_silu(pr, tiles, scr_e):
    npool = len(scr_e)
    for g0 in range(0, len(tiles), npool):
        grp = tiles[g0:g0 + npool]
        def s1(scalar):
            r = None
            for j, (x, F) in enumerate(grp):
                r = scalar.activation(scr_e[j][:, 0:F], x[:, 0:F], AF.Exp, scale=-1.0)
            return r
        pr.s(s1); pr.phase()
        def v1(vector):
            r = None
            for j, (x, F) in enumerate(grp):
                e = scr_e[j]
                vector.tensor_scalar(e[:, 0:F], e[:, 0:F], 1.0, None, ALU.add)
                vector.reciprocal(e[:, 0:F], e[:, 0:F])
                r = vector.tensor_tensor(x[:, 0:F], x[:, 0:F], e[:, 0:F], ALU.mult)
            return r
        pr.v(v1); pr.phase()


DBG_FFN = False


def build_ffn():
    I = 2816
    KI = I // 128
    KD = DM // 128
    nc = bass.Bass("TRN2", target_bir_lowering=False)
    xin = nc.dram_tensor("x", [NB, T, DM], F32, kind="ExternalInput").ap()
    wg = nc.dram_tensor("wg", [128, KI * DM], F32, kind="ExternalInput").ap()
    wu = nc.dram_tensor("wu", [128, KI * DM], F32, kind="ExternalInput").ap()
    wd = nc.dram_tensor("wd", [128, KD * I], F32, kind="ExternalInput").ap()
    xout = nc.dram_tensor("xo", [NB, T, DM], F32, kind="ExternalOutput").ap()
    with ExitStack() as es:
        pr = Pr(nc, es)
        load_consts(pr, nc)
        emit_ffn(nc, pr, xin, wg, wu, wd, xout, "F_")
        pr.finish()
    return nc


def emit_ffn(nc, pr, xin, wg, wu, wd, xout, pf):
    I = 2816
    KI = I // 128
    KD = DM // 128
    with ExitStack() as es:
        scr = mk_quant_scr(pr, 512, n=3, es=es)
        pr.scr_sq = scr["q1"][0]

        swgT = es.enter_context(nc.sbuf_tensor(pf + "swgT", [128, KD * I], BF16))
        swuT = es.enter_context(nc.sbuf_tensor(pf + "swuT", [128, KD * I], BF16))
        swdT = es.enter_context(nc.sbuf_tensor(pf + "swdT", [128, KI * DM], BF16))
        pst = es.enter_context(nc.psum_tensor(pf + "pst", [128, 2048], BF16))
        psA = es.enter_context(nc.psum_tensor(pf + "psA", [128, 1024], F32))
        psA2 = es.enter_context(nc.psum_tensor(pf + "psA2", [128, 1024], F32))
        psD = es.enter_context(nc.psum_tensor(pf + "psD", [128, 1024], F32))
        xnT = es.enter_context(nc.sbuf_tensor(pf + "xnT", [128, KD * T], BF16))

        _pc = [0]

        def prep(wdram, n_rowtiles, fcols, dstT, slice_tiles):
            for s0 in range(0, n_rowtiles, slice_tiles):
                ns = min(slice_tiles, n_rowtiles - s0)
                _pc[0] += 1
                with nc.sbuf_tensor(f"{pf}wstg{_pc[0]}", [128, ns * fcols], F32) as stg, \
                     nc.sbuf_tensor(f"{pf}wstq{_pc[0]}", [128, ns * fcols], BF16) as stq:
                    pr.dma(stg[:], wdram[:, s0 * fcols:(s0 + ns) * fcols])
                    pr.phase()
                    qw = 512 if fcols % 512 == 0 else 352
                    qjobs = []
                    for j in range(ns):
                        for c in range(0, fcols, qw):
                            qjobs.append((stg[:, j * fcols + c:j * fcols + c + qw], qw,
                                          stq[:, j * fcols + c:j * fcols + c + qw]))
                    emit_quant(pr, qjobs, scr)
                    blocks = []
                    for j in range(ns):
                        kt = s0 + j
                        for fc in range(fcols // 128):
                            blocks.append((stq, j * fcols + fc * 128,
                                           dstT, fc * (n_rowtiles * 128) + kt * 128))
                    emit_transpose(pr, pst, blocks)

        with nc.named_scope("ffn_wprep"):
            prep(wg, KI, DM, swgT, 3)
            prep(wu, KI, DM, swuT, 3)
            prep(wd, KD, I, swdT, 1)

        stat = es.enter_context(nc.sbuf_tensor(pf + "stat", [128, 8], F32))
        g32 = es.enter_context(nc.sbuf_tensor(pf + "g32", [128, DM], F32))
        u32 = es.enter_context(nc.sbuf_tensor(pf + "u32", [128, DM], F32))
        o_t = u32
        x_t = g32
        pr.scr_sq = u32

        for b in range(NB):
            QT = 1
            for h0 in range(0, NTT, QT):
                with nc.sbuf_tensor(f"{pf}xs{b}_{h0}", [128, QT * DM], F32) as xs, \
                     nc.sbuf_tensor(f"{pf}xq{b}_{h0}", [128, QT * DM], BF16) as xq:
                    for i in range(QT):
                        tt = h0 + i
                        pr.dma(xs[:, i * DM:(i + 1) * DM], xin[b, tt * 128:(tt + 1) * 128, :])
                    pr.phase()
                    emit_ln(pr, xs, xs, QT, DM, stat)
                    emit_quant(pr, [(xs[:, i * DM + c:i * DM + c + 512], 512,
                                     xq[:, i * DM + c:i * DM + c + 512])
                                    for i in range(QT) for c in (0, 512)], scr)
                    emit_transpose(pr, pst,
                                   [(xq, i * DM + kd * 128, xnT, kd * T + (h0 + i) * 128)
                                    for i in range(QT) for kd in range(KD)])


            with nc.sbuf_tensor(f"{pf}h_t{b}", [128, I], BF16) as h_t:
              hT_t = h_t
              for tt in range(NTT):
                  slices = []
                  for c0 in range(0, I, DM):
                      cw = min(DM, I - c0)
                      for i0 in range(c0, c0 + cw, 512):
                          slices.append((c0, i0, min(512, c0 + cw - i0)))
                  nsl = len(slices)

                  def emit_mm(si):
                      c0, i0, iw = slices[si]
                      ps = psA if si % 2 == 0 else psA2
                      def tf(tensor, i0=i0, iw=iw, ps=ps, tt=tt):
                          r = None
                          for half, w in ((0, swgT), (1, swuT)):
                              for kd in range(KD):
                                  r = tensor.matmul(
                                      ps[:, half * 512:half * 512 + iw],
                                      xnT[:, kd * T + tt * 128:kd * T + tt * 128 + 128],
                                      w[:, kd * I + i0:kd * I + i0 + iw],
                                      start=(kd == 0), stop=(kd == KD - 1))
                          return r
                      pr.t(tf)

                  def emit_evac(si):
                      c0, i0, iw = slices[si]
                      ps = psA if si % 2 == 0 else psA2
                      def vf(vector, c0=c0, i0=i0, iw=iw, ps=ps):
                          c = i0 - c0
                          vector.tensor_copy(g32[:, c:c + iw], ps[:, 0:iw])
                          return vector.tensor_copy(u32[:, c:c + iw], ps[:, 512:512 + iw])
                      pr.v(vf)

                  def emit_silu(c0, cw):
                      pr.s(lambda scalar, c0=c0, cw=cw: scalar.activation(
                          g32[:, 0:cw], g32[:, 0:cw], AF.Silu))

                  def emit_mult(c0, cw):
                      pr.v(lambda vector, c0=c0, cw=cw: vector.tensor_tensor(
                          g32[:, 0:cw], g32[:, 0:cw], u32[:, 0:cw], ALU.mult))

                  def emit_q(c0, cw):
                      qjobs = []
                      for q0 in range(0, cw, 512):
                          qw = min(512, cw - q0)
                          qjobs.append((g32[:, q0:q0 + qw], qw,
                                        h_t[:, c0 + q0:c0 + q0 + qw]))
                      emit_quant(pr, qjobs, scr)

                  def emit_tr(j0, nblk):
                      def tfr(tensor, j0=j0, nblk=nblk):
                          r = None
                          for j in range(nblk):
                              r = tensor.transpose(pst[:, j * 128:(j + 1) * 128],
                                                   h_t[:, (j0 + j) * 128:(j0 + j + 1) * 128],
                                                   pr.ident16[:])
                          return r
                      pr.t(tfr)

                  def emit_tre(j0, nblk):
                      def vfr(vector, j0=j0, nblk=nblk):
                          r = None
                          for j in range(nblk):
                              r = vector.tensor_copy(
                                  h_t[:, (j0 + j) * 128:(j0 + j + 1) * 128],
                                  pst[:, j * 128:(j + 1) * 128])
                          return r
                      pr.v(vfr)

                  emit_mm(0); pr.phase()
                  emit_mm(1); emit_evac(0); pr.phase()
                  emit_mm(2); emit_evac(1); pr.phase()
                  emit_mm(3); emit_silu(0, DM); pr.phase()
                  emit_mult(0, DM); pr.phase()
                  emit_q(0, DM)
                  emit_evac(2); pr.phase()
                  emit_mm(4); emit_evac(3); pr.phase()
                  emit_mm(5); emit_silu(DM, DM); pr.phase()
                  emit_mult(DM, DM); pr.phase()
                  emit_q(DM, DM)
                  emit_evac(4); emit_tr(0, 16); pr.phase()
                  emit_evac(5); emit_tre(0, 16); pr.phase()
                  emit_silu(2 * DM, I - 2 * DM); pr.phase()
                  emit_mult(2 * DM, I - 2 * DM); pr.phase()
                  emit_q(2 * DM, I - 2 * DM)
                  emit_tr(16, 6); pr.phase()
                  emit_tre(16, 6); pr.phase()

                  def tf2(tensor, tt=tt):
                      r = None
                      for c0 in range(0, DM, 512):
                          for ki in range(KI):
                              r = tensor.matmul(psD[:, c0:c0 + 512],
                                                hT_t[:, ki * 128:(ki + 1) * 128],
                                                swdT[:, ki * DM + c0:ki * DM + c0 + 512],
                                                start=(ki == 0), stop=(ki == KI - 1))
                      return r
                  pr.t(tf2)
                  pr.dma(x_t[:, 0:DM], xin[b, tt * 128:(tt + 1) * 128, :])
                  pr.phase()

                  def vf2(vector):
                      return vector.tensor_tensor(o_t[:], psD[:, 0:DM], x_t[:, 0:DM], ALU.add)
                  pr.v(vf2); pr.phase()
                  pr.dma(xout[b, tt * 128:(tt + 1) * 128, :], o_t[:])
            pr.phase()
        pr.phase()


def build_l0_attn():
    nc = bass.Bass("TRN2", target_bir_lowering=False)
    xin = nc.dram_tensor("x", [NB, T, DM], F32, kind="ExternalInput").ap()
    xout = nc.dram_tensor("xo", [NB, T, DM], F32, kind="ExternalOutput").ap()
    W = _l0_declare(nc)
    with ExitStack() as es:
        pr = Pr(nc, es)
        load_consts(pr, nc, tri=True, identf=True, self32=True)
        emit_l0(nc, pr, xin, xout, W, "A_")
        pr.finish()
    return nc


def _l0_declare(nc, pf=""):
    names = [("inprojT", [128, 8 * 4384]), ("convw", [128, 18 * 4]),
             ("convbi", [128, 18]), ("dtbias", [32, 1]), ("alogi", [32, 1]),
             ("dvec_rep", [128, 2048]), ("normw_rep", [128, 2048]),
             ("lnw_rep", [128, 1024]), ("lnb_rep", [128, 1024]),
             ("outprojT", [128, 16 * 1024])]
    return {n: nc.dram_tensor(pf + n, sh, F32, kind="ExternalInput").ap()
            for n, sh in names}


L0SER = {"a": 0, "b": 0, "c": 0, "d": 0, "e": 0, "f": 1}


def emit_l0(nc, pr, xin, xout, W, pf):
    def SP(g="a"):
        if L0SER[g]:
            pr.phase()
    H = 32
    Q = 128
    NCH = T // Q
    inprojT = W["inprojT"]; convw = W["convw"]; convbi = W["convbi"]
    dtbias = W["dtbias"]; alogi = W["alogi"]; dvec_rep = W["dvec_rep"]
    normw_rep = W["normw_rep"]; lnw_rep = W["lnw_rep"]; lnb_rep = W["lnb_rep"]
    outprojT = W["outprojT"]
    z0_sp = nc.dram_tensor(pf + "z0_sp", [128, 16 * T], BF16, kind="Internal").ap()
    xbc_sp = nc.dram_tensor(pf + "xbc_sp", [128, 16 * T], BF16, kind="Internal").ap()
    y0_sp = nc.dram_tensor(pf + "y0_sp", [128, 16 * DM], BF16, kind="Internal").ap()
    dk_sp = nc.dram_tensor(pf + "dk_sp", [32, 16], F32, kind="Internal").ap()
    cselap = pr.csel
    if True:
        es = ExitStack()

        stat = es.enter_context(nc.sbuf_tensor(pf + "stat", [128, 8], F32))
        scr_sq = es.enter_context(nc.sbuf_tensor(pf + "scr_sq", [128, DM], F32))
        pr.scr_sq = scr_sq
        onesf = es.enter_context(nc.sbuf_tensor(pf + "onesf", [1, 128], F32))
        cw_sb = es.enter_context(nc.sbuf_tensor(pf + "cw_sb", [128, 18 * 4], F32))
        cb_sb = es.enter_context(nc.sbuf_tensor(pf + "cb_sb", [128, 18], F32))
        dtb_sb = es.enter_context(nc.sbuf_tensor(pf + "dtb_sb", [32, 1], F32))
        an_sb = es.enter_context(nc.sbuf_tensor(pf + "an_sb", [32, 1], F32))
        drep = es.enter_context(nc.sbuf_tensor(pf + "drep", [128, 2048], BF16))
        nwrep = es.enter_context(nc.sbuf_tensor(pf + "nwrep", [128, 2048], BF16))
        lwrep = es.enter_context(nc.sbuf_tensor(pf + "lwrep", [128, 1024], BF16))
        lbrep = es.enter_context(nc.sbuf_tensor(pf + "lbrep", [128, 1024], BF16))
        pr.dma(cw_sb[:], convw)
        pr.dma(cb_sb[:], convbi)
        pr.dma(dtb_sb[:], dtbias)
        pr.dma(an_sb[:], alogi)
        pr.dma(drep[:], dvec_rep, queue="g")
        pr.dma(nwrep[:], normw_rep, queue="g")
        pr.dma(lwrep[:], lnw_rep, queue="g")
        pr.dma(lbrep[:], lnb_rep, queue="g")
        pr.v(lambda vector: vector.memset(onesf[:], 1.0))
        pr.phase()
        pr.s(lambda scalar: scalar.activation(an_sb[:], an_sb[:], AF.Exp))
        pr.phase()
        pr.v(lambda vector: vector.tensor_scalar(an_sb[:], an_sb[:], -1.0, None, ALU.mult))
        pr.phase()

        BT_sb = es.enter_context(nc.sbuf_tensor(pf + "BT_sb", [128, T], BF16))
        CT_sb = es.enter_context(nc.sbuf_tensor(pf + "CT_sb", [128, T], BF16))
        dt_sb = es.enter_context(nc.sbuf_tensor(pf + "dt_sb", [32, T], F32))
        ca_sb = es.enter_context(nc.sbuf_tensor(pf + "ca_sb", [32, T], F32))
        det_sb = es.enter_context(nc.sbuf_tensor(pf + "det_sb", [32, T], F32))
        caT = es.enter_context(nc.sbuf_tensor(pf + "caT", [128, NCH * 32], F32))
        dtT = es.enter_context(nc.sbuf_tensor(pf + "dtT", [128, NCH * 32], F32))
        decE = es.enter_context(nc.sbuf_tensor(pf + "decE", [128, NCH * 32], F32))
        expcaT = es.enter_context(nc.sbuf_tensor(pf + "expcaT", [128, NCH * 32], F32))
        dchkr = es.enter_context(nc.sbuf_tensor(pf + "dchkr", [128, 512], F32))
        dchk_row = es.enter_context(nc.sbuf_tensor(pf + "dchk_row", [1, 512], F32))
        S_sb = es.enter_context(nc.sbuf_tensor(pf + "S_sb", [128, H * 64], BF16))
        ones32 = es.enter_context(nc.sbuf_tensor(pf + "ones32", [32, Q], F32))
        ci = es.enter_context(nc.sbuf_tensor(pf + "ci", [128, T + 4], BF16))
        wslc = es.enter_context(nc.sbuf_tensor(pf + "wslc", [128, 8 * 128], BF16))
        pr.v(lambda vector: vector.memset(ones32[:], 1.0))
        pr.phase()

        for b in range(NB):
            cih = [0]
            spl = [0]
            y0h = [0]
            y0hs = []
            with nc.sbuf_tensor(f"{pf}lnx{b}", [128, DM], F32) as xs, \
                 nc.sbuf_tensor(f"{pf}y0t{b}", [128, DM], F32) as y0t, \
                 nc.sbuf_tensor(f"{pf}xnb{b}", [128, DM], BF16) as xnb, \
                 nc.sbuf_tensor(f"{pf}xnT{b}", [128, 8 * T], BF16) as xnT, \
                 nc.sbuf_tensor(f"{pf}e32p{b}", [128, T], F32) as e32, \
                 nc.sbuf_tensor(f"{pf}f32p{b}", [128, T], F32) as f32b:
                with nc.psum_tensor(f"{pf}pstT{b}", [128, 2048], BF16) as pst:
                    for tt in range(NTT):
                        pr.dma(xs[:], xin[b, tt * 128:(tt + 1) * 128, :])
                        pr.phase()
                        pr.need(y0h[0])
                        emit_ln(pr, xs, y0t, 1, DM, stat)
                        y0h[0] = pr.dma(y0_sp[:, tt * DM:(tt + 1) * DM], y0t[:],
                                        queue="g", lazy=True)
                        y0hs.append(y0h[0])
                        emit_ln(pr, y0t, xnb, 1, DM, stat, gamma_rep=lwrep, beta_rep=lbrep)
                        emit_transpose(pr, pst, [(xnb, kd * 128, xnT, kd * T + tt * 128)
                                                 for kd in range(8)])

                with nc.psum_tensor(f"{pf}psA{b}", [128, 2048], F32) as psA:
                    for ot in range(35):
                        orows = 32 if ot == 34 else 128
                        wv = inprojT.rearrange("p (kt o) -> p kt o", o=4384)
                        pr.dma(wslc[:, 0:8 * orows].rearrange("p (kt o) -> p kt o", o=orows),
                               wv[:, :, ot * 128:ot * 128 + orows], queue="g")
                        pr.phase()
                        def tf(tensor, ot=ot, orows=orows):
                            r = None
                            for n0 in range(0, T, 512):
                                for kd in range(8):
                                    r = tensor.matmul(psA[0:orows, n0:n0 + 512],
                                                      wslc[:, kd * orows:(kd + 1) * orows],
                                                      xnT[:, kd * T + n0:kd * T + n0 + 512],
                                                      start=(kd == 0), stop=(kd == 7))
                            return r
                        pr.t(tf); pr.phase()
                        if ot < 16:
                            pr.need(cih[0])
                            pr.v(lambda vector: vector.tensor_copy(ci[:, 4:4 + T],
                                                                   psA[:, 0:T]))
                            pr.phase()
                            h = pr.dma(z0_sp[:, ot * T:(ot + 1) * T], ci[:, 4:4 + T],
                                       queue="g", lazy=True)
                            cih[0] = h; spl[0] = max(spl[0], h)
                            pr.phase()
                        elif ot < 34:
                            cc = ot - 16
                            pr.need(cih[0])
                            def vc0(vector, cc=cc):
                                vector.memset(ci[:, 0:4], 0.0)
                                vector.tensor_copy(ci[:, 4:4 + T], psA[:, 0:T])
                                vector.tensor_scalar(f32b[:, 0:T], ci[:, 1:1 + T],
                                                     cw_sb[:, cc * 4:cc * 4 + 1], None, ALU.mult)
                                for k in (1, 2, 3):
                                    vector.scalar_tensor_tensor(
                                        f32b[:, 0:T], ci[:, 1 + k:1 + k + T],
                                        cw_sb[:, cc * 4 + k:cc * 4 + k + 1],
                                        f32b[:, 0:T], ALU.mult, ALU.add)
                                return vector.tensor_scalar(
                                    f32b[:, 0:T], f32b[:, 0:T], cb_sb[:, cc:cc + 1],
                                    None, ALU.add)
                            pr.v(vc0); pr.phase()
                            if cc < 16:
                                dst0 = ci[:, 4:4 + T]
                            elif cc == 16:
                                dst0 = BT_sb[:, 0:T]
                            else:
                                dst0 = CT_sb[:, 0:T]
                            pr.s(lambda scalar, dst0=dst0: scalar.activation(
                                dst0, f32b[:, 0:T], AF.Silu))
                            pr.phase()
                            if cc < 16:
                                h = pr.dma(xbc_sp[:, cc * T:(cc + 1) * T], ci[:, 4:4 + T],
                                           queue="g", lazy=True)
                                cih[0] = h; spl[0] = max(spl[0], h)
                                pr.phase()
                        else:
                            def vd0(vector):
                                vector.tensor_scalar(f32b[0:32, 0:T], psA[0:32, 0:T],
                                                     dtb_sb[:, 0:1], None, ALU.add)
                                return vector.tensor_single_scalar(
                                    e32[0:32, 0:T].bitcast(U32),
                                    f32b[0:32, 0:T].bitcast(U32),
                                    0x7FFFFFFF, ALU.bitwise_and)
                            pr.v(vd0); pr.phase()
                            def sd1(scalar):
                                scalar.activation(e32[0:32, 0:T], e32[0:32, 0:T],
                                                  AF.Exp, scale=-1.0)
                                return scalar.activation(e32[0:32, 0:T], e32[0:32, 0:T],
                                                         AF.Ln, bias=1.0)
                            pr.s(sd1); pr.phase()
                            def vd2(vector):
                                return vector.scalar_tensor_tensor(
                                    dt_sb[:, 0:T], f32b[0:32, 0:T], 0.0, e32[0:32, 0:T],
                                    ALU.max, ALU.add)
                            pr.v(vd2); pr.phase()

            pr.v(lambda vector: vector.tensor_scalar(ca_sb[:, 0:T], dt_sb[:, 0:T],
                                                     an_sb[:, 0:1], None, ALU.mult))
            pr.phase()
            def vca(vector):
                r = None
                for c in range(NCH):
                    r = vector.tensor_tensor_scan(ca_sb[:, c * Q:(c + 1) * Q], ones32[:],
                                                  ca_sb[:, c * Q:(c + 1) * Q], 0.0,
                                                  ALU.mult, ALU.add)
                return r
            pr.v(vca); pr.phase()
            def vde(vector):
                r = None
                for c in range(NCH):
                    r = vector.tensor_scalar(det_sb[:, c * Q:(c + 1) * Q],
                                             ca_sb[:, c * Q:(c + 1) * Q],
                                             ca_sb[:, (c + 1) * Q - 1:(c + 1) * Q],
                                             None, ALU.subtract)
                return r
            pr.v(vde); pr.phase()
            pr.s(lambda scalar: scalar.activation(det_sb[:, 0:T], det_sb[:, 0:T],
                                                  AF.Exp, scale=-1.0))
            pr.phase()
            with nc.psum_tensor(f"{pf}pstc{b}", [128, 2048], F32) as pstc:
                for src, dst in ((ca_sb, caT), (dt_sb, dtT), (det_sb, decE)):
                    def tca(tensor, src=src):
                        r = None
                        for c in range(NCH):
                            r = tensor.transpose(pstc[:, c * 128:c * 128 + 32],
                                                 src[:, c * Q:(c + 1) * Q],
                                                 pr.identf[0:32, 0:32])
                        return r
                    pr.t(tca); pr.phase()
                    def vca2(vector, dst=dst):
                        r = None
                        for c in range(NCH):
                            r = vector.tensor_copy(dst[:, c * 32:(c + 1) * 32],
                                                   pstc[:, c * 128:c * 128 + 32])
                        return r
                    pr.v(vca2); pr.phase()
            pr.s(lambda scalar: scalar.activation(expcaT[:], caT[:], AF.Exp))
            pr.phase()
            def vdk(vector):
                return vector.tensor_copy(det_sb[:, 0:16], ca_sb[:, Q - 1::Q])
            pr.v(vdk); pr.phase()
            pr.s(lambda scalar: scalar.activation(det_sb[:, 0:16], det_sb[:, 0:16], AF.Exp))
            pr.phase()
            pr.dma(dk_sp, det_sb[:, 0:16])
            pr.phase()
            pr.dma(dchk_row[:], dk_sp.rearrange("p c -> (p c)").unsqueeze(0))
            pr.phase()
            with nc.psum_tensor(f"{pf}psk{b}", [128, 512], F32) as psk:
                pr.t(lambda tensor: tensor.matmul(psk[:], onesf[:, 0:128], dchk_row[:],
                                                  start=True, stop=True))
                pr.phase()
                pr.v(lambda vector: vector.tensor_copy(dchkr[:], psk[:]))
                pr.phase()

            pr.v(lambda vector: vector.memset(S_sb[:], 0.0))
            pr.phase()
            with ExitStack() as ces:
                op_sb = ces.enter_context(nc.sbuf_tensor(f"{pf}op{b}", [128, 16 * 1024], BF16))
                self32x = ces.enter_context(nc.sbuf_tensor(f"{pf}sel32{b}", [32, 32 * 128], F32))
                dif = ces.enter_context(nc.sbuf_tensor(f"{pf}dif{b}", [128, 2048], F32))
                mh = ces.enter_context(nc.sbuf_tensor(f"{pf}mh{b}", [128, 32 * 128], BF16))
                cbm = ces.enter_context(nc.sbuf_tensor(f"{pf}cbm{b}", [128, 128], BF16))
                bts = ces.enter_context(nc.sbuf_tensor(f"{pf}bts{b}", [128, 128], BF16))
                xdt = ces.enter_context(nc.sbuf_tensor(f"{pf}xdt{b}", [128, 2048], BF16))
                xw = ces.enter_context(nc.sbuf_tensor(f"{pf}xw{b}", [128, 2048], BF16))
                xtc = ces.enter_context(nc.sbuf_tensor(f"{pf}xtc{b}", [128, 2048], BF16))
                ygc = ces.enter_context(nc.sbuf_tensor(f"{pf}ygc{b}", [128, 2048], BF16))
                f32s = ces.enter_context(nc.sbuf_tensor(f"{pf}f32s{b}", [128, 2048], F32))
                g32s = ces.enter_context(nc.sbuf_tensor(f"{pf}g32s{b}", [128, 2048], F32))
                ygT = ces.enter_context(nc.sbuf_tensor(f"{pf}ygT{b}", [128, 2048], BF16))
                ot8 = ces.enter_context(nc.sbuf_tensor(f"{pf}ot8{b}", [128, 1024], BF16))
                psY = ces.enter_context(nc.psum_tensor(f"{pf}psY{b}", [128, 1024], F32))
                psY2 = ces.enter_context(nc.psum_tensor(f"{pf}psY2{b}", [128, 1024], F32))
                pstx = ces.enter_context(nc.psum_tensor(f"{pf}pstx{b}", [128, 1024], BF16))
                psS = ces.enter_context(nc.psum_tensor(f"{pf}psS{b}", [128, 1024], F32))
                pr.dma(op_sb[:], outprojT, queue="g")
                pr.dma(self32x[:], cselap)
                pr.phase()
                for c in range(NCH):
                    t0 = c * Q
                    xstrh = [0]

                    def tsel(r):
                        ps = psY if r % 2 == 0 else psY2
                        def tf_(tensor, r=r, ps=ps):
                            rr = None
                            for i in range(8):
                                h = r * 8 + i
                                rr = tensor.matmul(ps[:, i * 128:(i + 1) * 128],
                                                   self32x[:, h * 128:(h + 1) * 128],
                                                   ca_sb[:, t0:t0 + Q],
                                                   start=True, stop=True)
                            return rr
                        pr.t(tf_)

                    def vdif(r):
                        ps = psY if r % 2 == 0 else psY2
                        dh = (r % 2) * 1024
                        def vf_(vector, r=r, ps=ps, dh=dh, c=c):
                            rr = None
                            for i in range(8):
                                h = r * 8 + i
                                rr = vector.tensor_scalar(
                                    dif[:, dh + i * 128:dh + (i + 1) * 128],
                                    ps[:, i * 128:(i + 1) * 128],
                                    caT[:, c * 32 + h:c * 32 + h + 1],
                                    0.0, ALU.subtract, ALU.min)
                            return rr
                        pr.v(vf_)

                    def sexp(r):
                        dh = (r % 2) * 1024
                        pr.s(lambda scalar, r=r, dh=dh: scalar.activation(
                            mh[:, r * 1024:(r + 1) * 1024],
                            dif[:, dh:dh + 1024], AF.Exp))

                    tsel(0); pr.phase()
                    tsel(1); SP("a"); vdif(0); pr.phase()
                    tsel(2); SP("a"); vdif(1); SP("a"); sexp(0); pr.phase()
                    tsel(3); SP("a"); vdif(2); SP("a"); sexp(1); pr.phase()
                    pr.t(lambda tensor: tensor.matmul(psS[0:128, 0:128], BT_sb[:, t0:t0 + Q],
                                                      CT_sb[:, t0:t0 + Q],
                                                      start=True, stop=True))
                    SP("a"); vdif(3); SP("a"); sexp(2); pr.phase()
                    pr.v(lambda vector: vector.tensor_tensor(cbm[:], psS[:, 0:128],
                                                             pr.tri16[:], ALU.mult))
                    SP("a")
                    sexp(3)
                    xv2 = xbc_sp.rearrange("p (k t) -> p k t", t=T)
                    pr.need(spl[0])
                    xstrh[0] = pr.dma(xw[:].rearrange("p (k t) -> p k t", t=128),
                                      xv2[:, :, t0:t0 + Q], lazy=True)
                    pr.phase()
                    def vmh(vector):
                        return vector.tensor_tensor(
                            mh[:].rearrange("p (h s) -> p h s", s=128),
                            mh[:].rearrange("p (h s) -> p h s", s=128),
                            cbm[:].unsqueeze(1).broadcast_to([128, 32, 128]), ALU.mult)
                    pr.need(xstrh[0])
                    pr.v(vmh)
                    SP("b")
                    def txt(tensor):
                        r = None
                        for i in range(8):
                            r = tensor.transpose(pstx[:, i * 128:(i + 1) * 128],
                                                 xw[:, i * 128:(i + 1) * 128],
                                                 pr.ident16[:])
                        return r
                    pr.t(txt); pr.phase()
                    pr.v(lambda vector: vector.tensor_copy(xtc[:, 0:1024], pstx[:, 0:1024]))
                    pr.phase()
                    def txt2(tensor):
                        r = None
                        for i in range(8):
                            r = tensor.transpose(pstx[:, i * 128:(i + 1) * 128],
                                                 xw[:, (8 + i) * 128:(9 + i) * 128],
                                                 pr.ident16[:])
                        return r
                    pr.t(txt2); pr.phase()
                    pr.v(lambda vector: vector.tensor_copy(xtc[:, 1024:2048], pstx[:, 0:1024]))
                    pr.phase()
                    def vxdt(vector, c=c):
                        dtv = dtT[:, c * 32:(c + 1) * 32]
                        bc = dtv.unsqueeze(2).broadcast_to([128, 32, 64])
                        return vector.tensor_tensor(
                            xdt[:].rearrange("p (h q) -> p h q", q=64),
                            xtc[:].rearrange("p (h q) -> p h q", q=64), bc, ALU.mult)
                    pr.v(vxdt)
                    SP("c")
                    pr.t(lambda tensor: tensor.transpose(
                        pstx[:, 0:128], BT_sb[:, t0:t0 + Q], pr.ident16[:]))
                    pr.phase()

                    def tin(hh):
                        def tf_(tensor, hh=hh, t0=t0):
                            r = None
                            for i in range(16):
                                h = hh * 16 + i
                                tensor.matmul(psY[:, i * 64:(i + 1) * 64],
                                              mh[:, h * 128:(h + 1) * 128],
                                              xdt[:, h * 64:(h + 1) * 64],
                                              start=True, stop=True)
                                r = tensor.matmul(psY2[:, i * 64:(i + 1) * 64],
                                                  CT_sb[:, t0:t0 + Q],
                                                  S_sb[:, h * 64:(h + 1) * 64],
                                                  start=True, stop=True)
                            return r
                        pr.t(tf_)

                    def vev(hh):
                        def vf_(vector, hh=hh, c=c):
                            r = None
                            for i in range(16):
                                h = hh * 16 + i
                                vector.tensor_scalar(
                                    f32s[:, i * 64:(i + 1) * 64],
                                    psY2[:, i * 64:(i + 1) * 64],
                                    expcaT[:, c * 32 + h:c * 32 + h + 1], None, ALU.mult)
                                r = vector.tensor_tensor(
                                    ygc[:, h * 64:(h + 1) * 64],
                                    f32s[:, i * 64:(i + 1) * 64],
                                    psY[:, i * 64:(i + 1) * 64], ALU.add)
                            return r
                        pr.v(vf_)

                    def tst(r):
                        half = (r % 2) * 512
                        def tf_(tensor, r=r, half=half):
                            rr = None
                            for i in range(8):
                                h = r * 8 + i
                                rr = tensor.matmul(psS[:, half + i * 64:half + (i + 1) * 64],
                                                   bts[:], xw[:, h * 64:(h + 1) * 64],
                                                   start=True, stop=True)
                            return rr
                        pr.t(tf_)

                    def vst(r):
                        half = (r % 2) * 512
                        def vf_(vector, r=r, half=half, c=c):
                            rr = None
                            for i in range(8):
                                h = r * 8 + i
                                rr = vector.scalar_tensor_tensor(
                                    S_sb[:, h * 64:(h + 1) * 64],
                                    S_sb[:, h * 64:(h + 1) * 64],
                                    dchkr[:, h * 16 + c:h * 16 + c + 1],
                                    psS[:, half + i * 64:half + (i + 1) * 64],
                                    ALU.mult, ALU.add)
                            return rr
                        pr.v(vf_)

                    def vxw(vector, c=c):
                        dv2 = decE[:, c * 32:(c + 1) * 32]
                        bc = dv2.unsqueeze(2).broadcast_to([128, 32, 64])
                        vector.tensor_tensor(
                            xw[:].rearrange("p (h q) -> p h q", q=64),
                            xdt[:].rearrange("p (h q) -> p h q", q=64), bc, ALU.mult)
                        return vector.tensor_copy(bts[:], pstx[:, 0:128])
                    pr.v(vxw); SP("d"); tin(0); pr.phase()
                    vev(0); SP("e"); tst(0); pr.phase()
                    vst(0); SP("e"); tin(1); pr.phase()
                    vev(1); SP("e"); tst(1); pr.phase()
                    vst(1); SP("e"); tst(2); pr.phase()
                    vst(2); SP("e"); tst(3); pr.phase()
                    vst(3); SP("e")
                    def vsk(vector):
                        vector.tensor_tensor(f32s[:], xtc[:], drep[:], ALU.mult)
                        return vector.tensor_tensor(ygc[:], ygc[:], f32s[:], ALU.add)
                    pr.v(vsk)
                    zv = z0_sp.rearrange("p (k t) -> p k t", t=T)
                    pr.need(spl[0])
                    xstrh[0] = pr.dma(xw[:].rearrange("p (k t) -> p k t", t=128),
                                      zv[:, :, t0:t0 + Q], lazy=True)
                    pr.phase()

                    pr.need(xstrh[0])
                    def tzt(tensor):
                        r = None
                        for i in range(8):
                            r = tensor.transpose(pstx[:, i * 128:(i + 1) * 128],
                                                 xw[:, i * 128:(i + 1) * 128],
                                                 pr.ident16[:])
                        return r
                    pr.t(tzt); pr.phase()
                    pr.v(lambda vector: vector.tensor_copy(xtc[:, 0:1024], pstx[:, 0:1024]))
                    pr.phase()
                    def tzt2(tensor):
                        r = None
                        for i in range(8):
                            r = tensor.transpose(pstx[:, i * 128:(i + 1) * 128],
                                                 xw[:, (8 + i) * 128:(9 + i) * 128],
                                                 pr.ident16[:])
                        return r
                    pr.t(tzt2); pr.phase()
                    pr.v(lambda vector: vector.tensor_copy(xtc[:, 1024:2048], pstx[:, 0:1024]))
                    pr.phase()
                    pr.s(lambda scalar: scalar.activation(g32s[:], xtc[:], AF.Silu))
                    pr.phase()
                    def vsl(vector):
                        return vector.tensor_tensor(ygc[:], ygc[:], g32s[:], ALU.mult)
                    pr.v(vsl); pr.phase()
                    pr.s(lambda scalar: scalar.activation(g32s[:], ygc[:], AF.Square,
                                                          accum_out=stat[:, 0:1]))
                    pr.phase()
                    pr.v(lambda vector: vector.tensor_scalar(
                        stat[:, 1:2], stat[:, 0:1], 1.0 / 2048.0, EPS, ALU.mult, ALU.add))
                    pr.phase()
                    pr.s(lambda scalar: scalar.activation(stat[:, 2:3], stat[:, 1:2], AF.Ln))
                    pr.phase()
                    pr.s(lambda scalar: scalar.activation(stat[:, 3:4], stat[:, 2:3],
                                                          AF.Exp, scale=-0.5))
                    pr.phase()
                    def vrm2(vector):
                        vector.tensor_scalar(ygc[:], ygc[:], stat[:, 3:4], None, ALU.mult)
                        return vector.tensor_tensor(ygc[:], ygc[:], nwrep[:], ALU.mult)
                    pr.v(vrm2); pr.phase()
                    def tyg(tensor):
                        r = None
                        for i in range(8):
                            r = tensor.transpose(pstx[:, i * 128:(i + 1) * 128],
                                                 ygc[:, i * 128:(i + 1) * 128],
                                                 pr.ident16[:])
                        return r
                    pr.t(tyg); pr.phase()
                    pr.v(lambda vector: vector.tensor_copy(ygT[:, 0:1024], pstx[:, 0:1024]))
                    pr.phase()
                    def tyg2(tensor):
                        r = None
                        for i in range(8):
                            r = tensor.transpose(pstx[:, i * 128:(i + 1) * 128],
                                                 ygc[:, (8 + i) * 128:(9 + i) * 128],
                                                 pr.ident16[:])
                        return r
                    def topk(k0, k1):
                        def tf_(tensor, k0=k0, k1=k1):
                            r = None
                            for dm in range(8):
                                ps = psY if dm < 4 else psY2
                                o = (dm % 4) * 128
                                for kt in range(k0, k1):
                                    r = tensor.matmul(
                                        ps[:, o:o + 128],
                                        op_sb[:, kt * 1024 + dm * 128:kt * 1024 + dm * 128 + 128],
                                        ygT[:, kt * 128:(kt + 1) * 128],
                                        start=(kt == 0), stop=(kt == 15))
                            return r
                        pr.t(tf_)
                    if L0SER["f"]:
                        pr.t(tyg2); pr.phase()
                        pr.v(lambda vector: vector.tensor_copy(ygT[:, 1024:2048],
                                                               pstx[:, 0:1024]))
                        pr.phase()
                        topk(0, 16); pr.phase()
                    else:
                        pr.t(tyg2); topk(0, 8); pr.phase()
                        pr.v(lambda vector: vector.tensor_copy(ygT[:, 1024:2048],
                                                               pstx[:, 0:1024]))
                        pr.phase()
                        topk(8, 16); pr.phase()
                    def vot(vector):
                        r = None
                        for dm in range(8):
                            ps = psY if dm < 4 else psY2
                            o = (dm % 4) * 128
                            r = vector.tensor_copy(ot8[:, dm * 128:(dm + 1) * 128],
                                                   ps[:, o:o + 128])
                        return r
                    pr.v(vot); pr.phase()
                    def tot(tensor):
                        r = None
                        for dm in range(8):
                            r = tensor.transpose(pstx[:, dm * 128:(dm + 1) * 128],
                                                 ot8[:, dm * 128:(dm + 1) * 128],
                                                 pr.ident16[:])
                        return r
                    pr.t(tot)
                    pr.need(y0hs[c])
                    pr.dma(xtc[:, 0:DM], y0_sp[:, c * DM:(c + 1) * DM])
                    pr.dma(f32s[:, 0:DM], xin[b, c * 128:(c + 1) * 128, :])
                    pr.phase()
                    def vres(vector):
                        vector.tensor_tensor(f32s[:, 0:DM], f32s[:, 0:DM],
                                             xtc[:, 0:DM], ALU.add)
                        return vector.tensor_tensor(f32s[:, 0:DM], f32s[:, 0:DM],
                                                    pstx[:, 0:1024], ALU.add)
                    pr.v(vres); pr.phase()
                    pr.dma(xout[b, c * 128:(c + 1) * 128, :], f32s[:, 0:DM])
                    pr.phase()
    es.close()
    pr.phase()


def build_l1_attn():
    nc = bass.Bass("TRN2", target_bir_lowering=False)
    xin = nc.dram_tensor("x", [NB, T, DM], F32, kind="ExternalInput").ap()
    xout = nc.dram_tensor("xo", [NB, T, DM], F32, kind="ExternalOutput").ap()
    W = _l1_declare(nc)
    with ExitStack() as es:
        pr = Pr(nc, es)
        load_consts(pr, nc, sel=True)
        emit_l1(nc, pr, xin, xout, W, "B_")
        pr.finish()
    return nc


def _l1_declare(nc, pf=""):
    W = {}
    for d in ("fw", "bw"):
        W[d] = dict(
            winT=nc.dram_tensor(f"{pf}{d}_winT", [128, 8 * 4096], F32, kind="ExternalInput").ap(),
            convw=nc.dram_tensor(f"{pf}{d}_convw", [128, 16 * 4], F32, kind="ExternalInput").ap(),
            convb=nc.dram_tensor(f"{pf}{d}_convb", [128, 16], F32, kind="ExternalInput").ap(),
            wxT=nc.dram_tensor(f"{pf}{d}_wxT", [128, 16 * 96], F32, kind="ExternalInput").ap(),
            wdtT=nc.dram_tensor(f"{pf}{d}_wdtT", [64, 2048], F32, kind="ExternalInput").ap(),
            bdt=nc.dram_tensor(f"{pf}{d}_bdt", [128, 16], F32, kind="ExternalInput").ap(),
            alog=nc.dram_tensor(f"{pf}{d}_alog", [128, 16 * 16], F32, kind="ExternalInput").ap(),
            dvec=nc.dram_tensor(f"{pf}{d}_dvec", [128, 16], F32, kind="ExternalInput").ap(),
            woutT=nc.dram_tensor(f"{pf}{d}_woutT", [128, 16 * 1024], F32,
                                 kind="ExternalInput").ap(),
        )
    W["fuseWT"] = nc.dram_tensor(pf + "fuseWT", [128, 16 * 1024], F32,
                                 kind="ExternalInput").ap()
    W["fuseb"] = nc.dram_tensor(pf + "fuseb", [128, 8], F32, kind="ExternalInput").ap()
    return W


def emit_l1(nc, pr, xin, xout, Wd, pf):
    DI = 2048
    NDT = DI // 128
    N = 16
    TH = 512
    NSEG = T // TH
    W = {"fw": Wd["fw"], "bw": Wd["bw"]}
    fuseWT = Wd["fuseWT"]
    fuseb = Wd["fuseb"]
    u_sp = nc.dram_tensor(pf + "u_sp", [128, NDT * T], BF16, kind="Internal").ap()
    dt_sp = nc.dram_tensor(pf + "dt_sp", [128, NDT * T], BF16, kind="Internal").ap()
    z_sp = nc.dram_tensor(pf + "z_sp", [128, NDT * T], BF16, kind="Internal").ap()
    y_sp = nc.dram_tensor(pf + "y_sp", [128, NDT * T], BF16, kind="Internal").ap()
    fo_sp = nc.dram_tensor(pf + "fo_sp", [128, 8 * T], BF16, kind="Internal").ap()
    bo_sp = nc.dram_tensor(pf + "bo_sp", [128, 8 * T], BF16, kind="Internal").ap()
    if True:
        es = ExitStack()
        sel16 = es.enter_context(nc.sbuf_tensor(pf + "sel16", [32, 32 * 128], BF16))
        pr.dma(sel16[:], pr.csel, queue="g")
        pr.phase()
        stat = es.enter_context(nc.sbuf_tensor(pf + "stat", [128, 8], F32))
        xnT = es.enter_context(nc.sbuf_tensor(pf + "xnT", [128, 8 * T], BF16))
        wx_sb = es.enter_context(nc.sbuf_tensor(pf + "wx_sb", [128, 16 * 96], BF16))
        wdt_sb = es.enter_context(nc.sbuf_tensor(pf + "wdt_sb", [64, 2048], BF16))
        cw_sb = es.enter_context(nc.sbuf_tensor(pf + "cw_sb", [128, 16 * 4], F32))
        cb_sb = es.enter_context(nc.sbuf_tensor(pf + "cb_sb", [128, 16], F32))
        bdt_sb = es.enter_context(nc.sbuf_tensor(pf + "bdt_sb", [128, 16], F32))
        dv_sb = es.enter_context(nc.sbuf_tensor(pf + "dv_sb", [128, 16], F32))
        aneg = es.enter_context(nc.sbuf_tensor(pf + "aneg", [128, 16 * 16], F32))
        ci = es.enter_context(nc.sbuf_tensor(pf + "ci", [128, T + 4], BF16))
        wslc = es.enter_context(nc.sbuf_tensor(pf + "wslc", [128, 8 * 128], BF16))
        fuseb_sb = es.enter_context(nc.sbuf_tensor(pf + "fuseb_sb", [128, 8], F32))
        bct_sb = es.enter_context(nc.sbuf_tensor(pf + "bct_sb", [32, T], BF16))
        pr.dma(fuseb_sb[:], fuseb)
        pr.phase()

        for b in range(NB):
            with nc.sbuf_tensor(f"{pf}lnx{b}", [128, DM], F32) as xs, \
                 nc.sbuf_tensor(f"{pf}xnb{b}", [128, DM], BF16) as xnb, \
                 nc.sbuf_tensor(f"{pf}sq{b}", [128, DM], F32) as sqs, \
                 nc.psum_tensor(f"{pf}pstT{b}", [128, 2048], BF16) as pst:
                pr.scr_sq = sqs
                for tt in range(NTT):
                    pr.dma(xs[:], xin[b, tt * 128:(tt + 1) * 128, :])
                    pr.phase()
                    emit_ln(pr, xs, xnb, 1, DM, stat)
                    emit_transpose(pr, pst, [(xnb, kd * 128, xnT, kd * T + tt * 128)
                                             for kd in range(8)])

            foh = [0]
            for di, d in enumerate(("fw", "bw")):
                wt = W[d]
                if di == 1:
                    def rev(vector):
                        r = None
                        for kd in range(8):
                            vector.tensor_copy(ci[:, 0:T], xnT[:, kd * T:(kd + 1) * T])
                            src = ci[:, 0:T]
                            r = vector.tensor_copy(xnT[:, kd * T:(kd + 1) * T], src[:, ::-1])
                        return r
                    pr.v(rev)
                    pr.phase()

                pr.dma(wx_sb[:], wt["wxT"], queue="g")
                pr.dma(wdt_sb[:], wt["wdtT"], queue="g")
                pr.dma(cw_sb[:], wt["convw"])
                pr.dma(cb_sb[:], wt["convb"])
                pr.dma(bdt_sb[:], wt["bdt"])
                pr.dma(dv_sb[:], wt["dvec"])
                pr.dma(aneg[:], wt["alog"])
                pr.phase()
                pr.s(lambda scalar: scalar.activation(aneg[:], aneg[:], AF.Exp))
                pr.phase()
                pr.v(lambda vector: vector.tensor_scalar(aneg[:], aneg[:], -1.0, None, ALU.mult))
                pr.phase()

                l1xz, _ = nc.enter_named_scope(f"l1xz{b}{di}", False)
                cih = [0]
                spl = [0]
                dtscope = ExitStack()
                dtp_sb = dtscope.enter_context(
                    nc.sbuf_tensor(f"{pf}dtp{b}{di}", [64, T], BF16))
                with nc.psum_tensor(f"{pf}psA{b}{di}", [128, 2048], F32) as psA, \
                     nc.psum_tensor(f"{pf}psX{b}{di}", [128, 2048], F32) as psX, \
                     nc.sbuf_tensor(f"{pf}ciB{b}{di}", [128, T + 4], BF16) as ciB, \
                     nc.sbuf_tensor(f"{pf}wslB{b}{di}", [128, 8 * 128], BF16) as wslc2, \
                     nc.sbuf_tensor(f"{pf}e32a{b}{di}", [128, T], F32) as e32, \
                     nc.sbuf_tensor(f"{pf}f32a{b}{di}", [128, T], F32) as f32b:
                    cis = [ci, ciB]
                    fbs = [f32b, e32]
                    wsl = [wslc, wslc2]
                    wh = [0, 0]
                    sph = [0] * 32
                    wv = wt["winT"].rearrange("p (kt o) -> p kt o", o=4096)

                    def ldw(ot):
                        wh[ot % 2] = pr.dma(
                            wsl[ot % 2][:].rearrange("p (kt o) -> p kt o", o=128),
                            wv[:, :, ot * 128:(ot + 1) * 128], queue="g", lazy=True)

                    def mm(ot):
                        pr.need(wh[ot % 2])
                        def tf(tensor, ot=ot):
                            r = None
                            for n0 in range(0, T, 512):
                                for kd in range(8):
                                    r = tensor.matmul(
                                        psA[:, n0:n0 + 512],
                                        wsl[ot % 2][:, kd * 128:(kd + 1) * 128],
                                        xnT[:, kd * T + n0:kd * T + n0 + 512],
                                        start=(kd == 0), stop=(kd == 7))
                            return r
                        pr.t(tf)

                    def evconv(ot):
                        p = ot % 2
                        if ot >= 2:
                            pr.need(sph[ot - 2])
                        if ot < 16:
                            def vc0(vector, ot=ot, p=p):
                                vector.memset(cis[p][:, 0:4], 0.0)
                                vector.tensor_copy(cis[p][:, 4:4 + T], psA[:, 0:T])
                                vector.tensor_scalar(fbs[p][:, 0:T], cis[p][:, 1:1 + T],
                                                     cw_sb[:, ot * 4:ot * 4 + 1], None,
                                                     ALU.mult)
                                for k in (1, 2, 3):
                                    vector.scalar_tensor_tensor(
                                        fbs[p][:, 0:T], cis[p][:, 1 + k:1 + k + T],
                                        cw_sb[:, ot * 4 + k:ot * 4 + k + 1],
                                        fbs[p][:, 0:T], ALU.mult, ALU.add)
                                return vector.tensor_scalar(
                                    fbs[p][:, 0:T], fbs[p][:, 0:T], cb_sb[:, ot:ot + 1],
                                    None, ALU.add)
                            pr.v(vc0)
                        else:
                            pr.v(lambda vector, p=p: vector.tensor_copy(
                                cis[p][:, 4:4 + T], psA[:, 0:T]))

                    def silu(ot):
                        if ot < 16:
                            p = ot % 2
                            pr.s(lambda scalar, p=p: scalar.activation(
                                cis[p][:, 4:4 + T], fbs[p][:, 0:T], AF.Silu))

                    def post(ot):
                        p = ot % 2
                        if ot < 16:
                            def tfx(tensor, ot=ot, p=p):
                                r = None
                                for n0 in range(0, T, 512):
                                    r = tensor.matmul(psX[0:96, n0:n0 + 512],
                                                      wx_sb[:, ot * 96:(ot + 1) * 96],
                                                      cis[p][:, 4 + n0:4 + n0 + 512],
                                                      start=(ot == 0), stop=(ot == 15))
                                return r
                            pr.t(tfx)
                            h = pr.dma(u_sp[:, ot * T:(ot + 1) * T], cis[p][:, 4:4 + T],
                                       queue="g", lazy=True)
                        else:
                            h = pr.dma(z_sp[:, (ot - 16) * T:(ot - 15) * T],
                                       cis[p][:, 4:4 + T], queue="g", lazy=True)
                        sph[ot] = h
                        spl[0] = max(spl[0], h)

                    ldw(0); pr.phase()
                    for ot in range(33):
                        if ot < 32:
                            mm(ot)
                        if ot >= 1:
                            silu(ot - 1)
                        if ot + 1 < 32:
                            ldw(ot + 1)
                        pr.phase()
                        if ot < 32:
                            evconv(ot)
                        if ot >= 1:
                            post(ot - 1)
                        pr.phase()
                    def vx(vector):
                        vector.tensor_copy(dtp_sb[:, 0:T], psX[0:64, 0:T])
                        return vector.tensor_copy(bct_sb[:, 0:T], psX[64:96, 0:T])
                    pr.v(vx); pr.phase()

                with nc.psum_tensor(f"{pf}psB{b}{di}", [128, 2048], F32) as psB, \
                     nc.sbuf_tensor(f"{pf}e32b{b}{di}", [128, T], F32) as e32, \
                     nc.sbuf_tensor(f"{pf}f32c{b}{di}", [128, T], F32) as f32b:
                    for kt in range(NDT):
                        def tf2(tensor, kt=kt):
                            r = None
                            for n0 in range(0, T, 512):
                                r = tensor.matmul(psB[:, n0:n0 + 512],
                                                  wdt_sb[:, kt * 128:(kt + 1) * 128],
                                                  dtp_sb[:, n0:n0 + 512],
                                                  start=True, stop=True)
                            return r
                        pr.t(tf2); pr.phase()
                        def vd0(vector, kt=kt):
                            vector.tensor_scalar(f32b[:, 0:T], psB[:, 0:T],
                                                 bdt_sb[:, kt:kt + 1], None, ALU.add)
                            return vector.tensor_single_scalar(
                                e32[:, 0:T].bitcast(U32), f32b[:, 0:T].bitcast(U32),
                                0x7FFFFFFF, ALU.bitwise_and)
                        pr.v(vd0); pr.phase()
                        def sd1(scalar):
                            scalar.activation(e32[:, 0:T], e32[:, 0:T], AF.Exp, scale=-1.0)
                            return scalar.activation(e32[:, 0:T], e32[:, 0:T], AF.Ln, bias=1.0)
                        pr.s(sd1); pr.phase()
                        pr.need(cih[0])
                        def vd2(vector):
                            return vector.scalar_tensor_tensor(
                                ci[:, 4:4 + T], f32b[:, 0:T], 0.0, e32[:, 0:T],
                                ALU.max, ALU.add)
                        pr.v(vd2)
                        pr.phase()
                        h = pr.dma(dt_sp[:, kt * T:(kt + 1) * T], ci[:, 4:4 + T],
                                   queue="g", lazy=True)
                        cih[0] = h; spl[0] = max(spl[0], h)
                        pr.phase()
                dtscope.close()
                nc.leave_named_scope(f"l1xz{b}{di}", l1xz, False)

                yh = [0]
                l1sc, _ = nc.enter_named_scope(f"l1scan{b}{di}", False)
                with nc.sbuf_tensor(f"{pf}brep{b}{di}", [128, N * TH], BF16) as brep, \
                     nc.sbuf_tensor(f"{pf}crep{b}{di}", [128, N * TH], BF16) as crep, \
                     nc.sbuf_tensor(f"{pf}su{b}{di}", [128, NDT * TH], BF16) as s_u, \
                     nc.sbuf_tensor(f"{pf}sdt{b}{di}", [128, NDT * TH], BF16) as s_dt, \
                     nc.sbuf_tensor(f"{pf}sa0{b}{di}", [128, NDT * TH], BF16) as dA0, \
                     nc.sbuf_tensor(f"{pf}sa1{b}{di}", [128, NDT * TH], BF16) as dA1, \
                     nc.sbuf_tensor(f"{pf}sh{b}{di}", [128, NDT * TH], BF16) as h_h, \
                     nc.sbuf_tensor(f"{pf}shB{b}{di}", [128, NDT * TH], BF16) as h_hB, \
                     nc.sbuf_tensor(f"{pf}sy{b}{di}", [128, NDT * TH], BF16) as ya_h, \
                     nc.sbuf_tensor(f"{pf}hcar{b}{di}", [128, NDT * N], F32) as hcar:
                    for seg in range(NSEG):
                        t0 = seg * TH
                        with nc.psum_tensor(f"{pf}psR{b}{di}{seg}", [128, 2048], F32) as psR:
                            for n0 in range(0, N, 2):
                                def tr2(tensor, n0=n0, t0=t0):
                                    r = None
                                    for i, n in enumerate((n0, n0 + 1)):
                                        tensor.matmul(psR[:, i * 1024:i * 1024 + 512],
                                                      sel16[:, n * 128:(n + 1) * 128],
                                                      bct_sb[:, t0:t0 + TH],
                                                      start=True, stop=True)
                                        r = tensor.matmul(
                                            psR[:, i * 1024 + 512:i * 1024 + 1024],
                                            sel16[:, (16 + n) * 128:(17 + n) * 128],
                                            bct_sb[:, t0:t0 + TH],
                                            start=True, stop=True)
                                    return r
                                pr.t(tr2); pr.phase()
                                def vr2(vector, n0=n0):
                                    r = None
                                    for i, n in enumerate((n0, n0 + 1)):
                                        vector.tensor_copy(brep[:, n * TH:(n + 1) * TH],
                                                           psR[:, i * 1024:i * 1024 + 512])
                                        r = vector.tensor_copy(
                                            crep[:, n * TH:(n + 1) * TH],
                                            psR[:, i * 1024 + 512:i * 1024 + 1024])
                                    return r
                                pr.v(vr2); pr.phase()

                        uv = u_sp.rearrange("p (k t) -> p k t", t=T)
                        dv = dt_sp.rearrange("p (k t) -> p k t", t=T)
                        pr.need(spl[0])
                        pr.dma(s_u[:].rearrange("p (k t) -> p k t", t=TH),
                               uv[:, :, t0:t0 + TH])
                        pr.dma(s_dt[:].rearrange("p (k t) -> p k t", t=TH),
                               dv[:, :, t0:t0 + TH])
                        pr.phase()
                        pr.need(yh[0])
                        def vseed(vector):
                            r = None
                            for j in range(NDT):
                                sl = slice(j * TH, (j + 1) * TH)
                                r = vector.tensor_scalar(ya_h[:, sl], s_u[:, sl],
                                                         dv_sb[:, j:j + 1], None, ALU.mult)
                            return r
                        pr.v(vseed); pr.phase()
                        def vw(vector):
                            return vector.tensor_tensor(s_u[:], s_dt[:], s_u[:], ALU.mult)
                        pr.v(vw)
                        def sa0(scalar):
                            r = None
                            for j in range(NDT):
                                r = scalar.activation(
                                    dA0[:, j * TH:(j + 1) * TH],
                                    s_dt[:, j * TH:(j + 1) * TH], AF.Exp,
                                    scale=aneg[:, j * 16:j * 16 + 1])
                            return r
                        pr.s(sa0); pr.phase()

                        for n in range(N):
                            dAc = dA0 if n % 2 == 0 else dA1
                            dAn = dA1 if n % 2 == 0 else dA0
                            hb = h_h if n % 2 == 0 else h_hB
                            hprev = h_hB if n % 2 == 0 else h_h
                            if n + 1 < N:
                                def sa(scalar, n=n, dAn=dAn):
                                    r = None
                                    for j in range(NDT):
                                        r = scalar.activation(
                                            dAn[:, j * TH:(j + 1) * TH],
                                            s_dt[:, j * TH:(j + 1) * TH], AF.Exp,
                                            scale=aneg[:, j * 16 + n + 1:j * 16 + n + 2])
                                    return r
                                pr.s(sa)
                            if n >= 1:
                                def gacc(vector, hprev=hprev):
                                    return vector.tensor_tensor(ya_h[:], ya_h[:], hprev[:],
                                                                ALU.add)
                                pr.v(gacc)
                            def vscan(vector, n=n, dAc=dAc, seg=seg, hb=hb):
                                bv = brep[:, n * TH:(n + 1) * TH].unsqueeze(1) \
                                    .broadcast_to([128, NDT, TH])
                                cv = crep[:, n * TH:(n + 1) * TH].unsqueeze(1) \
                                    .broadcast_to([128, NDT, TH])
                                hv = hb[:].rearrange("p (k t) -> p k t", t=TH)
                                uv3 = s_u[:].rearrange("p (k t) -> p k t", t=TH)
                                vector.tensor_tensor(hv, uv3, bv, ALU.mult)
                                for j in range(NDT):
                                    sl = slice(j * TH, (j + 1) * TH)
                                    col = j * N + n
                                    ini = 0.0 if seg == 0 else hcar[:, col:col + 1]
                                    vector.tensor_tensor_scan(hb[:, sl], dAc[:, sl],
                                                              hb[:, sl], ini,
                                                              ALU.mult, ALU.add)
                                vector.tensor_copy(hcar[:, n::N], hb[:, TH - 1::TH])
                                return vector.tensor_tensor(hv, hv, cv, ALU.mult)
                            pr.v(vscan); pr.phase()
                        def gacc2(vector):
                            return vector.tensor_tensor(ya_h[:], ya_h[:],
                                                        (h_hB if (N - 1) % 2 else h_h)[:],
                                                        ALU.add)
                        pr.v(gacc2); pr.phase()

                        yv = y_sp.rearrange("p (k t) -> p k t", t=T)
                        yh[0] = pr.dma(yv[:, :, t0:t0 + TH],
                                       ya_h[:].rearrange("p (k t) -> p k t", t=TH),
                                       queue="g", lazy=True)
                        pr.phase()

                nc.leave_named_scope(f"l1scan{b}{di}", l1sc, False)
                l1ga, _ = nc.enter_named_scope(f"l1gate{b}{di}", False)
                osp = fo_sp if di == 0 else bo_sp
                with nc.sbuf_tensor(f"{pf}gy{b}{di}", [128, 2 * 16 * 512], BF16) as gy2, \
                     nc.sbuf_tensor(f"{pf}gz{b}{di}", [128, 2 * 16 * 512], BF16) as gz2, \
                     nc.sbuf_tensor(f"{pf}wo{b}{di}", [128, 16 * 1024], BF16) as wo_sb, \
                     nc.psum_tensor(f"{pf}psO{b}{di}", [128, 8 * 512], F32) as psO:
                    yv = y_sp.rearrange("p (k t) -> p k t", t=T)
                    zv = z_sp.rearrange("p (k t) -> p k t", t=T)
                    woh = pr.dma(wo_sb[:], wt["woutT"], queue="g", lazy=True)
                    gh = [0, 0, 0, 0]

                    def pref(si):
                        q0 = si * 512
                        o = (si % 2) * (16 * 512)
                        pr.dma(gy2[:, o:o + 16 * 512].rearrange("p (k t) -> p k t", t=512),
                               yv[:, :, q0:q0 + 512], lazy=True)
                        gh[si] = pr.dma(
                            gz2[:, o:o + 16 * 512].rearrange("p (k t) -> p k t", t=512),
                            zv[:, :, q0:q0 + 512], lazy=True)
                    pr.need(yh[0]); pr.need(spl[0]); pr.need(foh[0])
                    pref(0)
                    pr.phase()
                    for si in range(4):
                        o = (si % 2) * (16 * 512)
                        gyb = gy2[:, o:o + 16 * 512]
                        gzb = gz2[:, o:o + 16 * 512]
                        pr.need(gh[si])
                        pr.s(lambda scalar, gzb=gzb: scalar.activation(gzb, gzb, AF.Silu))
                        pr.phase()
                        pr.v(lambda vector, gyb=gyb, gzb=gzb:
                             vector.tensor_tensor(gyb, gyb, gzb, ALU.mult))
                        if si < 3:
                            pr.need(foh[0])
                            pref(si + 1)
                        pr.phase()
                        pr.need(woh)
                        def tw(tensor, gyb=gyb):
                            r = None
                            for kt in range(16):
                                for dm in range(8):
                                    r = tensor.matmul(
                                        psO[:, dm * 512:(dm + 1) * 512],
                                        wo_sb[:, kt * 1024 + dm * 128:
                                              kt * 1024 + dm * 128 + 128],
                                        gyb[:, kt * 512:(kt + 1) * 512],
                                        start=(kt == 0), stop=(kt == NDT - 1))
                            return r
                        pr.t(tw); pr.phase()
                        def ve2(vector, di=di, gyb=gyb):
                            r = None
                            for dm in range(8):
                                dst = gyb[:, dm * 512:(dm + 1) * 512]
                                if di == 1:
                                    dst = dst[:, ::-1]
                                r = vector.tensor_copy(dst, psO[:, dm * 512:(dm + 1) * 512])
                            return r
                        pr.v(ve2); pr.phase()
                        q0 = si * 512
                        for dm in range(8):
                            if di == 0:
                                h = pr.dma(osp[:, dm * T + q0:dm * T + q0 + 512],
                                           gyb[:, dm * 512:(dm + 1) * 512], queue="g",
                                           lazy=True)
                            else:
                                h = pr.dma(osp[:, dm * T + T - q0 - 512:dm * T + T - q0],
                                           gyb[:, dm * 512:(dm + 1) * 512], queue="g",
                                           lazy=True)
                            foh[0] = h
                        pr.phase()

                nc.leave_named_scope(f"l1gate{b}{di}", l1ga, False)
            with nc.sbuf_tensor(f"{pf}o_t{b}", [128, DM], F32) as o_t, \
                 nc.sbuf_tensor(f"{pf}fsw{b}", [128, 16 * 1024], BF16) as fw_sb, \
                 nc.sbuf_tensor(f"{pf}fio{b}", [128, 16 * 512], BF16) as fio, \
                 nc.sbuf_tensor(f"{pf}fot{b}", [128, 8 * 512], BF16) as fot, \
                 nc.psum_tensor(f"{pf}psF{b}", [128, 4 * 512], F32) as psF, \
                 nc.psum_tensor(f"{pf}pstF{b}", [128, 1024], BF16) as pstF:
                pr.dma(fw_sb[:], fuseWT, queue="g")
                pr.phase()
                for q0 in range(0, T, 512):
                    fv = fo_sp.rearrange("p (k t) -> p k t", t=T)
                    bv = bo_sp.rearrange("p (k t) -> p k t", t=T)
                    pr.need(foh[0])
                    pr.dma(fio[:, 0:8 * 512].rearrange("p (k t) -> p k t", t=512),
                           fv[:, :, q0:q0 + 512])
                    pr.dma(fio[:, 8 * 512:].rearrange("p (k t) -> p k t", t=512),
                           bv[:, :, q0:q0 + 512])
                    pr.phase()
                    for dh in range(2):
                        def tff(tensor, dh=dh):
                            r = None
                            for dmo in range(dh * 4, dh * 4 + 4):
                                for kt in range(16):
                                    r = tensor.matmul(
                                        psF[:, (dmo % 4) * 512:(dmo % 4) * 512 + 512],
                                        fw_sb[:, kt * 1024 + dmo * 128:
                                              kt * 1024 + dmo * 128 + 128],
                                        fio[:, kt * 512:(kt + 1) * 512],
                                        start=(kt == 0), stop=(kt == 15))
                            return r
                        pr.t(tff); pr.phase()
                        def vff(vector, dh=dh):
                            r = None
                            for dmo in range(dh * 4, dh * 4 + 4):
                                r = vector.tensor_scalar(
                                    fot[:, dmo * 512:(dmo + 1) * 512],
                                    psF[:, (dmo % 4) * 512:(dmo % 4) * 512 + 512],
                                    fuseb_sb[:, dmo:dmo + 1], None, ALU.add)
                            return r
                        pr.v(vff); pr.phase()
                    for st in range(4):
                        def tft(tensor, st=st):
                            r = None
                            for dmo in range(8):
                                r = tensor.transpose(pstF[:, dmo * 128:(dmo + 1) * 128],
                                                     fot[:, dmo * 512 + st * 128:
                                                         dmo * 512 + st * 128 + 128],
                                                     pr.ident16[:])
                            return r
                        pr.t(tft)
                        tt = (q0 + st * 128) // 128
                        pr.dma(o_t[:], xin[b, tt * 128:(tt + 1) * 128, :])
                        pr.phase()
                        def vres(vector):
                            return vector.tensor_tensor(o_t[:], o_t[:], pstF[:, 0:1024],
                                                        ALU.add)
                        pr.v(vres); pr.phase()
                        pr.dma(xout[b, tt * 128:(tt + 1) * 128, :], o_t[:])
                        pr.phase()
    es.close()
    pr.phase()


_PROGS = {}


def _tile_rows(w):
    K_, M_ = w.shape[0] // 128, w.shape[1]
    return np.ascontiguousarray(w.reshape(K_, 128, M_).transpose(1, 0, 2).reshape(128, K_ * M_))


def _get_prog(name):
    if name not in _PROGS:
        _PROGS[name] = {"ffn": build_ffn, "l1": build_l1_attn, "l0": build_l0_attn,
                         "full": build_full}[name]()
    return _PROGS[name]


def _run_ffn(x, wg, wu, wd):
    nc = _get_prog("ffn")
    consts = host_consts()
    in_maps = []
    for c in range(NCORES):
        m = {"c_ident": consts["c_ident"]}
        m["x"] = np.ascontiguousarray(x[c * NB:(c + 1) * NB])
        m["wg"] = _tile_rows(wg)
        m["wu"] = _tile_rows(wu)
        m["wd"] = _tile_rows(wd)
        in_maps.append(m)
    res = run_bass_kernel_spmd(nc, in_maps, core_ids=list(range(NCORES)))
    return np.concatenate([r["xo"] for r in res.results], axis=0)


def _tile_vec(v):
    return np.ascontiguousarray(v.reshape(-1, 128).T)


def _l1_weight_maps(I):
    m = {}
    for d in ("fw", "bw"):
        m[f"{d}_winT"] = _tile_rows(np.ascontiguousarray(I[f"L1_attn_{d}_Win"].T))
        m[f"{d}_convw"] = _tile_rows(np.ascontiguousarray(I[f"L1_attn_{d}_convw"][:, 0, :]))
        m[f"{d}_convb"] = _tile_vec(I[f"L1_attn_{d}_convb"])
        m[f"{d}_wxT"] = _tile_rows(np.ascontiguousarray(I[f"L1_attn_{d}_Wx"].T))
        m[f"{d}_wdtT"] = np.ascontiguousarray(I[f"L1_attn_{d}_Wdt"].T)
        m[f"{d}_bdt"] = _tile_vec(I[f"L1_attn_{d}_bdt"])
        m[f"{d}_alog"] = _tile_rows(np.ascontiguousarray(I[f"L1_attn_{d}_Alog"]))
        m[f"{d}_dvec"] = _tile_vec(I[f"L1_attn_{d}_D"])
        m[f"{d}_woutT"] = _tile_rows(np.ascontiguousarray(I[f"L1_attn_{d}_Wout"].T))
    m["fuseWT"] = _tile_rows(np.ascontiguousarray(I["L1_attn_fuse_W"].T))
    m["fuseb"] = _tile_vec(I["L1_attn_fuse_b"])
    return m


def _l0_weight_maps(I):
    H = 32
    D = I["L0_attn_D"]
    hp = np.repeat(D, 64)
    return {
        "inprojT": _tile_rows(np.ascontiguousarray(I["L0_attn_in_proj_w"].T)),
        "convw": _tile_rows(np.ascontiguousarray(I["L0_attn_conv_w"])),
        "convbi": _tile_vec(I["L0_attn_conv_b"]),
        "dtbias": np.ascontiguousarray(I["L0_attn_dt_bias"].reshape(32, 1)),
        "alogi": np.ascontiguousarray(I["L0_attn_A_log"].reshape(32, 1)),
        "dvec_rep": np.ascontiguousarray(np.broadcast_to(hp, (128, 2048))),
        "normw_rep": np.ascontiguousarray(np.broadcast_to(I["L0_attn_norm_w"], (128, 2048))),
        "lnw_rep": np.ascontiguousarray(np.broadcast_to(I["L0_attn_ln_w"], (128, 1024))),
        "lnb_rep": np.ascontiguousarray(np.broadcast_to(I["L0_attn_ln_b"], (128, 1024))),
        "outprojT": _tile_rows(np.ascontiguousarray(I["L0_attn_out_proj_w"].T)),
    }


def _run_l0(x, I):
    nc = _get_prog("l0")
    consts = host_consts()
    wm = _l0_weight_maps(I)
    in_maps = []
    for c in range(NCORES):
        m = {"c_ident": consts["c_ident"], "c_sel": consts["c_sel"],
             "c_tri": consts["c_tri"]}
        m.update(wm)
        m["x"] = np.ascontiguousarray(x[c * NB:(c + 1) * NB])
        in_maps.append(m)
    res = run_bass_kernel_spmd(nc, in_maps, core_ids=list(range(NCORES)))
    return np.concatenate([r["xo"] for r in res.results], axis=0)


def _run_l1(x, I):
    nc = _get_prog("l1")
    consts = host_consts()
    wm = _l1_weight_maps(I)
    in_maps = []
    for c in range(NCORES):
        m = {"c_ident": consts["c_ident"], "c_sel": consts["c_sel"]}
        m.update(wm)
        m["x"] = np.ascontiguousarray(x[c * NB:(c + 1) * NB])
        in_maps.append(m)
    res = run_bass_kernel_spmd(nc, in_maps, core_ids=list(range(NCORES)))
    return np.concatenate([r["xo"] for r in res.results], axis=0)


def build_full():
    I = 2816
    KI = I // 128
    KD = DM // 128
    nc = bass.Bass("TRN2", target_bir_lowering=False)
    xin = nc.dram_tensor("x", [NB, T, DM], F32, kind="ExternalInput").ap()
    xout = nc.dram_tensor("xo", [NB, T, DM], F32, kind="ExternalOutput").ap()
    x1 = nc.dram_tensor("x1", [NB, T, DM], F32, kind="Internal").ap()
    x2 = nc.dram_tensor("x2", [NB, T, DM], F32, kind="Internal").ap()
    x3 = nc.dram_tensor("x3", [NB, T, DM], F32, kind="Internal").ap()
    WL0 = _l0_declare(nc)
    WL1 = _l1_declare(nc)
    ffw = {}
    for t_ in ("0", "1"):
        ffw[t_] = (nc.dram_tensor(f"wg{t_}", [128, KI * DM], F32, kind="ExternalInput").ap(),
                   nc.dram_tensor(f"wu{t_}", [128, KI * DM], F32, kind="ExternalInput").ap(),
                   nc.dram_tensor(f"wd{t_}", [128, KD * I], F32, kind="ExternalInput").ap())
    with ExitStack() as es:
        pr = Pr(nc, es)
        load_consts(pr, nc, sel=True, tri=True, identf=True)
        with nc.named_scope("L0"):
            emit_l0(nc, pr, xin, x1, WL0, "A_")
        with nc.named_scope("FFN0"):
            emit_ffn(nc, pr, x1, ffw["0"][0], ffw["0"][1], ffw["0"][2], x2, "C_")
        with nc.named_scope("L1"):
            emit_l1(nc, pr, x2, x3, WL1, "B_")
        with nc.named_scope("FFN1"):
            emit_ffn(nc, pr, x3, ffw["1"][0], ffw["1"][1], ffw["1"][2], xout, "D_")
        pr.finish()
    return nc


def kernel(**inputs):
    nc = _get_prog("full")
    consts = host_consts()
    m0 = _l0_weight_maps(inputs)
    m1 = _l1_weight_maps(inputs)
    base = {"c_ident": consts["c_ident"], "c_sel": consts["c_sel"],
            "c_tri": consts["c_tri"]}
    base.update(m0)
    base.update(m1)
    for t_, pre in (("0", "L0_ffn"), ("1", "L1_ffn")):
        base[f"wg{t_}"] = _tile_rows(inputs[f"{pre}_w_gate"])
        base[f"wu{t_}"] = _tile_rows(inputs[f"{pre}_w_up"])
        base[f"wd{t_}"] = _tile_rows(inputs[f"{pre}_w_down"])
    x = np.ascontiguousarray(inputs["L0_attn_x"], dtype=np.float32)
    in_maps = []
    for c in range(NCORES):
        m = dict(base)
        m["x"] = np.ascontiguousarray(x[c * NB:(c + 1) * NB])
        in_maps.append(m)
    res = run_bass_kernel_spmd(nc, in_maps, core_ids=list(range(NCORES)))
    return np.concatenate([r["xo"] for r in res.results], axis=0)


def _kernel_unfused(**inputs):
    x = np.ascontiguousarray(inputs["L0_attn_x"], dtype=np.float32)
    x = _run_l0(x, inputs)
    x = _run_ffn(x, inputs["L0_ffn_w_gate"], inputs["L0_ffn_w_up"], inputs["L0_ffn_w_down"])
    x = _run_l1(x, inputs)
    x = _run_ffn(x, inputs["L1_ffn_w_gate"], inputs["L1_ffn_w_up"], inputs["L1_ffn_w_down"])
    return x


# version 20
# speedup vs baseline: 1.2634x; 1.0150x over previous
import numpy as np
import ml_dtypes
from contextlib import ExitStack

import concourse.bass as bass
import concourse.mybir as mybir
from concourse.bass_utils import run_bass_kernel_spmd

F32 = mybir.dt.float32
BF16 = mybir.dt.bfloat16
U8 = mybir.dt.uint8
U16 = mybir.dt.uint16
U32 = mybir.dt.uint32
AF = mybir.ActivationFunctionType
ALU = mybir.AluOpType
AX = mybir.AxisListType
BFNP = ml_dtypes.bfloat16

T = 2048
DM = 1024
NB = 2
NCORES = 8
NTT = T // 128
EPS = 1e-5
MAGIC = 12582912.0


class Pr:
    ENGS = ("v", "s", "t", "g", "y")

    def __init__(self, nc, es):
        self.nc = nc
        self.es = es
        self.sems = {e: es.enter_context(nc.semaphore("sem_" + e)) for e in self.ENGS}
        self.dsem = es.enter_context(nc.semaphore("sem_dma"))
        self.lzsem = es.enter_context(nc.semaphore("sem_lzdma"))
        self.cnt = {e: 0 for e in self.ENGS}
        self.dcnt = 0
        self.pend_dma = 0
        self.lzcnt = 0
        self.need_lz = 0
        self.lz_seen = {e: 0 for e in self.ENGS}
        self.seen = {e: {o: 0 for o in self.ENGS} for e in self.ENGS}
        self.dseen = {e: 0 for e in self.ENGS}
        self.blk = es.enter_context(nc.Block())
        self.q = {e: [] for e in self.ENGS}
        self._names = 0

    def v(self, fn): self.q["v"].append(fn)
    def s(self, fn): self.q["s"].append(fn)
    def t(self, fn): self.q["t"].append(fn)
    def g(self, fn): self.q["g"].append(fn)
    def y(self, fn): self.q["y"].append(fn)

    def dma(self, out, in_, queue="y", lazy=False, **kw):
        if lazy:
            lzsem = self.lzsem
            def f(eng):
                eng.dma_start(out=out, in_=in_, **kw).then_inc(lzsem, 16)
                return None
            self.q[queue].append(f)
            self.lzcnt += 1
            return self.lzcnt
        dsem = self.dsem
        def f(eng):
            eng.dma_start(out=out, in_=in_, **kw).then_inc(dsem, 16)
            return None
        self.q[queue].append(f)
        self.pend_dma += 1
        return None

    def need(self, h):
        if h:
            self.need_lz = max(self.need_lz, h)

    def sbuf(self, shape, dtype=F32, name=None):
        if name is None:
            self._names += 1
            name = f"sb{self._names}"
        return self.es.enter_context(self.nc.sbuf_tensor(name, shape, dtype))

    def psum(self, shape, dtype=F32, name=None):
        if name is None:
            self._names += 1
            name = f"ps{self._names}"
        return self.es.enter_context(self.nc.psum_tensor(name, shape, dtype))

    def phase(self):
        waits = {}
        for e in self.ENGS:
            if self.q[e]:
                w = []
                for oe in self.ENGS:
                    c = self.cnt[oe]
                    if c > 0 and (oe == e or c > self.seen[e][oe]):
                        if c > self.seen[e][oe]:
                            w.append((self.sems[oe], c))
                            self.seen[e][oe] = c
                waits[e] = w
        dwait = self.dcnt
        for e in self.ENGS:
            ops = self.q[e]
            if not ops:
                continue
            sem = self.sems[e]
            mywaits = waits[e]
            dsem = self.dsem
            dw_eff = dwait if dwait > self.dseen[e] else 0
            if dw_eff:
                self.dseen[e] = dwait
            lzw = self.need_lz if self.need_lz > self.lz_seen[e] else 0
            if lzw:
                self.lz_seen[e] = lzw
            lzsem = self.lzsem
            def body(eng, ops=ops, mywaits=mywaits, dwait=dw_eff, sem=sem, dsem=dsem,
                     lzw=lzw, lzsem=lzsem):
                for s_, c_ in mywaits:
                    eng.wait_ge(s_, c_)
                if dwait:
                    eng.wait_ge(dsem, 16 * dwait)
                if lzw:
                    eng.wait_ge(lzsem, 16 * lzw)
                last = None
                for fn in ops:
                    r = fn(eng)
                    if r is not None:
                        last = r
                if last is not None:
                    last.then_inc(sem, 1)
                else:
                    eng.nop().then_inc(sem, 1)
            getattr(self.blk, {"v": "vector", "s": "scalar", "t": "tensor",
                               "g": "gpsimd", "y": "sync"}[e])(body)
            self.cnt[e] += 1
            self.seen[e][e] = self.cnt[e]
            self.q[e] = []
        self.dcnt += self.pend_dma
        self.pend_dma = 0

    def finish(self):
        self.phase()
        dw = self.dcnt
        lw = self.lzcnt
        waits = [(self.sems[e], self.cnt[e]) for e in self.ENGS if self.cnt[e] > 0]
        dsem = self.dsem
        lzsem = self.lzsem
        def body(eng):
            for s_, c_ in waits:
                eng.wait_ge(s_, c_)
            eng.wait_ge(dsem, 16 * dw)
            if lw:
                eng.wait_ge(lzsem, 16 * lw)
        self.blk.sync(body)


def load_consts(pr, nc, sel=False, tri=False, ones=False, identf=False, self32=False):
    cid = nc.dram_tensor("c_ident", [128, 128], F32, kind="ExternalInput").ap()
    pr.ident16 = pr.sbuf([128, 128], BF16, "identb")
    pr.dma(pr.ident16[:], cid, queue="g")
    if identf:
        pr.identf = pr.sbuf([128, 128], F32, "identf")
        pr.dma(pr.identf[:], cid)
    if sel or self32:
        csel = nc.dram_tensor("c_sel", [32, 32 * 128], F32, kind="ExternalInput").ap()
        pr.csel = csel
        if sel:
            pass
        if self32:
            pr.self32 = pr.sbuf([32, 32 * 128], F32, "self32")
            pr.dma(pr.self32[:], csel)
    if tri:
        ctri = nc.dram_tensor("c_tri", [128, 128], F32, kind="ExternalInput").ap()
        pr.tri16 = pr.sbuf([128, 128], BF16, "trib")
        pr.dma(pr.tri16[:], ctri, queue="g")
    if ones:
        pr.ones1f = pr.sbuf([1, 512], F32, "ones1f")
        def vf(vector):
            return vector.memset(pr.ones1f[:], 1.0)
        pr.v(vf)
    pr.phase()


def host_consts():
    ident = np.eye(128, dtype=np.float32)
    sel = np.zeros((32, 32 * 128), dtype=np.float32)
    for j in range(32):
        sel[j, j * 128:(j + 1) * 128] = 1.0
    tri = np.triu(np.ones((128, 128), dtype=np.float32))
    return {"c_ident": ident, "c_sel": sel, "c_tri": tri}


def emit_ln(pr, xin, xo, ntiles, D, stat, gamma_rep=None, beta_rep=None, xo2=None):
    sq = pr.scr_sq

    def s1(scalar):
        r = None
        for i in range(ntiles):
            r = scalar.activation(sq[:, 0:D], xin[:, i * D:(i + 1) * D], AF.Square,
                                  accum_out=stat[:, i * 8 + 1:i * 8 + 2])
        return r
    def v1(vector):
        r = None
        for i in range(ntiles):
            r = vector.reduce_sum(stat[:, i * 8:i * 8 + 1], xin[:, i * D:(i + 1) * D], axis=AX.X)
        return r
    pr.s(s1); pr.v(v1); pr.phase()

    def v2(vector):
        r = None
        for i in range(ntiles):
            c = i * 8
            vector.tensor_scalar(stat[:, c + 2:c + 3], stat[:, c:c + 1], 1.0 / D, None, ALU.mult)
            r = vector.tensor_scalar(stat[:, c + 4:c + 5], stat[:, c + 1:c + 2],
                                     1.0 / D, EPS, ALU.mult, ALU.add)
        return r
    pr.v(v2); pr.phase()

    def v3(vector):
        r = None
        for i in range(ntiles):
            c = i * 8
            vector.scalar_tensor_tensor(stat[:, c + 5:c + 6], stat[:, c + 2:c + 3],
                                        stat[:, c + 2:c + 3], stat[:, c + 4:c + 5],
                                        ALU.mult, ALU.subtract)
            r = vector.tensor_scalar(stat[:, c + 7:c + 8], stat[:, c + 2:c + 3],
                                     -1.0, None, ALU.mult)
        return r
    pr.v(v3); pr.phase()

    def s2(scalar):
        r = None
        for i in range(ntiles):
            c = i * 8
            r = scalar.activation(stat[:, c + 3:c + 4], stat[:, c + 5:c + 6], AF.Ln, scale=-1.0)
        return r
    pr.s(s2); pr.phase()

    def s3(scalar):
        r = None
        for i in range(ntiles):
            c = i * 8
            r = scalar.activation(stat[:, c + 6:c + 7], stat[:, c + 3:c + 4], AF.Exp, scale=-0.5)
        return r
    pr.s(s3); pr.phase()

    def v4(vector):
        r = None
        for i in range(ntiles):
            c = i * 8
            r = vector.tensor_scalar(xo[:, i * D:(i + 1) * D], xin[:, i * D:(i + 1) * D],
                                     stat[:, c + 7:c + 8], stat[:, c + 6:c + 7], ALU.add, ALU.mult)
            if xo2 is not None:
                r = vector.tensor_copy(xo2[:, i * D:(i + 1) * D], xo[:, i * D:(i + 1) * D])
        return r
    pr.v(v4); pr.phase()

    if gamma_rep is not None:
        def v5(vector):
            r = None
            for i in range(ntiles):
                sl = slice(i * D, (i + 1) * D)
                vector.tensor_tensor(xo[:, sl], xo[:, sl], gamma_rep[:, 0:D], ALU.mult)
                r = vector.tensor_tensor(xo[:, sl], xo[:, sl], beta_rep[:, 0:D], ALU.add)
            return r
        pr.v(v5); pr.phase()


def emit_transpose(pr, pst, blocks):
    group = pst.shape[1] // 128
    ident = pr.ident16[:]
    for i0 in range(0, len(blocks), group):
        n = min(group, len(blocks) - i0)
        def tf(tensor, i0=i0, n=n):
            r = None
            for j in range(n):
                st, sc, _, _ = blocks[i0 + j]
                r = tensor.transpose(pst[:, j * 128:(j + 1) * 128], st[:, sc:sc + 128], ident)
            return r
        pr.t(tf); pr.phase()
        def vf(vector, i0=i0, n=n):
            r = None
            for j in range(n):
                _, _, dt_, dc = blocks[i0 + j]
                r = vector.tensor_copy(dt_[:, dc:dc + 128], pst[:, j * 128:(j + 1) * 128])
            return r
        pr.v(vf); pr.phase()


def _emit_quant_single(pr, job, scr):
    x, F, out = job
    G = F // 16
    bm, rs = scr["bm"][0], scr["rs"][0]
    def p1(vector):
        xg = x[:, 0:F].rearrange("p (g o) -> p g o", o=16)
        return vector.reduce_max(bm[:, 0:G], xg, axis=AX.X, apply_absolute_value=True)
    pr.v(p1); pr.phase()
    pr.v(lambda vector: vector.tensor_scalar(bm[:, 0:G], bm[:, 0:G], 1.0 / 6.0, 1e-12,
                                             ALU.mult, ALU.max))
    pr.phase()
    pr.v(lambda vector: vector.reciprocal(rs[:, 0:G], bm[:, 0:G]))
    pr.phase()
    pr.v(lambda vector: vector.tensor_tensor(scr["q1"][0][:, 0:G], bm[:, 0:G],
                                             rs[:, 0:G], ALU.mult))
    pr.phase()
    pr.v(lambda vector: vector.tensor_scalar(scr["q1"][0][:, 0:G], scr["q1"][0][:, 0:G],
                                             -1.0, 2.0, ALU.mult, ALU.add))
    pr.phase()
    pr.v(lambda vector: vector.tensor_tensor(rs[:, 0:G], rs[:, 0:G],
                                             scr["q1"][0][:, 0:G], ALU.mult))
    pr.phase()
    def p2(vector):
        u, m = scr["u"][0], scr["m"][0]
        q1, qh, mg = scr["q1"][0], scr["qh"][0], scr["mg"][0]
        rsb = rs[:, 0:G].unsqueeze(2).broadcast_to([128, G, 16])
        smb = bm[:, 0:G].unsqueeze(2).broadcast_to([128, G, 16])
        xv = x[:, 0:F].rearrange("p (g o) -> p g o", o=16)
        vector.tensor_copy(u[:, 0:F], x[:, 0:F])
        vector.tensor_single_scalar(m[:, 0:F].bitcast(U16), u[:, 0:F].bitcast(U16),
                                    0x8000, ALU.bitwise_and)
        mgv = mg[:, 0:F].rearrange("p (g o) -> p g o", o=16)
        vector.tensor_tensor(mgv, xv, rsb, ALU.mult)
        vector.tensor_single_scalar(mg[:, 0:F].bitcast(U32), mg[:, 0:F].bitcast(U32),
                                    0x7FFFFFFF, ALU.bitwise_and)
        vector.tensor_scalar(q1[:, 0:F], mg[:, 0:F], 2.0, MAGIC, ALU.mult, ALU.add)
        vector.tensor_scalar(q1[:, 0:F], q1[:, 0:F], MAGIC, 0.5, ALU.subtract, ALU.mult)
        vector.tensor_scalar(qh[:, 0:F], mg[:, 0:F], MAGIC, None, ALU.add)
        vector.tensor_scalar(qh[:, 0:F], qh[:, 0:F], MAGIC, 4.0, ALU.subtract, ALU.min)
        vector.tensor_single_scalar(u[:, 0:F], mg[:, 0:F], 5.0, ALU.is_ge)
        vector.scalar_tensor_tensor(qh[:, 0:F], u[:, 0:F], 2.0, qh[:, 0:F],
                                    ALU.mult, ALU.add)
        vector.tensor_single_scalar(u[:, 0:F], mg[:, 0:F], 1.75, ALU.is_lt)
        vector.tensor_tensor(q1[:, 0:F], q1[:, 0:F], qh[:, 0:F], ALU.subtract)
        vector.tensor_tensor(q1[:, 0:F], q1[:, 0:F], u[:, 0:F], ALU.mult)
        vector.tensor_tensor(q1[:, 0:F], q1[:, 0:F], qh[:, 0:F], ALU.add)
        vector.tensor_copy(u[:, 0:F], q1[:, 0:F])
        vector.tensor_tensor(u[:, 0:F].bitcast(U16), u[:, 0:F].bitcast(U16),
                             m[:, 0:F].bitcast(U16), ALU.bitwise_or)
        uv2 = u[:, 0:F].rearrange("p (g o) -> p g o", o=16)
        ov = out[:, 0:F].rearrange("p (g o) -> p g o", o=16)
        return vector.tensor_tensor(ov, uv2, smb, ALU.mult)
    pr.v(p2); pr.phase()


def emit_quant(pr, jobs, scr):
    npool = len(scr["u"])
    for g0 in range(0, len(jobs), npool):
        grp = jobs[g0:g0 + npool]
        if len(grp) == 1:
            _emit_quant_single(pr, grp[0], scr)
            continue

        def vq(vector):
            for j, (x, F, out) in enumerate(grp):
                G = F // 16
                xg = x[:, 0:F].rearrange("p (g o) -> p g o", o=16)
                vector.reduce_max(scr["bm"][j][:, 0:G], xg, axis=AX.X,
                                  apply_absolute_value=True)
            for j, (x, F, out) in enumerate(grp):
                G = F // 16
                vector.tensor_scalar(scr["bm"][j][:, 0:G], scr["bm"][j][:, 0:G],
                                     1.0 / 6.0, 1e-12, ALU.mult, ALU.max)
            for j, (x, F, out) in enumerate(grp):
                G = F // 16
                vector.reciprocal(scr["rs"][j][:, 0:G], scr["bm"][j][:, 0:G])
            for j, (x, F, out) in enumerate(grp):
                G = F // 16
                vector.tensor_tensor(scr["q1"][j][:, 0:G], scr["bm"][j][:, 0:G],
                                     scr["rs"][j][:, 0:G], ALU.mult)
            for j, (x, F, out) in enumerate(grp):
                G = F // 16
                vector.tensor_scalar(scr["q1"][j][:, 0:G], scr["q1"][j][:, 0:G],
                                     -1.0, 2.0, ALU.mult, ALU.add)
            for j, (x, F, out) in enumerate(grp):
                G = F // 16
                vector.tensor_tensor(scr["rs"][j][:, 0:G], scr["rs"][j][:, 0:G],
                                     scr["q1"][j][:, 0:G], ALU.mult)
            r = None
            for j, (x, F, out) in enumerate(grp):
                G = F // 16
                bm, rs = scr["bm"][j], scr["rs"][j]
                u, m = scr["u"][j], scr["m"][j]
                q1, qh, mg = scr["q1"][j], scr["qh"][j], scr["mg"][j]
                rsb = rs[:, 0:G].unsqueeze(2).broadcast_to([128, G, 16])
                smb = bm[:, 0:G].unsqueeze(2).broadcast_to([128, G, 16])
                xv = x[:, 0:F].rearrange("p (g o) -> p g o", o=16)
                vector.tensor_copy(u[:, 0:F], x[:, 0:F])
                vector.tensor_single_scalar(m[:, 0:F].bitcast(U16), u[:, 0:F].bitcast(U16),
                                            0x8000, ALU.bitwise_and)
                mgv = mg[:, 0:F].rearrange("p (g o) -> p g o", o=16)
                vector.tensor_tensor(mgv, xv, rsb, ALU.mult)
                vector.tensor_single_scalar(mg[:, 0:F].bitcast(U32), mg[:, 0:F].bitcast(U32),
                                            0x7FFFFFFF, ALU.bitwise_and)
                vector.tensor_scalar(q1[:, 0:F], mg[:, 0:F], 2.0, MAGIC, ALU.mult, ALU.add)
                vector.tensor_scalar(q1[:, 0:F], q1[:, 0:F], MAGIC, 0.5, ALU.subtract, ALU.mult)
                vector.tensor_scalar(qh[:, 0:F], mg[:, 0:F], MAGIC, None, ALU.add)
                vector.tensor_scalar(qh[:, 0:F], qh[:, 0:F], MAGIC, 4.0, ALU.subtract, ALU.min)
                vector.tensor_single_scalar(u[:, 0:F], mg[:, 0:F], 5.0, ALU.is_ge)
                vector.scalar_tensor_tensor(qh[:, 0:F], u[:, 0:F], 2.0, qh[:, 0:F],
                                            ALU.mult, ALU.add)
                vector.tensor_single_scalar(u[:, 0:F], mg[:, 0:F], 1.75, ALU.is_lt)
                vector.tensor_tensor(q1[:, 0:F], q1[:, 0:F], qh[:, 0:F], ALU.subtract)
                vector.tensor_tensor(q1[:, 0:F], q1[:, 0:F], u[:, 0:F], ALU.mult)
                vector.tensor_tensor(q1[:, 0:F], q1[:, 0:F], qh[:, 0:F], ALU.add)
                vector.tensor_copy(u[:, 0:F], q1[:, 0:F])
                vector.tensor_tensor(u[:, 0:F].bitcast(U16), u[:, 0:F].bitcast(U16),
                                     m[:, 0:F].bitcast(U16), ALU.bitwise_or)
                uv2 = u[:, 0:F].rearrange("p (g o) -> p g o", o=16)
                ov = out[:, 0:F].rearrange("p (g o) -> p g o", o=16)
                r = vector.tensor_tensor(ov, uv2, smb, ALU.mult)
            return r
        pr.v(vq); pr.phase()


def mk_quant_scr(pr, F, n=1, es=None):
    def alloc(shape, dt_):
        pr._names += 1
        name = f"qscr{pr._names}"
        cm = pr.nc.sbuf_tensor(name, shape, dt_)
        return (es or pr.es).enter_context(cm)
    return {
        "bm": [alloc([128, F // 16], F32) for _ in range(n)],
        "rs": [alloc([128, F // 16], F32) for _ in range(n)],
        "q1": [alloc([128, F], F32) for _ in range(n)],
        "qh": [alloc([128, F], F32) for _ in range(n)],
        "mg": [alloc([128, F], F32) for _ in range(n)],
        "u": [alloc([128, F], BF16) for _ in range(n)],
        "m": [alloc([128, F], BF16) for _ in range(n)],
    }


def emit_silu(pr, tiles, scr_e):
    npool = len(scr_e)
    for g0 in range(0, len(tiles), npool):
        grp = tiles[g0:g0 + npool]
        def s1(scalar):
            r = None
            for j, (x, F) in enumerate(grp):
                r = scalar.activation(scr_e[j][:, 0:F], x[:, 0:F], AF.Exp, scale=-1.0)
            return r
        pr.s(s1); pr.phase()
        def v1(vector):
            r = None
            for j, (x, F) in enumerate(grp):
                e = scr_e[j]
                vector.tensor_scalar(e[:, 0:F], e[:, 0:F], 1.0, None, ALU.add)
                vector.reciprocal(e[:, 0:F], e[:, 0:F])
                r = vector.tensor_tensor(x[:, 0:F], x[:, 0:F], e[:, 0:F], ALU.mult)
            return r
        pr.v(v1); pr.phase()


DBG_FFN = False


def build_ffn():
    I = 2816
    KI = I // 128
    KD = DM // 128
    nc = bass.Bass("TRN2", target_bir_lowering=False)
    xin = nc.dram_tensor("x", [NB, T, DM], F32, kind="ExternalInput").ap()
    wg = nc.dram_tensor("wg", [128, KI * DM], F32, kind="ExternalInput").ap()
    wu = nc.dram_tensor("wu", [128, KI * DM], F32, kind="ExternalInput").ap()
    wd = nc.dram_tensor("wd", [128, KD * I], F32, kind="ExternalInput").ap()
    xout = nc.dram_tensor("xo", [NB, T, DM], F32, kind="ExternalOutput").ap()
    with ExitStack() as es:
        pr = Pr(nc, es)
        load_consts(pr, nc)
        emit_ffn(nc, pr, xin, wg, wu, wd, xout, "F_")
        pr.finish()
    return nc


def emit_ffn(nc, pr, xin, wg, wu, wd, xout, pf):
    I = 2816
    KI = I // 128
    KD = DM // 128
    with ExitStack() as es:
        scr = mk_quant_scr(pr, 512, n=3, es=es)
        pr.scr_sq = scr["q1"][0]

        swgT = es.enter_context(nc.sbuf_tensor(pf + "swgT", [128, KD * I], BF16))
        swuT = es.enter_context(nc.sbuf_tensor(pf + "swuT", [128, KD * I], BF16))
        swdT = es.enter_context(nc.sbuf_tensor(pf + "swdT", [128, KI * DM], BF16))
        pst = es.enter_context(nc.psum_tensor(pf + "pst", [128, 2048], BF16))
        psA = es.enter_context(nc.psum_tensor(pf + "psA", [128, 1024], F32))
        psA2 = es.enter_context(nc.psum_tensor(pf + "psA2", [128, 1024], F32))
        psD = es.enter_context(nc.psum_tensor(pf + "psD", [128, 1024], F32))
        xnT = es.enter_context(nc.sbuf_tensor(pf + "xnT", [128, KD * T], BF16))

        _pc = [0]

        def prep(wdram, n_rowtiles, fcols, dstT, slice_tiles):
            for s0 in range(0, n_rowtiles, slice_tiles):
                ns = min(slice_tiles, n_rowtiles - s0)
                _pc[0] += 1
                with nc.sbuf_tensor(f"{pf}wstg{_pc[0]}", [128, ns * fcols], F32) as stg, \
                     nc.sbuf_tensor(f"{pf}wstq{_pc[0]}", [128, ns * fcols], BF16) as stq:
                    pr.dma(stg[:], wdram[:, s0 * fcols:(s0 + ns) * fcols])
                    pr.phase()
                    qw = 512 if fcols % 512 == 0 else 352
                    qjobs = []
                    for j in range(ns):
                        for c in range(0, fcols, qw):
                            qjobs.append((stg[:, j * fcols + c:j * fcols + c + qw], qw,
                                          stq[:, j * fcols + c:j * fcols + c + qw]))
                    emit_quant(pr, qjobs, scr)
                    blocks = []
                    for j in range(ns):
                        kt = s0 + j
                        for fc in range(fcols // 128):
                            blocks.append((stq, j * fcols + fc * 128,
                                           dstT, fc * (n_rowtiles * 128) + kt * 128))
                    emit_transpose(pr, pst, blocks)

        with nc.named_scope("ffn_wprep"):
            prep(wg, KI, DM, swgT, 3)
            prep(wu, KI, DM, swuT, 3)
            prep(wd, KD, I, swdT, 1)

        stat = es.enter_context(nc.sbuf_tensor(pf + "stat", [128, 8], F32))
        g32 = es.enter_context(nc.sbuf_tensor(pf + "g32", [128, DM], F32))
        u32 = es.enter_context(nc.sbuf_tensor(pf + "u32", [128, DM], F32))
        o_t = u32
        x_t = g32
        pr.scr_sq = u32

        for b in range(NB):
            QT = 1
            for h0 in range(0, NTT, QT):
                with nc.sbuf_tensor(f"{pf}xs{b}_{h0}", [128, QT * DM], F32) as xs, \
                     nc.sbuf_tensor(f"{pf}xq{b}_{h0}", [128, QT * DM], BF16) as xq:
                    for i in range(QT):
                        tt = h0 + i
                        pr.dma(xs[:, i * DM:(i + 1) * DM], xin[b, tt * 128:(tt + 1) * 128, :])
                    pr.phase()
                    emit_ln(pr, xs, xs, QT, DM, stat)
                    emit_quant(pr, [(xs[:, i * DM + c:i * DM + c + 512], 512,
                                     xq[:, i * DM + c:i * DM + c + 512])
                                    for i in range(QT) for c in (0, 512)], scr)
                    emit_transpose(pr, pst,
                                   [(xq, i * DM + kd * 128, xnT, kd * T + (h0 + i) * 128)
                                    for i in range(QT) for kd in range(KD)])


            with nc.sbuf_tensor(f"{pf}h_t{b}", [128, I], BF16) as h_t:
              hT_t = h_t
              for tt in range(NTT):
                  slices = []
                  for c0 in range(0, I, DM):
                      cw = min(DM, I - c0)
                      for i0 in range(c0, c0 + cw, 512):
                          slices.append((c0, i0, min(512, c0 + cw - i0)))
                  nsl = len(slices)

                  def emit_mm(si):
                      c0, i0, iw = slices[si]
                      ps = psA if si % 2 == 0 else psA2
                      def tf(tensor, i0=i0, iw=iw, ps=ps, tt=tt):
                          r = None
                          for half, w in ((0, swgT), (1, swuT)):
                              for kd in range(KD):
                                  r = tensor.matmul(
                                      ps[:, half * 512:half * 512 + iw],
                                      xnT[:, kd * T + tt * 128:kd * T + tt * 128 + 128],
                                      w[:, kd * I + i0:kd * I + i0 + iw],
                                      start=(kd == 0), stop=(kd == KD - 1))
                          return r
                      pr.t(tf)

                  def emit_evac(si):
                      c0, i0, iw = slices[si]
                      ps = psA if si % 2 == 0 else psA2
                      def vf(vector, c0=c0, i0=i0, iw=iw, ps=ps):
                          c = i0 - c0
                          vector.tensor_copy(g32[:, c:c + iw], ps[:, 0:iw])
                          return vector.tensor_copy(u32[:, c:c + iw], ps[:, 512:512 + iw])
                      pr.v(vf)

                  def emit_silu(c0, cw):
                      pr.s(lambda scalar, c0=c0, cw=cw: scalar.activation(
                          g32[:, 0:cw], g32[:, 0:cw], AF.Silu))

                  def emit_mult(c0, cw):
                      pr.v(lambda vector, c0=c0, cw=cw: vector.tensor_tensor(
                          g32[:, 0:cw], g32[:, 0:cw], u32[:, 0:cw], ALU.mult))

                  def emit_q(c0, cw):
                      qjobs = []
                      for q0 in range(0, cw, 512):
                          qw = min(512, cw - q0)
                          qjobs.append((g32[:, q0:q0 + qw], qw,
                                        h_t[:, c0 + q0:c0 + q0 + qw]))
                      emit_quant(pr, qjobs, scr)

                  def emit_tr(j0, nblk):
                      def tfr(tensor, j0=j0, nblk=nblk):
                          r = None
                          for j in range(nblk):
                              r = tensor.transpose(pst[:, j * 128:(j + 1) * 128],
                                                   h_t[:, (j0 + j) * 128:(j0 + j + 1) * 128],
                                                   pr.ident16[:])
                          return r
                      pr.t(tfr)

                  def emit_tre(j0, nblk):
                      def vfr(vector, j0=j0, nblk=nblk):
                          r = None
                          for j in range(nblk):
                              r = vector.tensor_copy(
                                  h_t[:, (j0 + j) * 128:(j0 + j + 1) * 128],
                                  pst[:, j * 128:(j + 1) * 128])
                          return r
                      pr.v(vfr)

                  emit_mm(0); pr.phase()
                  emit_mm(1); emit_evac(0); pr.phase()
                  emit_mm(2); emit_evac(1); pr.phase()
                  emit_mm(3); emit_silu(0, DM); pr.phase()
                  emit_mult(0, DM); pr.phase()
                  emit_q(0, DM)
                  emit_evac(2); pr.phase()
                  emit_mm(4); emit_evac(3); pr.phase()
                  emit_mm(5); emit_silu(DM, DM); pr.phase()
                  emit_mult(DM, DM); pr.phase()
                  emit_q(DM, DM)
                  emit_evac(4); emit_tr(0, 16); pr.phase()
                  emit_evac(5); emit_tre(0, 16); pr.phase()
                  emit_silu(2 * DM, I - 2 * DM); pr.phase()
                  emit_mult(2 * DM, I - 2 * DM); pr.phase()
                  emit_q(2 * DM, I - 2 * DM)
                  emit_tr(16, 6); pr.phase()
                  emit_tre(16, 6); pr.phase()

                  def tf2(tensor, tt=tt):
                      r = None
                      for c0 in range(0, DM, 512):
                          for ki in range(KI):
                              r = tensor.matmul(psD[:, c0:c0 + 512],
                                                hT_t[:, ki * 128:(ki + 1) * 128],
                                                swdT[:, ki * DM + c0:ki * DM + c0 + 512],
                                                start=(ki == 0), stop=(ki == KI - 1))
                      return r
                  pr.t(tf2)
                  pr.dma(x_t[:, 0:DM], xin[b, tt * 128:(tt + 1) * 128, :])
                  pr.phase()

                  def vf2(vector):
                      return vector.tensor_tensor(o_t[:], psD[:, 0:DM], x_t[:, 0:DM], ALU.add)
                  pr.v(vf2); pr.phase()
                  pr.dma(xout[b, tt * 128:(tt + 1) * 128, :], o_t[:])
            pr.phase()
        pr.phase()


def build_l0_attn():
    nc = bass.Bass("TRN2", target_bir_lowering=False)
    xin = nc.dram_tensor("x", [NB, T, DM], F32, kind="ExternalInput").ap()
    xout = nc.dram_tensor("xo", [NB, T, DM], F32, kind="ExternalOutput").ap()
    W = _l0_declare(nc)
    with ExitStack() as es:
        pr = Pr(nc, es)
        load_consts(pr, nc, tri=True, identf=True, self32=True)
        emit_l0(nc, pr, xin, xout, W, "A_")
        pr.finish()
    return nc


def _l0_declare(nc, pf=""):
    names = [("inprojT", [128, 8 * 4384]), ("convw", [128, 18 * 4]),
             ("convbi", [128, 18]), ("dtbias", [32, 1]), ("alogi", [32, 1]),
             ("dvec_rep", [128, 2048]), ("normw_rep", [128, 2048]),
             ("lnw_rep", [128, 1024]), ("lnb_rep", [128, 1024]),
             ("outprojT", [128, 16 * 1024])]
    return {n: nc.dram_tensor(pf + n, sh, F32, kind="ExternalInput").ap()
            for n, sh in names}


L0SER = {"a": 0, "b": 0, "c": 0, "d": 0, "e": 0, "f": 1}


def emit_l0(nc, pr, xin, xout, W, pf):
    def SP(g="a"):
        if L0SER[g]:
            pr.phase()
    H = 32
    Q = 128
    NCH = T // Q
    inprojT = W["inprojT"]; convw = W["convw"]; convbi = W["convbi"]
    dtbias = W["dtbias"]; alogi = W["alogi"]; dvec_rep = W["dvec_rep"]
    normw_rep = W["normw_rep"]; lnw_rep = W["lnw_rep"]; lnb_rep = W["lnb_rep"]
    outprojT = W["outprojT"]
    z0_sp = nc.dram_tensor(pf + "z0_sp", [128, 16 * T], BF16, kind="Internal").ap()
    xbc_sp = nc.dram_tensor(pf + "xbc_sp", [128, 16 * T], BF16, kind="Internal").ap()
    y0_sp = nc.dram_tensor(pf + "y0_sp", [128, 16 * DM], BF16, kind="Internal").ap()
    dk_sp = nc.dram_tensor(pf + "dk_sp", [32, 16], F32, kind="Internal").ap()
    cselap = pr.csel
    if True:
        es = ExitStack()

        stat = es.enter_context(nc.sbuf_tensor(pf + "stat", [128, 8], F32))
        scr_sq = es.enter_context(nc.sbuf_tensor(pf + "scr_sq", [128, DM], F32))
        pr.scr_sq = scr_sq
        onesf = es.enter_context(nc.sbuf_tensor(pf + "onesf", [1, 128], F32))
        cw_sb = es.enter_context(nc.sbuf_tensor(pf + "cw_sb", [128, 18 * 4], F32))
        cb_sb = es.enter_context(nc.sbuf_tensor(pf + "cb_sb", [128, 18], F32))
        dtb_sb = es.enter_context(nc.sbuf_tensor(pf + "dtb_sb", [32, 1], F32))
        an_sb = es.enter_context(nc.sbuf_tensor(pf + "an_sb", [32, 1], F32))
        drep = es.enter_context(nc.sbuf_tensor(pf + "drep", [128, 2048], BF16))
        nwrep = es.enter_context(nc.sbuf_tensor(pf + "nwrep", [128, 2048], BF16))
        lwrep = es.enter_context(nc.sbuf_tensor(pf + "lwrep", [128, 1024], BF16))
        lbrep = es.enter_context(nc.sbuf_tensor(pf + "lbrep", [128, 1024], BF16))
        pr.dma(cw_sb[:], convw)
        pr.dma(cb_sb[:], convbi)
        pr.dma(dtb_sb[:], dtbias)
        pr.dma(an_sb[:], alogi)
        pr.dma(drep[:], dvec_rep, queue="g")
        pr.dma(nwrep[:], normw_rep, queue="g")
        pr.dma(lwrep[:], lnw_rep, queue="g")
        pr.dma(lbrep[:], lnb_rep, queue="g")
        pr.v(lambda vector: vector.memset(onesf[:], 1.0))
        pr.phase()
        pr.s(lambda scalar: scalar.activation(an_sb[:], an_sb[:], AF.Exp))
        pr.phase()
        pr.v(lambda vector: vector.tensor_scalar(an_sb[:], an_sb[:], -1.0, None, ALU.mult))
        pr.phase()

        BT_sb = es.enter_context(nc.sbuf_tensor(pf + "BT_sb", [128, T], BF16))
        CT_sb = es.enter_context(nc.sbuf_tensor(pf + "CT_sb", [128, T], BF16))
        dt_sb = es.enter_context(nc.sbuf_tensor(pf + "dt_sb", [32, T], F32))
        ca_sb = es.enter_context(nc.sbuf_tensor(pf + "ca_sb", [32, T], F32))
        det_sb = es.enter_context(nc.sbuf_tensor(pf + "det_sb", [32, T], F32))
        caT = es.enter_context(nc.sbuf_tensor(pf + "caT", [128, NCH * 32], F32))
        dtT = es.enter_context(nc.sbuf_tensor(pf + "dtT", [128, NCH * 32], F32))
        decE = es.enter_context(nc.sbuf_tensor(pf + "decE", [128, NCH * 32], F32))
        expcaT = es.enter_context(nc.sbuf_tensor(pf + "expcaT", [128, NCH * 32], F32))
        dchkr = es.enter_context(nc.sbuf_tensor(pf + "dchkr", [128, 512], F32))
        dchk_row = es.enter_context(nc.sbuf_tensor(pf + "dchk_row", [1, 512], F32))
        S_sb = es.enter_context(nc.sbuf_tensor(pf + "S_sb", [128, H * 64], BF16))
        ones32 = es.enter_context(nc.sbuf_tensor(pf + "ones32", [32, Q], F32))
        ci = es.enter_context(nc.sbuf_tensor(pf + "ci", [128, T + 4], BF16))
        wslc = es.enter_context(nc.sbuf_tensor(pf + "wslc", [128, 2 * 8 * 128], BF16))
        pr.v(lambda vector: vector.memset(ones32[:], 1.0))
        pr.phase()

        for b in range(NB):
            cih = [0]
            spl = [0]
            y0h = [0]
            y0hs = []
            with nc.sbuf_tensor(f"{pf}lnx{b}", [128, DM], F32) as xs, \
                 nc.sbuf_tensor(f"{pf}y0t{b}", [128, DM], F32) as y0t, \
                 nc.sbuf_tensor(f"{pf}xnb{b}", [128, DM], BF16) as xnb, \
                 nc.sbuf_tensor(f"{pf}xnT{b}", [128, 8 * T], BF16) as xnT, \
                 nc.sbuf_tensor(f"{pf}e32p{b}", [128, T], F32) as e32, \
                 nc.sbuf_tensor(f"{pf}f32p{b}", [128, T], F32) as f32b:
                with nc.psum_tensor(f"{pf}pstT{b}", [128, 2048], BF16) as pst:
                    for tt in range(NTT):
                        pr.dma(xs[:], xin[b, tt * 128:(tt + 1) * 128, :])
                        pr.phase()
                        pr.need(y0h[0])
                        emit_ln(pr, xs, y0t, 1, DM, stat)
                        y0h[0] = pr.dma(y0_sp[:, tt * DM:(tt + 1) * DM], y0t[:],
                                        queue="g", lazy=True)
                        y0hs.append(y0h[0])
                        emit_ln(pr, y0t, xnb, 1, DM, stat, gamma_rep=lwrep, beta_rep=lbrep)
                        emit_transpose(pr, pst, [(xnb, kd * 128, xnT, kd * T + tt * 128)
                                                 for kd in range(8)])

                with nc.psum_tensor(f"{pf}psA{b}", [128, 2048], F32) as psA:
                    wv = inprojT.rearrange("p (kt o) -> p kt o", o=4384)
                    wlh = [0] * 35

                    def wpref(ot):
                        orows = 32 if ot == 34 else 128
                        wo = (ot % 2) * 1024
                        wlh[ot] = pr.dma(
                            wslc[:, wo:wo + 8 * orows].rearrange("p (kt o) -> p kt o",
                                                                 o=orows),
                            wv[:, :, ot * 128:ot * 128 + orows], queue="g", lazy=True)
                    wpref(0)
                    pr.phase()
                    for ot in range(35):
                        orows = 32 if ot == 34 else 128
                        wo = (ot % 2) * 1024
                        pr.need(wlh[ot])
                        if ot < 34:
                            wpref(ot + 1)
                        def tf(tensor, ot=ot, orows=orows, wo=wo):
                            r = None
                            for n0 in range(0, T, 512):
                                for kd in range(8):
                                    r = tensor.matmul(psA[0:orows, n0:n0 + 512],
                                                      wslc[:, wo + kd * orows:
                                                           wo + (kd + 1) * orows],
                                                      xnT[:, kd * T + n0:kd * T + n0 + 512],
                                                      start=(kd == 0), stop=(kd == 7))
                            return r
                        pr.t(tf); pr.phase()
                        if ot < 16:
                            pr.need(cih[0])
                            pr.v(lambda vector: vector.tensor_copy(ci[:, 4:4 + T],
                                                                   psA[:, 0:T]))
                            pr.phase()
                            h = pr.dma(z0_sp[:, ot * T:(ot + 1) * T], ci[:, 4:4 + T],
                                       queue="g", lazy=True)
                            cih[0] = h; spl[0] = max(spl[0], h)
                            pr.phase()
                        elif ot < 34:
                            cc = ot - 16
                            pr.need(cih[0])
                            def vc0(vector, cc=cc):
                                vector.memset(ci[:, 0:4], 0.0)
                                vector.tensor_copy(ci[:, 4:4 + T], psA[:, 0:T])
                                vector.tensor_scalar(f32b[:, 0:T], ci[:, 1:1 + T],
                                                     cw_sb[:, cc * 4:cc * 4 + 1], None, ALU.mult)
                                for k in (1, 2, 3):
                                    vector.scalar_tensor_tensor(
                                        f32b[:, 0:T], ci[:, 1 + k:1 + k + T],
                                        cw_sb[:, cc * 4 + k:cc * 4 + k + 1],
                                        f32b[:, 0:T], ALU.mult, ALU.add)
                                return vector.tensor_scalar(
                                    f32b[:, 0:T], f32b[:, 0:T], cb_sb[:, cc:cc + 1],
                                    None, ALU.add)
                            pr.v(vc0); pr.phase()
                            if cc < 16:
                                dst0 = ci[:, 4:4 + T]
                            elif cc == 16:
                                dst0 = BT_sb[:, 0:T]
                            else:
                                dst0 = CT_sb[:, 0:T]
                            pr.s(lambda scalar, dst0=dst0: scalar.activation(
                                dst0, f32b[:, 0:T], AF.Silu))
                            pr.phase()
                            if cc < 16:
                                h = pr.dma(xbc_sp[:, cc * T:(cc + 1) * T], ci[:, 4:4 + T],
                                           queue="g", lazy=True)
                                cih[0] = h; spl[0] = max(spl[0], h)
                                pr.phase()
                        else:
                            def vd0(vector):
                                vector.tensor_scalar(f32b[0:32, 0:T], psA[0:32, 0:T],
                                                     dtb_sb[:, 0:1], None, ALU.add)
                                return vector.tensor_single_scalar(
                                    e32[0:32, 0:T].bitcast(U32),
                                    f32b[0:32, 0:T].bitcast(U32),
                                    0x7FFFFFFF, ALU.bitwise_and)
                            pr.v(vd0); pr.phase()
                            def sd1(scalar):
                                scalar.activation(e32[0:32, 0:T], e32[0:32, 0:T],
                                                  AF.Exp, scale=-1.0)
                                return scalar.activation(e32[0:32, 0:T], e32[0:32, 0:T],
                                                         AF.Ln, bias=1.0)
                            pr.s(sd1); pr.phase()
                            def vd2(vector):
                                return vector.scalar_tensor_tensor(
                                    dt_sb[:, 0:T], f32b[0:32, 0:T], 0.0, e32[0:32, 0:T],
                                    ALU.max, ALU.add)
                            pr.v(vd2); pr.phase()

            pr.v(lambda vector: vector.tensor_scalar(ca_sb[:, 0:T], dt_sb[:, 0:T],
                                                     an_sb[:, 0:1], None, ALU.mult))
            pr.phase()
            def vca(vector):
                r = None
                for c in range(NCH):
                    r = vector.tensor_tensor_scan(ca_sb[:, c * Q:(c + 1) * Q], ones32[:],
                                                  ca_sb[:, c * Q:(c + 1) * Q], 0.0,
                                                  ALU.mult, ALU.add)
                return r
            pr.v(vca); pr.phase()
            def vde(vector):
                r = None
                for c in range(NCH):
                    r = vector.tensor_scalar(det_sb[:, c * Q:(c + 1) * Q],
                                             ca_sb[:, c * Q:(c + 1) * Q],
                                             ca_sb[:, (c + 1) * Q - 1:(c + 1) * Q],
                                             None, ALU.subtract)
                return r
            pr.v(vde); pr.phase()
            pr.s(lambda scalar: scalar.activation(det_sb[:, 0:T], det_sb[:, 0:T],
                                                  AF.Exp, scale=-1.0))
            pr.phase()
            with nc.psum_tensor(f"{pf}pstc{b}", [128, 2048], F32) as pstc:
                for src, dst in ((ca_sb, caT), (dt_sb, dtT), (det_sb, decE)):
                    def tca(tensor, src=src):
                        r = None
                        for c in range(NCH):
                            r = tensor.transpose(pstc[:, c * 128:c * 128 + 32],
                                                 src[:, c * Q:(c + 1) * Q],
                                                 pr.identf[0:32, 0:32])
                        return r
                    pr.t(tca); pr.phase()
                    def vca2(vector, dst=dst):
                        r = None
                        for c in range(NCH):
                            r = vector.tensor_copy(dst[:, c * 32:(c + 1) * 32],
                                                   pstc[:, c * 128:c * 128 + 32])
                        return r
                    pr.v(vca2); pr.phase()
            pr.s(lambda scalar: scalar.activation(expcaT[:], caT[:], AF.Exp))
            pr.phase()
            def vdk(vector):
                return vector.tensor_copy(det_sb[:, 0:16], ca_sb[:, Q - 1::Q])
            pr.v(vdk); pr.phase()
            pr.s(lambda scalar: scalar.activation(det_sb[:, 0:16], det_sb[:, 0:16], AF.Exp))
            pr.phase()
            pr.dma(dk_sp, det_sb[:, 0:16])
            pr.phase()
            pr.dma(dchk_row[:], dk_sp.rearrange("p c -> (p c)").unsqueeze(0))
            pr.phase()
            with nc.psum_tensor(f"{pf}psk{b}", [128, 512], F32) as psk:
                pr.t(lambda tensor: tensor.matmul(psk[:], onesf[:, 0:128], dchk_row[:],
                                                  start=True, stop=True))
                pr.phase()
                pr.v(lambda vector: vector.tensor_copy(dchkr[:], psk[:]))
                pr.phase()

            pr.v(lambda vector: vector.memset(S_sb[:], 0.0))
            pr.phase()
            with ExitStack() as ces:
                op_sb = ces.enter_context(nc.sbuf_tensor(f"{pf}op{b}", [128, 16 * 1024], BF16))
                self32x = ces.enter_context(nc.sbuf_tensor(f"{pf}sel32{b}", [32, 32 * 128], F32))
                dif = ces.enter_context(nc.sbuf_tensor(f"{pf}dif{b}", [128, 2048], F32))
                mh = ces.enter_context(nc.sbuf_tensor(f"{pf}mh{b}", [128, 32 * 128], BF16))
                cbm = ces.enter_context(nc.sbuf_tensor(f"{pf}cbm{b}", [128, 128], BF16))
                bts = ces.enter_context(nc.sbuf_tensor(f"{pf}bts{b}", [128, 128], BF16))
                xdt = ces.enter_context(nc.sbuf_tensor(f"{pf}xdt{b}", [128, 2048], BF16))
                xw = ces.enter_context(nc.sbuf_tensor(f"{pf}xw{b}", [128, 2048], BF16))
                xtc = ces.enter_context(nc.sbuf_tensor(f"{pf}xtc{b}", [128, 2048], BF16))
                ygc = ces.enter_context(nc.sbuf_tensor(f"{pf}ygc{b}", [128, 2048], BF16))
                f32s = ces.enter_context(nc.sbuf_tensor(f"{pf}f32s{b}", [128, 2048], F32))
                g32s = ces.enter_context(nc.sbuf_tensor(f"{pf}g32s{b}", [128, 2048], F32))
                ygT = ces.enter_context(nc.sbuf_tensor(f"{pf}ygT{b}", [128, 2048], BF16))
                ot8 = ces.enter_context(nc.sbuf_tensor(f"{pf}ot8{b}", [128, 1024], BF16))
                psY = ces.enter_context(nc.psum_tensor(f"{pf}psY{b}", [128, 1024], F32))
                psY2 = ces.enter_context(nc.psum_tensor(f"{pf}psY2{b}", [128, 1024], F32))
                pstx = ces.enter_context(nc.psum_tensor(f"{pf}pstx{b}", [128, 1024], BF16))
                psS = ces.enter_context(nc.psum_tensor(f"{pf}psS{b}", [128, 1024], F32))
                pr.dma(op_sb[:], outprojT, queue="g")
                pr.dma(self32x[:], cselap)
                pr.phase()
                for c in range(NCH):
                    t0 = c * Q
                    xstrh = [0]

                    def tsel(r):
                        ps = psY if r % 2 == 0 else psY2
                        def tf_(tensor, r=r, ps=ps):
                            rr = None
                            for i in range(8):
                                h = r * 8 + i
                                rr = tensor.matmul(ps[:, i * 128:(i + 1) * 128],
                                                   self32x[:, h * 128:(h + 1) * 128],
                                                   ca_sb[:, t0:t0 + Q],
                                                   start=True, stop=True)
                            return rr
                        pr.t(tf_)

                    def vdif(r):
                        ps = psY if r % 2 == 0 else psY2
                        dh = (r % 2) * 1024
                        def vf_(vector, r=r, ps=ps, dh=dh, c=c):
                            rr = None
                            for i in range(8):
                                h = r * 8 + i
                                rr = vector.tensor_scalar(
                                    dif[:, dh + i * 128:dh + (i + 1) * 128],
                                    ps[:, i * 128:(i + 1) * 128],
                                    caT[:, c * 32 + h:c * 32 + h + 1],
                                    0.0, ALU.subtract, ALU.min)
                            return rr
                        pr.v(vf_)

                    def sexp(r):
                        dh = (r % 2) * 1024
                        pr.s(lambda scalar, r=r, dh=dh: scalar.activation(
                            mh[:, r * 1024:(r + 1) * 1024],
                            dif[:, dh:dh + 1024], AF.Exp))

                    tsel(0); pr.phase()
                    tsel(1); SP("a"); vdif(0); pr.phase()
                    tsel(2); SP("a"); vdif(1); SP("a"); sexp(0); pr.phase()
                    tsel(3); SP("a"); vdif(2); SP("a"); sexp(1); pr.phase()
                    pr.t(lambda tensor: tensor.matmul(psS[0:128, 0:128], BT_sb[:, t0:t0 + Q],
                                                      CT_sb[:, t0:t0 + Q],
                                                      start=True, stop=True))
                    SP("a"); vdif(3); SP("a"); sexp(2); pr.phase()
                    pr.v(lambda vector: vector.tensor_tensor(cbm[:], psS[:, 0:128],
                                                             pr.tri16[:], ALU.mult))
                    SP("a")
                    sexp(3)
                    xv2 = xbc_sp.rearrange("p (k t) -> p k t", t=T)
                    pr.need(spl[0])
                    xstrh[0] = pr.dma(xw[:].rearrange("p (k t) -> p k t", t=128),
                                      xv2[:, :, t0:t0 + Q], lazy=True)
                    pr.phase()
                    def vmh(vector):
                        return vector.tensor_tensor(
                            mh[:].rearrange("p (h s) -> p h s", s=128),
                            mh[:].rearrange("p (h s) -> p h s", s=128),
                            cbm[:].unsqueeze(1).broadcast_to([128, 32, 128]), ALU.mult)
                    pr.need(xstrh[0])
                    pr.v(vmh)
                    SP("b")
                    def txt(tensor):
                        r = None
                        for i in range(8):
                            r = tensor.transpose(pstx[:, i * 128:(i + 1) * 128],
                                                 xw[:, i * 128:(i + 1) * 128],
                                                 pr.ident16[:])
                        return r
                    pr.t(txt); pr.phase()
                    pr.v(lambda vector: vector.tensor_copy(xtc[:, 0:1024], pstx[:, 0:1024]))
                    pr.phase()
                    def txt2(tensor):
                        r = None
                        for i in range(8):
                            r = tensor.transpose(pstx[:, i * 128:(i + 1) * 128],
                                                 xw[:, (8 + i) * 128:(9 + i) * 128],
                                                 pr.ident16[:])
                        return r
                    pr.t(txt2); pr.phase()
                    pr.v(lambda vector: vector.tensor_copy(xtc[:, 1024:2048], pstx[:, 0:1024]))
                    pr.phase()
                    def vxdt(vector, c=c):
                        dtv = dtT[:, c * 32:(c + 1) * 32]
                        bc = dtv.unsqueeze(2).broadcast_to([128, 32, 64])
                        return vector.tensor_tensor(
                            xdt[:].rearrange("p (h q) -> p h q", q=64),
                            xtc[:].rearrange("p (h q) -> p h q", q=64), bc, ALU.mult)
                    pr.v(vxdt)
                    SP("c")
                    pr.t(lambda tensor: tensor.transpose(
                        pstx[:, 0:128], BT_sb[:, t0:t0 + Q], pr.ident16[:]))
                    pr.phase()

                    def tin(hh):
                        def tf_(tensor, hh=hh, t0=t0):
                            r = None
                            for i in range(16):
                                h = hh * 16 + i
                                tensor.matmul(psY[:, i * 64:(i + 1) * 64],
                                              mh[:, h * 128:(h + 1) * 128],
                                              xdt[:, h * 64:(h + 1) * 64],
                                              start=True, stop=True)
                                r = tensor.matmul(psY2[:, i * 64:(i + 1) * 64],
                                                  CT_sb[:, t0:t0 + Q],
                                                  S_sb[:, h * 64:(h + 1) * 64],
                                                  start=True, stop=True)
                            return r
                        pr.t(tf_)

                    def vev(hh):
                        def vf_(vector, hh=hh, c=c):
                            r = None
                            for i in range(16):
                                h = hh * 16 + i
                                vector.tensor_scalar(
                                    f32s[:, i * 64:(i + 1) * 64],
                                    psY2[:, i * 64:(i + 1) * 64],
                                    expcaT[:, c * 32 + h:c * 32 + h + 1], None, ALU.mult)
                                r = vector.tensor_tensor(
                                    ygc[:, h * 64:(h + 1) * 64],
                                    f32s[:, i * 64:(i + 1) * 64],
                                    psY[:, i * 64:(i + 1) * 64], ALU.add)
                            return r
                        pr.v(vf_)

                    def tst(r):
                        half = (r % 2) * 512
                        def tf_(tensor, r=r, half=half):
                            rr = None
                            for i in range(8):
                                h = r * 8 + i
                                rr = tensor.matmul(psS[:, half + i * 64:half + (i + 1) * 64],
                                                   bts[:], xw[:, h * 64:(h + 1) * 64],
                                                   start=True, stop=True)
                            return rr
                        pr.t(tf_)

                    def vst(r):
                        half = (r % 2) * 512
                        def vf_(vector, r=r, half=half, c=c):
                            rr = None
                            for i in range(8):
                                h = r * 8 + i
                                rr = vector.scalar_tensor_tensor(
                                    S_sb[:, h * 64:(h + 1) * 64],
                                    S_sb[:, h * 64:(h + 1) * 64],
                                    dchkr[:, h * 16 + c:h * 16 + c + 1],
                                    psS[:, half + i * 64:half + (i + 1) * 64],
                                    ALU.mult, ALU.add)
                            return rr
                        pr.v(vf_)

                    def vxw(vector, c=c):
                        dv2 = decE[:, c * 32:(c + 1) * 32]
                        bc = dv2.unsqueeze(2).broadcast_to([128, 32, 64])
                        vector.tensor_tensor(
                            xw[:].rearrange("p (h q) -> p h q", q=64),
                            xdt[:].rearrange("p (h q) -> p h q", q=64), bc, ALU.mult)
                        return vector.tensor_copy(bts[:], pstx[:, 0:128])
                    pr.v(vxw); SP("d"); tin(0); pr.phase()
                    vev(0); SP("e"); tst(0); pr.phase()
                    vst(0); SP("e"); tin(1); pr.phase()
                    vev(1); SP("e"); tst(1); pr.phase()
                    vst(1); SP("e"); tst(2); pr.phase()
                    vst(2); SP("e"); tst(3); pr.phase()
                    vst(3); SP("e")
                    def vsk(vector):
                        vector.tensor_tensor(f32s[:], xtc[:], drep[:], ALU.mult)
                        return vector.tensor_tensor(ygc[:], ygc[:], f32s[:], ALU.add)
                    pr.v(vsk)
                    zv = z0_sp.rearrange("p (k t) -> p k t", t=T)
                    pr.need(spl[0])
                    xstrh[0] = pr.dma(xw[:].rearrange("p (k t) -> p k t", t=128),
                                      zv[:, :, t0:t0 + Q], lazy=True)
                    pr.phase()

                    pr.need(xstrh[0])
                    def tzt(tensor):
                        r = None
                        for i in range(8):
                            r = tensor.transpose(pstx[:, i * 128:(i + 1) * 128],
                                                 xw[:, i * 128:(i + 1) * 128],
                                                 pr.ident16[:])
                        return r
                    pr.t(tzt); pr.phase()
                    pr.v(lambda vector: vector.tensor_copy(xtc[:, 0:1024], pstx[:, 0:1024]))
                    pr.phase()
                    def tzt2(tensor):
                        r = None
                        for i in range(8):
                            r = tensor.transpose(pstx[:, i * 128:(i + 1) * 128],
                                                 xw[:, (8 + i) * 128:(9 + i) * 128],
                                                 pr.ident16[:])
                        return r
                    pr.t(tzt2); pr.phase()
                    pr.v(lambda vector: vector.tensor_copy(xtc[:, 1024:2048], pstx[:, 0:1024]))
                    pr.phase()
                    pr.s(lambda scalar: scalar.activation(g32s[:], xtc[:], AF.Silu))
                    pr.phase()
                    def vsl(vector):
                        return vector.tensor_tensor(ygc[:], ygc[:], g32s[:], ALU.mult)
                    pr.v(vsl); pr.phase()
                    pr.s(lambda scalar: scalar.activation(g32s[:], ygc[:], AF.Square,
                                                          accum_out=stat[:, 0:1]))
                    pr.phase()
                    pr.v(lambda vector: vector.tensor_scalar(
                        stat[:, 1:2], stat[:, 0:1], 1.0 / 2048.0, EPS, ALU.mult, ALU.add))
                    pr.phase()
                    pr.s(lambda scalar: scalar.activation(stat[:, 2:3], stat[:, 1:2], AF.Ln))
                    pr.phase()
                    pr.s(lambda scalar: scalar.activation(stat[:, 3:4], stat[:, 2:3],
                                                          AF.Exp, scale=-0.5))
                    pr.phase()
                    def vrm2(vector):
                        vector.tensor_scalar(ygc[:], ygc[:], stat[:, 3:4], None, ALU.mult)
                        return vector.tensor_tensor(ygc[:], ygc[:], nwrep[:], ALU.mult)
                    pr.v(vrm2); pr.phase()
                    def tyg(tensor):
                        r = None
                        for i in range(8):
                            r = tensor.transpose(pstx[:, i * 128:(i + 1) * 128],
                                                 ygc[:, i * 128:(i + 1) * 128],
                                                 pr.ident16[:])
                        return r
                    pr.t(tyg); pr.phase()
                    pr.v(lambda vector: vector.tensor_copy(ygT[:, 0:1024], pstx[:, 0:1024]))
                    pr.phase()
                    def tyg2(tensor):
                        r = None
                        for i in range(8):
                            r = tensor.transpose(pstx[:, i * 128:(i + 1) * 128],
                                                 ygc[:, (8 + i) * 128:(9 + i) * 128],
                                                 pr.ident16[:])
                        return r
                    def topk(k0, k1):
                        def tf_(tensor, k0=k0, k1=k1):
                            r = None
                            for dm in range(8):
                                ps = psY if dm < 4 else psY2
                                o = (dm % 4) * 128
                                for kt in range(k0, k1):
                                    r = tensor.matmul(
                                        ps[:, o:o + 128],
                                        op_sb[:, kt * 1024 + dm * 128:kt * 1024 + dm * 128 + 128],
                                        ygT[:, kt * 128:(kt + 1) * 128],
                                        start=(kt == 0), stop=(kt == 15))
                            return r
                        pr.t(tf_)
                    if L0SER["f"]:
                        pr.t(tyg2); pr.phase()
                        pr.v(lambda vector: vector.tensor_copy(ygT[:, 1024:2048],
                                                               pstx[:, 0:1024]))
                        pr.phase()
                        topk(0, 16); pr.phase()
                    else:
                        pr.t(tyg2); topk(0, 8); pr.phase()
                        pr.v(lambda vector: vector.tensor_copy(ygT[:, 1024:2048],
                                                               pstx[:, 0:1024]))
                        pr.phase()
                        topk(8, 16); pr.phase()
                    def vot(vector):
                        r = None
                        for dm in range(8):
                            ps = psY if dm < 4 else psY2
                            o = (dm % 4) * 128
                            r = vector.tensor_copy(ot8[:, dm * 128:(dm + 1) * 128],
                                                   ps[:, o:o + 128])
                        return r
                    pr.v(vot); pr.phase()
                    def tot(tensor):
                        r = None
                        for dm in range(8):
                            r = tensor.transpose(pstx[:, dm * 128:(dm + 1) * 128],
                                                 ot8[:, dm * 128:(dm + 1) * 128],
                                                 pr.ident16[:])
                        return r
                    pr.t(tot)
                    pr.need(y0hs[c])
                    pr.dma(xtc[:, 0:DM], y0_sp[:, c * DM:(c + 1) * DM])
                    pr.dma(f32s[:, 0:DM], xin[b, c * 128:(c + 1) * 128, :])
                    pr.phase()
                    def vres(vector):
                        vector.tensor_tensor(f32s[:, 0:DM], f32s[:, 0:DM],
                                             xtc[:, 0:DM], ALU.add)
                        return vector.tensor_tensor(f32s[:, 0:DM], f32s[:, 0:DM],
                                                    pstx[:, 0:1024], ALU.add)
                    pr.v(vres); pr.phase()
                    pr.dma(xout[b, c * 128:(c + 1) * 128, :], f32s[:, 0:DM])
                    pr.phase()
    es.close()
    pr.phase()


def build_l1_attn():
    nc = bass.Bass("TRN2", target_bir_lowering=False)
    xin = nc.dram_tensor("x", [NB, T, DM], F32, kind="ExternalInput").ap()
    xout = nc.dram_tensor("xo", [NB, T, DM], F32, kind="ExternalOutput").ap()
    W = _l1_declare(nc)
    with ExitStack() as es:
        pr = Pr(nc, es)
        load_consts(pr, nc, sel=True)
        emit_l1(nc, pr, xin, xout, W, "B_")
        pr.finish()
    return nc


def _l1_declare(nc, pf=""):
    W = {}
    for d in ("fw", "bw"):
        W[d] = dict(
            winT=nc.dram_tensor(f"{pf}{d}_winT", [128, 8 * 4096], F32, kind="ExternalInput").ap(),
            convw=nc.dram_tensor(f"{pf}{d}_convw", [128, 16 * 4], F32, kind="ExternalInput").ap(),
            convb=nc.dram_tensor(f"{pf}{d}_convb", [128, 16], F32, kind="ExternalInput").ap(),
            wxT=nc.dram_tensor(f"{pf}{d}_wxT", [128, 16 * 96], F32, kind="ExternalInput").ap(),
            wdtT=nc.dram_tensor(f"{pf}{d}_wdtT", [64, 2048], F32, kind="ExternalInput").ap(),
            bdt=nc.dram_tensor(f"{pf}{d}_bdt", [128, 16], F32, kind="ExternalInput").ap(),
            alog=nc.dram_tensor(f"{pf}{d}_alog", [128, 16 * 16], F32, kind="ExternalInput").ap(),
            dvec=nc.dram_tensor(f"{pf}{d}_dvec", [128, 16], F32, kind="ExternalInput").ap(),
            woutT=nc.dram_tensor(f"{pf}{d}_woutT", [128, 16 * 1024], F32,
                                 kind="ExternalInput").ap(),
        )
    W["fuseWT"] = nc.dram_tensor(pf + "fuseWT", [128, 16 * 1024], F32,
                                 kind="ExternalInput").ap()
    W["fuseb"] = nc.dram_tensor(pf + "fuseb", [128, 8], F32, kind="ExternalInput").ap()
    return W


def emit_l1(nc, pr, xin, xout, Wd, pf):
    DI = 2048
    NDT = DI // 128
    N = 16
    TH = 512
    NSEG = T // TH
    W = {"fw": Wd["fw"], "bw": Wd["bw"]}
    fuseWT = Wd["fuseWT"]
    fuseb = Wd["fuseb"]
    u_sp = nc.dram_tensor(pf + "u_sp", [128, NDT * T], BF16, kind="Internal").ap()
    dt_sp = nc.dram_tensor(pf + "dt_sp", [128, NDT * T], BF16, kind="Internal").ap()
    z_sp = nc.dram_tensor(pf + "z_sp", [128, NDT * T], BF16, kind="Internal").ap()
    y_sp = nc.dram_tensor(pf + "y_sp", [128, NDT * T], BF16, kind="Internal").ap()
    fo_sp = nc.dram_tensor(pf + "fo_sp", [128, 8 * T], BF16, kind="Internal").ap()
    bo_sp = nc.dram_tensor(pf + "bo_sp", [128, 8 * T], BF16, kind="Internal").ap()
    if True:
        es = ExitStack()
        sel16 = es.enter_context(nc.sbuf_tensor(pf + "sel16", [32, 32 * 128], BF16))
        pr.dma(sel16[:], pr.csel, queue="g")
        pr.phase()
        stat = es.enter_context(nc.sbuf_tensor(pf + "stat", [128, 8], F32))
        xnT = es.enter_context(nc.sbuf_tensor(pf + "xnT", [128, 8 * T], BF16))
        wx_sb = es.enter_context(nc.sbuf_tensor(pf + "wx_sb", [128, 16 * 96], BF16))
        wdt_sb = es.enter_context(nc.sbuf_tensor(pf + "wdt_sb", [64, 2048], BF16))
        cw_sb = es.enter_context(nc.sbuf_tensor(pf + "cw_sb", [128, 16 * 4], F32))
        cb_sb = es.enter_context(nc.sbuf_tensor(pf + "cb_sb", [128, 16], F32))
        bdt_sb = es.enter_context(nc.sbuf_tensor(pf + "bdt_sb", [128, 16], F32))
        dv_sb = es.enter_context(nc.sbuf_tensor(pf + "dv_sb", [128, 16], F32))
        aneg = es.enter_context(nc.sbuf_tensor(pf + "aneg", [128, 16 * 16], F32))
        ci = es.enter_context(nc.sbuf_tensor(pf + "ci", [128, T + 4], BF16))
        wslc = es.enter_context(nc.sbuf_tensor(pf + "wslc", [128, 8 * 128], BF16))
        fuseb_sb = es.enter_context(nc.sbuf_tensor(pf + "fuseb_sb", [128, 8], F32))
        bct_sb = es.enter_context(nc.sbuf_tensor(pf + "bct_sb", [32, T], BF16))
        pr.dma(fuseb_sb[:], fuseb)
        pr.phase()

        for b in range(NB):
            with nc.sbuf_tensor(f"{pf}lnx{b}", [128, DM], F32) as xs, \
                 nc.sbuf_tensor(f"{pf}xnb{b}", [128, DM], BF16) as xnb, \
                 nc.sbuf_tensor(f"{pf}sq{b}", [128, DM], F32) as sqs, \
                 nc.psum_tensor(f"{pf}pstT{b}", [128, 2048], BF16) as pst:
                pr.scr_sq = sqs
                for tt in range(NTT):
                    pr.dma(xs[:], xin[b, tt * 128:(tt + 1) * 128, :])
                    pr.phase()
                    emit_ln(pr, xs, xnb, 1, DM, stat)
                    emit_transpose(pr, pst, [(xnb, kd * 128, xnT, kd * T + tt * 128)
                                             for kd in range(8)])

            foh = [0]
            for di, d in enumerate(("fw", "bw")):
                wt = W[d]
                if di == 1:
                    def rev(vector):
                        r = None
                        for kd in range(8):
                            vector.tensor_copy(ci[:, 0:T], xnT[:, kd * T:(kd + 1) * T])
                            src = ci[:, 0:T]
                            r = vector.tensor_copy(xnT[:, kd * T:(kd + 1) * T], src[:, ::-1])
                        return r
                    pr.v(rev)
                    pr.phase()

                pr.dma(wx_sb[:], wt["wxT"], queue="g")
                pr.dma(wdt_sb[:], wt["wdtT"], queue="g")
                pr.dma(cw_sb[:], wt["convw"])
                pr.dma(cb_sb[:], wt["convb"])
                pr.dma(bdt_sb[:], wt["bdt"])
                pr.dma(dv_sb[:], wt["dvec"])
                pr.dma(aneg[:], wt["alog"])
                pr.phase()
                pr.s(lambda scalar: scalar.activation(aneg[:], aneg[:], AF.Exp))
                pr.phase()
                pr.v(lambda vector: vector.tensor_scalar(aneg[:], aneg[:], -1.0, None, ALU.mult))
                pr.phase()

                l1xz, _ = nc.enter_named_scope(f"l1xz{b}{di}", False)
                cih = [0]
                spl = [0]
                dtscope = ExitStack()
                dtp_sb = dtscope.enter_context(
                    nc.sbuf_tensor(f"{pf}dtp{b}{di}", [64, T], BF16))
                with nc.psum_tensor(f"{pf}psA{b}{di}", [128, 2048], F32) as psA, \
                     nc.psum_tensor(f"{pf}psX{b}{di}", [128, 2048], F32) as psX, \
                     nc.sbuf_tensor(f"{pf}ciB{b}{di}", [128, T + 4], BF16) as ciB, \
                     nc.sbuf_tensor(f"{pf}wslB{b}{di}", [128, 8 * 128], BF16) as wslc2, \
                     nc.sbuf_tensor(f"{pf}e32a{b}{di}", [128, T], F32) as e32, \
                     nc.sbuf_tensor(f"{pf}f32a{b}{di}", [128, T], F32) as f32b:
                    cis = [ci, ciB]
                    fbs = [f32b, e32]
                    wsl = [wslc, wslc2]
                    wh = [0, 0]
                    sph = [0] * 32
                    wv = wt["winT"].rearrange("p (kt o) -> p kt o", o=4096)

                    def ldw(ot):
                        wh[ot % 2] = pr.dma(
                            wsl[ot % 2][:].rearrange("p (kt o) -> p kt o", o=128),
                            wv[:, :, ot * 128:(ot + 1) * 128], queue="g", lazy=True)

                    def mm(ot):
                        pr.need(wh[ot % 2])
                        def tf(tensor, ot=ot):
                            r = None
                            for n0 in range(0, T, 512):
                                for kd in range(8):
                                    r = tensor.matmul(
                                        psA[:, n0:n0 + 512],
                                        wsl[ot % 2][:, kd * 128:(kd + 1) * 128],
                                        xnT[:, kd * T + n0:kd * T + n0 + 512],
                                        start=(kd == 0), stop=(kd == 7))
                            return r
                        pr.t(tf)

                    def evconv(ot):
                        p = ot % 2
                        if ot >= 2:
                            pr.need(sph[ot - 2])
                        if ot < 16:
                            def vc0(vector, ot=ot, p=p):
                                vector.memset(cis[p][:, 0:4], 0.0)
                                vector.tensor_copy(cis[p][:, 4:4 + T], psA[:, 0:T])
                                vector.tensor_scalar(fbs[p][:, 0:T], cis[p][:, 1:1 + T],
                                                     cw_sb[:, ot * 4:ot * 4 + 1], None,
                                                     ALU.mult)
                                for k in (1, 2, 3):
                                    vector.scalar_tensor_tensor(
                                        fbs[p][:, 0:T], cis[p][:, 1 + k:1 + k + T],
                                        cw_sb[:, ot * 4 + k:ot * 4 + k + 1],
                                        fbs[p][:, 0:T], ALU.mult, ALU.add)
                                return vector.tensor_scalar(
                                    fbs[p][:, 0:T], fbs[p][:, 0:T], cb_sb[:, ot:ot + 1],
                                    None, ALU.add)
                            pr.v(vc0)
                        else:
                            pr.v(lambda vector, p=p: vector.tensor_copy(
                                cis[p][:, 4:4 + T], psA[:, 0:T]))

                    def silu(ot):
                        if ot < 16:
                            p = ot % 2
                            pr.s(lambda scalar, p=p: scalar.activation(
                                cis[p][:, 4:4 + T], fbs[p][:, 0:T], AF.Silu))

                    def post(ot):
                        p = ot % 2
                        if ot < 16:
                            def tfx(tensor, ot=ot, p=p):
                                r = None
                                for n0 in range(0, T, 512):
                                    r = tensor.matmul(psX[0:96, n0:n0 + 512],
                                                      wx_sb[:, ot * 96:(ot + 1) * 96],
                                                      cis[p][:, 4 + n0:4 + n0 + 512],
                                                      start=(ot == 0), stop=(ot == 15))
                                return r
                            pr.t(tfx)
                            h = pr.dma(u_sp[:, ot * T:(ot + 1) * T], cis[p][:, 4:4 + T],
                                       queue="g", lazy=True)
                        else:
                            h = pr.dma(z_sp[:, (ot - 16) * T:(ot - 15) * T],
                                       cis[p][:, 4:4 + T], queue="g", lazy=True)
                        sph[ot] = h
                        spl[0] = max(spl[0], h)

                    ldw(0); pr.phase()
                    for ot in range(33):
                        if ot < 32:
                            mm(ot)
                        if ot >= 1:
                            silu(ot - 1)
                        if ot + 1 < 32:
                            ldw(ot + 1)
                        pr.phase()
                        if ot < 32:
                            evconv(ot)
                        if ot >= 1:
                            post(ot - 1)
                        pr.phase()
                    def vx(vector):
                        vector.tensor_copy(dtp_sb[:, 0:T], psX[0:64, 0:T])
                        return vector.tensor_copy(bct_sb[:, 0:T], psX[64:96, 0:T])
                    pr.v(vx); pr.phase()

                with nc.psum_tensor(f"{pf}psB{b}{di}", [128, 2048], F32) as psB, \
                     nc.sbuf_tensor(f"{pf}e32b{b}{di}", [128, T], F32) as e32, \
                     nc.sbuf_tensor(f"{pf}f32c{b}{di}", [128, T], F32) as f32b:
                    for kt in range(NDT):
                        def tf2(tensor, kt=kt):
                            r = None
                            for n0 in range(0, T, 512):
                                r = tensor.matmul(psB[:, n0:n0 + 512],
                                                  wdt_sb[:, kt * 128:(kt + 1) * 128],
                                                  dtp_sb[:, n0:n0 + 512],
                                                  start=True, stop=True)
                            return r
                        pr.t(tf2); pr.phase()
                        def vd0(vector, kt=kt):
                            vector.tensor_scalar(f32b[:, 0:T], psB[:, 0:T],
                                                 bdt_sb[:, kt:kt + 1], None, ALU.add)
                            return vector.tensor_single_scalar(
                                e32[:, 0:T].bitcast(U32), f32b[:, 0:T].bitcast(U32),
                                0x7FFFFFFF, ALU.bitwise_and)
                        pr.v(vd0); pr.phase()
                        def sd1(scalar):
                            scalar.activation(e32[:, 0:T], e32[:, 0:T], AF.Exp, scale=-1.0)
                            return scalar.activation(e32[:, 0:T], e32[:, 0:T], AF.Ln, bias=1.0)
                        pr.s(sd1); pr.phase()
                        pr.need(cih[0])
                        def vd2(vector):
                            return vector.scalar_tensor_tensor(
                                ci[:, 4:4 + T], f32b[:, 0:T], 0.0, e32[:, 0:T],
                                ALU.max, ALU.add)
                        pr.v(vd2)
                        pr.phase()
                        h = pr.dma(dt_sp[:, kt * T:(kt + 1) * T], ci[:, 4:4 + T],
                                   queue="g", lazy=True)
                        cih[0] = h; spl[0] = max(spl[0], h)
                        pr.phase()
                dtscope.close()
                nc.leave_named_scope(f"l1xz{b}{di}", l1xz, False)

                yh = [0]
                l1sc, _ = nc.enter_named_scope(f"l1scan{b}{di}", False)
                with nc.sbuf_tensor(f"{pf}brep{b}{di}", [128, N * TH], BF16) as brep, \
                     nc.sbuf_tensor(f"{pf}crep{b}{di}", [128, N * TH], BF16) as crep, \
                     nc.sbuf_tensor(f"{pf}su{b}{di}", [128, NDT * TH], BF16) as s_u, \
                     nc.sbuf_tensor(f"{pf}sdt{b}{di}", [128, NDT * TH], BF16) as s_dt, \
                     nc.sbuf_tensor(f"{pf}sa0{b}{di}", [128, NDT * TH], BF16) as dA0, \
                     nc.sbuf_tensor(f"{pf}sa1{b}{di}", [128, NDT * TH], BF16) as dA1, \
                     nc.sbuf_tensor(f"{pf}sh{b}{di}", [128, NDT * TH], BF16) as h_h, \
                     nc.sbuf_tensor(f"{pf}shB{b}{di}", [128, NDT * TH], BF16) as h_hB, \
                     nc.sbuf_tensor(f"{pf}sy{b}{di}", [128, NDT * TH], BF16) as ya_h, \
                     nc.sbuf_tensor(f"{pf}hcar{b}{di}", [128, NDT * N], F32) as hcar:
                    for seg in range(NSEG):
                        t0 = seg * TH
                        with nc.psum_tensor(f"{pf}psR{b}{di}{seg}", [128, 2048], F32) as psR:
                            for n0 in range(0, N, 2):
                                def tr2(tensor, n0=n0, t0=t0):
                                    r = None
                                    for i, n in enumerate((n0, n0 + 1)):
                                        tensor.matmul(psR[:, i * 1024:i * 1024 + 512],
                                                      sel16[:, n * 128:(n + 1) * 128],
                                                      bct_sb[:, t0:t0 + TH],
                                                      start=True, stop=True)
                                        r = tensor.matmul(
                                            psR[:, i * 1024 + 512:i * 1024 + 1024],
                                            sel16[:, (16 + n) * 128:(17 + n) * 128],
                                            bct_sb[:, t0:t0 + TH],
                                            start=True, stop=True)
                                    return r
                                pr.t(tr2); pr.phase()
                                def vr2(vector, n0=n0):
                                    r = None
                                    for i, n in enumerate((n0, n0 + 1)):
                                        vector.tensor_copy(brep[:, n * TH:(n + 1) * TH],
                                                           psR[:, i * 1024:i * 1024 + 512])
                                        r = vector.tensor_copy(
                                            crep[:, n * TH:(n + 1) * TH],
                                            psR[:, i * 1024 + 512:i * 1024 + 1024])
                                    return r
                                pr.v(vr2); pr.phase()

                        uv = u_sp.rearrange("p (k t) -> p k t", t=T)
                        dv = dt_sp.rearrange("p (k t) -> p k t", t=T)
                        pr.need(spl[0])
                        pr.dma(s_u[:].rearrange("p (k t) -> p k t", t=TH),
                               uv[:, :, t0:t0 + TH])
                        pr.dma(s_dt[:].rearrange("p (k t) -> p k t", t=TH),
                               dv[:, :, t0:t0 + TH])
                        pr.phase()
                        pr.need(yh[0])
                        def vseed(vector):
                            r = None
                            for j in range(NDT):
                                sl = slice(j * TH, (j + 1) * TH)
                                r = vector.tensor_scalar(ya_h[:, sl], s_u[:, sl],
                                                         dv_sb[:, j:j + 1], None, ALU.mult)
                            return r
                        pr.v(vseed); pr.phase()
                        def vw(vector):
                            return vector.tensor_tensor(s_u[:], s_dt[:], s_u[:], ALU.mult)
                        pr.v(vw)
                        def sa0(scalar):
                            r = None
                            for j in range(NDT):
                                r = scalar.activation(
                                    dA0[:, j * TH:(j + 1) * TH],
                                    s_dt[:, j * TH:(j + 1) * TH], AF.Exp,
                                    scale=aneg[:, j * 16:j * 16 + 1])
                            return r
                        pr.s(sa0); pr.phase()

                        for n in range(N):
                            dAc = dA0 if n % 2 == 0 else dA1
                            dAn = dA1 if n % 2 == 0 else dA0
                            hb = h_h if n % 2 == 0 else h_hB
                            hprev = h_hB if n % 2 == 0 else h_h
                            if n + 1 < N:
                                def sa(scalar, n=n, dAn=dAn):
                                    r = None
                                    for j in range(NDT):
                                        r = scalar.activation(
                                            dAn[:, j * TH:(j + 1) * TH],
                                            s_dt[:, j * TH:(j + 1) * TH], AF.Exp,
                                            scale=aneg[:, j * 16 + n + 1:j * 16 + n + 2])
                                    return r
                                pr.s(sa)
                            if n >= 1:
                                def gacc(vector, hprev=hprev):
                                    return vector.tensor_tensor(ya_h[:], ya_h[:], hprev[:],
                                                                ALU.add)
                                pr.v(gacc)
                            def vscan(vector, n=n, dAc=dAc, seg=seg, hb=hb):
                                bv = brep[:, n * TH:(n + 1) * TH].unsqueeze(1) \
                                    .broadcast_to([128, NDT, TH])
                                cv = crep[:, n * TH:(n + 1) * TH].unsqueeze(1) \
                                    .broadcast_to([128, NDT, TH])
                                hv = hb[:].rearrange("p (k t) -> p k t", t=TH)
                                uv3 = s_u[:].rearrange("p (k t) -> p k t", t=TH)
                                vector.tensor_tensor(hv, uv3, bv, ALU.mult)
                                for j in range(NDT):
                                    sl = slice(j * TH, (j + 1) * TH)
                                    col = j * N + n
                                    ini = 0.0 if seg == 0 else hcar[:, col:col + 1]
                                    vector.tensor_tensor_scan(hb[:, sl], dAc[:, sl],
                                                              hb[:, sl], ini,
                                                              ALU.mult, ALU.add)
                                vector.tensor_copy(hcar[:, n::N], hb[:, TH - 1::TH])
                                return vector.tensor_tensor(hv, hv, cv, ALU.mult)
                            pr.v(vscan); pr.phase()
                        def gacc2(vector):
                            return vector.tensor_tensor(ya_h[:], ya_h[:],
                                                        (h_hB if (N - 1) % 2 else h_h)[:],
                                                        ALU.add)
                        pr.v(gacc2); pr.phase()

                        yv = y_sp.rearrange("p (k t) -> p k t", t=T)
                        yh[0] = pr.dma(yv[:, :, t0:t0 + TH],
                                       ya_h[:].rearrange("p (k t) -> p k t", t=TH),
                                       queue="g", lazy=True)
                        pr.phase()

                nc.leave_named_scope(f"l1scan{b}{di}", l1sc, False)
                l1ga, _ = nc.enter_named_scope(f"l1gate{b}{di}", False)
                osp = fo_sp if di == 0 else bo_sp
                with nc.sbuf_tensor(f"{pf}gy{b}{di}", [128, 2 * 16 * 512], BF16) as gy2, \
                     nc.sbuf_tensor(f"{pf}gz{b}{di}", [128, 2 * 16 * 512], BF16) as gz2, \
                     nc.sbuf_tensor(f"{pf}wo{b}{di}", [128, 16 * 1024], BF16) as wo_sb, \
                     nc.psum_tensor(f"{pf}psO{b}{di}", [128, 8 * 512], F32) as psO:
                    yv = y_sp.rearrange("p (k t) -> p k t", t=T)
                    zv = z_sp.rearrange("p (k t) -> p k t", t=T)
                    woh = pr.dma(wo_sb[:], wt["woutT"], queue="g", lazy=True)
                    gh = [0, 0, 0, 0]

                    def pref(si):
                        q0 = si * 512
                        o = (si % 2) * (16 * 512)
                        pr.dma(gy2[:, o:o + 16 * 512].rearrange("p (k t) -> p k t", t=512),
                               yv[:, :, q0:q0 + 512], lazy=True)
                        gh[si] = pr.dma(
                            gz2[:, o:o + 16 * 512].rearrange("p (k t) -> p k t", t=512),
                            zv[:, :, q0:q0 + 512], lazy=True)
                    pr.need(yh[0]); pr.need(spl[0]); pr.need(foh[0])
                    pref(0)
                    pr.phase()
                    for si in range(4):
                        o = (si % 2) * (16 * 512)
                        gyb = gy2[:, o:o + 16 * 512]
                        gzb = gz2[:, o:o + 16 * 512]
                        pr.need(gh[si])
                        pr.s(lambda scalar, gzb=gzb: scalar.activation(gzb, gzb, AF.Silu))
                        pr.phase()
                        pr.v(lambda vector, gyb=gyb, gzb=gzb:
                             vector.tensor_tensor(gyb, gyb, gzb, ALU.mult))
                        if si < 3:
                            pr.need(foh[0])
                            pref(si + 1)
                        pr.phase()
                        pr.need(woh)
                        def tw(tensor, gyb=gyb):
                            r = None
                            for kt in range(16):
                                for dm in range(8):
                                    r = tensor.matmul(
                                        psO[:, dm * 512:(dm + 1) * 512],
                                        wo_sb[:, kt * 1024 + dm * 128:
                                              kt * 1024 + dm * 128 + 128],
                                        gyb[:, kt * 512:(kt + 1) * 512],
                                        start=(kt == 0), stop=(kt == NDT - 1))
                            return r
                        pr.t(tw); pr.phase()
                        def ve2(vector, di=di, gyb=gyb):
                            r = None
                            for dm in range(8):
                                dst = gyb[:, dm * 512:(dm + 1) * 512]
                                if di == 1:
                                    dst = dst[:, ::-1]
                                r = vector.tensor_copy(dst, psO[:, dm * 512:(dm + 1) * 512])
                            return r
                        pr.v(ve2); pr.phase()
                        q0 = si * 512
                        for dm in range(8):
                            if di == 0:
                                h = pr.dma(osp[:, dm * T + q0:dm * T + q0 + 512],
                                           gyb[:, dm * 512:(dm + 1) * 512], queue="g",
                                           lazy=True)
                            else:
                                h = pr.dma(osp[:, dm * T + T - q0 - 512:dm * T + T - q0],
                                           gyb[:, dm * 512:(dm + 1) * 512], queue="g",
                                           lazy=True)
                            foh[0] = h
                        pr.phase()

                nc.leave_named_scope(f"l1gate{b}{di}", l1ga, False)
            with nc.sbuf_tensor(f"{pf}o_t{b}", [128, DM], F32) as o_t, \
                 nc.sbuf_tensor(f"{pf}fsw{b}", [128, 16 * 1024], BF16) as fw_sb, \
                 nc.sbuf_tensor(f"{pf}fio{b}", [128, 2 * 16 * 512], BF16) as fio, \
                 nc.sbuf_tensor(f"{pf}fot{b}", [128, 8 * 512], BF16) as fot, \
                 nc.psum_tensor(f"{pf}psF{b}", [128, 4 * 512], F32) as psF, \
                 nc.psum_tensor(f"{pf}pstF{b}", [128, 1024], BF16) as pstF:
                pr.dma(fw_sb[:], fuseWT, queue="g")
                fv = fo_sp.rearrange("p (k t) -> p k t", t=T)
                bv = bo_sp.rearrange("p (k t) -> p k t", t=T)
                fih = [0] * 4

                def fpref(si):
                    q0 = si * 512
                    fo = (si % 2) * (16 * 512)
                    pr.dma(fio[:, fo:fo + 8 * 512].rearrange("p (k t) -> p k t", t=512),
                           fv[:, :, q0:q0 + 512], lazy=True)
                    fih[si] = pr.dma(
                        fio[:, fo + 8 * 512:fo + 16 * 512].rearrange("p (k t) -> p k t",
                                                                     t=512),
                        bv[:, :, q0:q0 + 512], lazy=True)
                pr.need(foh[0])
                fpref(0)
                pr.phase()
                for si in range(4):
                    q0 = si * 512
                    fo = (si % 2) * (16 * 512)
                    pr.need(fih[si])
                    for dh in range(2):
                        if dh == 1 and si < 3:
                            fpref(si + 1)
                        def tff(tensor, dh=dh, fo=fo):
                            r = None
                            for dmo in range(dh * 4, dh * 4 + 4):
                                for kt in range(16):
                                    r = tensor.matmul(
                                        psF[:, (dmo % 4) * 512:(dmo % 4) * 512 + 512],
                                        fw_sb[:, kt * 1024 + dmo * 128:
                                              kt * 1024 + dmo * 128 + 128],
                                        fio[:, fo + kt * 512:fo + (kt + 1) * 512],
                                        start=(kt == 0), stop=(kt == 15))
                            return r
                        pr.t(tff); pr.phase()
                        def vff(vector, dh=dh):
                            r = None
                            for dmo in range(dh * 4, dh * 4 + 4):
                                r = vector.tensor_scalar(
                                    fot[:, dmo * 512:(dmo + 1) * 512],
                                    psF[:, (dmo % 4) * 512:(dmo % 4) * 512 + 512],
                                    fuseb_sb[:, dmo:dmo + 1], None, ALU.add)
                            return r
                        pr.v(vff); pr.phase()
                    for st in range(4):
                        def tft(tensor, st=st):
                            r = None
                            for dmo in range(8):
                                r = tensor.transpose(pstF[:, dmo * 128:(dmo + 1) * 128],
                                                     fot[:, dmo * 512 + st * 128:
                                                         dmo * 512 + st * 128 + 128],
                                                     pr.ident16[:])
                            return r
                        pr.t(tft)
                        tt = (q0 + st * 128) // 128
                        pr.dma(o_t[:], xin[b, tt * 128:(tt + 1) * 128, :])
                        pr.phase()
                        def vres(vector):
                            return vector.tensor_tensor(o_t[:], o_t[:], pstF[:, 0:1024],
                                                        ALU.add)
                        pr.v(vres); pr.phase()
                        pr.dma(xout[b, tt * 128:(tt + 1) * 128, :], o_t[:])
                        pr.phase()
    es.close()
    pr.phase()


_PROGS = {}


def _tile_rows(w):
    K_, M_ = w.shape[0] // 128, w.shape[1]
    return np.ascontiguousarray(w.reshape(K_, 128, M_).transpose(1, 0, 2).reshape(128, K_ * M_))


def _get_prog(name):
    if name not in _PROGS:
        _PROGS[name] = {"ffn": build_ffn, "l1": build_l1_attn, "l0": build_l0_attn,
                         "full": build_full}[name]()
    return _PROGS[name]


def _run_ffn(x, wg, wu, wd):
    nc = _get_prog("ffn")
    consts = host_consts()
    in_maps = []
    for c in range(NCORES):
        m = {"c_ident": consts["c_ident"]}
        m["x"] = np.ascontiguousarray(x[c * NB:(c + 1) * NB])
        m["wg"] = _tile_rows(wg)
        m["wu"] = _tile_rows(wu)
        m["wd"] = _tile_rows(wd)
        in_maps.append(m)
    res = run_bass_kernel_spmd(nc, in_maps, core_ids=list(range(NCORES)))
    return np.concatenate([r["xo"] for r in res.results], axis=0)


def _tile_vec(v):
    return np.ascontiguousarray(v.reshape(-1, 128).T)


def _l1_weight_maps(I):
    m = {}
    for d in ("fw", "bw"):
        m[f"{d}_winT"] = _tile_rows(np.ascontiguousarray(I[f"L1_attn_{d}_Win"].T))
        m[f"{d}_convw"] = _tile_rows(np.ascontiguousarray(I[f"L1_attn_{d}_convw"][:, 0, :]))
        m[f"{d}_convb"] = _tile_vec(I[f"L1_attn_{d}_convb"])
        m[f"{d}_wxT"] = _tile_rows(np.ascontiguousarray(I[f"L1_attn_{d}_Wx"].T))
        m[f"{d}_wdtT"] = np.ascontiguousarray(I[f"L1_attn_{d}_Wdt"].T)
        m[f"{d}_bdt"] = _tile_vec(I[f"L1_attn_{d}_bdt"])
        m[f"{d}_alog"] = _tile_rows(np.ascontiguousarray(I[f"L1_attn_{d}_Alog"]))
        m[f"{d}_dvec"] = _tile_vec(I[f"L1_attn_{d}_D"])
        m[f"{d}_woutT"] = _tile_rows(np.ascontiguousarray(I[f"L1_attn_{d}_Wout"].T))
    m["fuseWT"] = _tile_rows(np.ascontiguousarray(I["L1_attn_fuse_W"].T))
    m["fuseb"] = _tile_vec(I["L1_attn_fuse_b"])
    return m


def _l0_weight_maps(I):
    H = 32
    D = I["L0_attn_D"]
    hp = np.repeat(D, 64)
    return {
        "inprojT": _tile_rows(np.ascontiguousarray(I["L0_attn_in_proj_w"].T)),
        "convw": _tile_rows(np.ascontiguousarray(I["L0_attn_conv_w"])),
        "convbi": _tile_vec(I["L0_attn_conv_b"]),
        "dtbias": np.ascontiguousarray(I["L0_attn_dt_bias"].reshape(32, 1)),
        "alogi": np.ascontiguousarray(I["L0_attn_A_log"].reshape(32, 1)),
        "dvec_rep": np.ascontiguousarray(np.broadcast_to(hp, (128, 2048))),
        "normw_rep": np.ascontiguousarray(np.broadcast_to(I["L0_attn_norm_w"], (128, 2048))),
        "lnw_rep": np.ascontiguousarray(np.broadcast_to(I["L0_attn_ln_w"], (128, 1024))),
        "lnb_rep": np.ascontiguousarray(np.broadcast_to(I["L0_attn_ln_b"], (128, 1024))),
        "outprojT": _tile_rows(np.ascontiguousarray(I["L0_attn_out_proj_w"].T)),
    }


def _run_l0(x, I):
    nc = _get_prog("l0")
    consts = host_consts()
    wm = _l0_weight_maps(I)
    in_maps = []
    for c in range(NCORES):
        m = {"c_ident": consts["c_ident"], "c_sel": consts["c_sel"],
             "c_tri": consts["c_tri"]}
        m.update(wm)
        m["x"] = np.ascontiguousarray(x[c * NB:(c + 1) * NB])
        in_maps.append(m)
    res = run_bass_kernel_spmd(nc, in_maps, core_ids=list(range(NCORES)))
    return np.concatenate([r["xo"] for r in res.results], axis=0)


def _run_l1(x, I):
    nc = _get_prog("l1")
    consts = host_consts()
    wm = _l1_weight_maps(I)
    in_maps = []
    for c in range(NCORES):
        m = {"c_ident": consts["c_ident"], "c_sel": consts["c_sel"]}
        m.update(wm)
        m["x"] = np.ascontiguousarray(x[c * NB:(c + 1) * NB])
        in_maps.append(m)
    res = run_bass_kernel_spmd(nc, in_maps, core_ids=list(range(NCORES)))
    return np.concatenate([r["xo"] for r in res.results], axis=0)


def build_full():
    I = 2816
    KI = I // 128
    KD = DM // 128
    nc = bass.Bass("TRN2", target_bir_lowering=False)
    xin = nc.dram_tensor("x", [NB, T, DM], F32, kind="ExternalInput").ap()
    xout = nc.dram_tensor("xo", [NB, T, DM], F32, kind="ExternalOutput").ap()
    x1 = nc.dram_tensor("x1", [NB, T, DM], F32, kind="Internal").ap()
    x2 = nc.dram_tensor("x2", [NB, T, DM], F32, kind="Internal").ap()
    x3 = nc.dram_tensor("x3", [NB, T, DM], F32, kind="Internal").ap()
    WL0 = _l0_declare(nc)
    WL1 = _l1_declare(nc)
    ffw = {}
    for t_ in ("0", "1"):
        ffw[t_] = (nc.dram_tensor(f"wg{t_}", [128, KI * DM], F32, kind="ExternalInput").ap(),
                   nc.dram_tensor(f"wu{t_}", [128, KI * DM], F32, kind="ExternalInput").ap(),
                   nc.dram_tensor(f"wd{t_}", [128, KD * I], F32, kind="ExternalInput").ap())
    with ExitStack() as es:
        pr = Pr(nc, es)
        load_consts(pr, nc, sel=True, tri=True, identf=True)
        with nc.named_scope("L0"):
            emit_l0(nc, pr, xin, x1, WL0, "A_")
        with nc.named_scope("FFN0"):
            emit_ffn(nc, pr, x1, ffw["0"][0], ffw["0"][1], ffw["0"][2], x2, "C_")
        with nc.named_scope("L1"):
            emit_l1(nc, pr, x2, x3, WL1, "B_")
        with nc.named_scope("FFN1"):
            emit_ffn(nc, pr, x3, ffw["1"][0], ffw["1"][1], ffw["1"][2], xout, "D_")
        pr.finish()
    return nc


def kernel(**inputs):
    nc = _get_prog("full")
    consts = host_consts()
    m0 = _l0_weight_maps(inputs)
    m1 = _l1_weight_maps(inputs)
    base = {"c_ident": consts["c_ident"], "c_sel": consts["c_sel"],
            "c_tri": consts["c_tri"]}
    base.update(m0)
    base.update(m1)
    for t_, pre in (("0", "L0_ffn"), ("1", "L1_ffn")):
        base[f"wg{t_}"] = _tile_rows(inputs[f"{pre}_w_gate"])
        base[f"wu{t_}"] = _tile_rows(inputs[f"{pre}_w_up"])
        base[f"wd{t_}"] = _tile_rows(inputs[f"{pre}_w_down"])
    x = np.ascontiguousarray(inputs["L0_attn_x"], dtype=np.float32)
    in_maps = []
    for c in range(NCORES):
        m = dict(base)
        m["x"] = np.ascontiguousarray(x[c * NB:(c + 1) * NB])
        in_maps.append(m)
    res = run_bass_kernel_spmd(nc, in_maps, core_ids=list(range(NCORES)))
    return np.concatenate([r["xo"] for r in res.results], axis=0)


def _kernel_unfused(**inputs):
    x = np.ascontiguousarray(inputs["L0_attn_x"], dtype=np.float32)
    x = _run_l0(x, inputs)
    x = _run_ffn(x, inputs["L0_ffn_w_gate"], inputs["L0_ffn_w_up"], inputs["L0_ffn_w_down"])
    x = _run_l1(x, inputs)
    x = _run_ffn(x, inputs["L1_ffn_w_gate"], inputs["L1_ffn_w_up"], inputs["L1_ffn_w_down"])
    return x
